# version 15
# baseline (speedup 1.0000x reference)
import numpy as np
import ml_dtypes
from contextlib import ExitStack
import concourse.bass as bass
import concourse.mybir as mybir
from concourse.bass_utils import run_bass_kernel_spmd

F32 = mybir.dt.float32
BF16 = mybir.dt.bfloat16
AF = mybir.ActivationFunctionType
ALU = mybir.AluOpType
AX = mybir.AxisListType

NCORE = 8
B, T, D = 2, 8192, 1024
HL = 2
CP = HL * 6
NT = 2048
NBLK = NT // 128
TILE = 512
NTILE = NT // TILE
NKB = T // 128
NSEG = T // TILE
E = 16
EI = 512
NE = 18
SCALE = 6.0 ** -0.5
LN_EPS = 1e-5

NBCOL = 152
NBROW = 2304
WVQK = 160

_PAIRS = [(0, 1), (0, 2), (0, 3), (1, 2), (1, 3), (2, 3)]
_JSIGN = [1.0, -1.0, 1.0, 1.0, -1.0, 1.0]


class Sch:
    ENGS = ("pe", "act", "dve", "pool", "sp")

    def __init__(self):
        self.prog = {e: [] for e in self.ENGS}
        self.cnt = {}
        self.lastwait = {e: {} for e in self.ENGS}

    def on(self, eng, fn, wait=(), inc=None, incn=1):
        wl = []
        for tok in wait:
            if tok is None:
                continue
            s, v = tok
            if v > self.lastwait[eng].get(s, 0):
                wl.append((s, v))
                self.lastwait[eng][s] = v
        res = None
        if inc is not None:
            self.cnt[inc] = self.cnt.get(inc, 0) + incn
            res = (inc, self.cnt[inc])
        self.prog[eng].append((fn, tuple(wl), (inc, incn) if inc else None))
        return res

    def now(self, sem):
        return (sem, self.cnt.get(sem, 0))

    def barrier(self):
        snap = [(s, c) for s, c in self.cnt.items() if c > 0]
        for e in self.ENGS:
            self.on(e, None, wait=snap)


def fv(tsr, dims, off=0):
    base = tsr[:]
    return bass.AP(base.tensor, off, [list(base.ap[0])] +
                   [[s0, c0] for s0, c0 in dims])


def build_nc(stage=6, dbg=()):
    nc = bass.Bass("TRN2", target_bir_lowering=False, debug=False)
    dt = {}

    def din(name, shape):
        dt[name] = nc.dram_tensor(name, list(shape), F32, kind="ExternalInput")

    din("xin", (D, NT))
    din("we1", (E, D, EI))
    din("we3", (E, D, EI))
    din("we2", (E, EI, D))
    din("ws1", (D, D))
    din("ws3", (D, D))
    din("ws2", (D, D))
    din("gwT", (D, E))
    din("be2m", (E, D))
    din("bcol_h", (128, NBCOL))
    din("brow_h", (1, NBROW))
    din("growf", (1, 4 * E))
    din("wvqk", (D, WVQK))
    din("wqb", (D, 16))
    din("wo", (D, D))
    din("ident", (128, 128))
    din("trih", (128, 128))
    din("hmask_h", (CP, HL))
    din("ebc_h", (E, E * 128))

    xout = nc.dram_tensor("xout", [D, NT], F32, kind="ExternalOutput")
    agx_in = nc.dram_tensor("agx_in", [D, NT], BF16, kind="Internal")
    agx_outA = nc.dram_tensor("agx_outA", [NCORE * D // 2, NT], BF16, kind="Internal")
    agx_outB = nc.dram_tensor("agx_outB", [NCORE * D // 2, NT], BF16, kind="Internal")
    aga_in = nc.dram_tensor("aga_in", [B * 128, T], BF16, kind="Internal")
    aga_out = nc.dram_tensor("aga_out", [B * NCORE * 128, T], BF16, kind="Internal")

    sb = {}

    def sbuf(name, shape, dtype):
        sb[name] = nc.alloc_sbuf_tensor(name, list(shape), dtype)
        return sb[name]

    X = sbuf("X", (128, 8, NT), F32)
    id_bf = sbuf("id_bf", (128, 128), BF16)
    tri_bf = sbuf("tri_bf", (128, 128), BF16)
    ones_bf = sbuf("ones_bf", (1, TILE), BF16)
    onesr_f = sbuf("onesr_f", (1, 128), F32)
    onesc_f = sbuf("onesc_f", (128, 1), F32)
    bcol = sbuf("bcol", (128, NBCOL), F32)
    brow_bf = sbuf("brow_bf", (1, NBROW), BF16)
    be2_bf = sbuf("be2_bf", (E, D), BF16)
    gwT_bf = sbuf("gwT_bf", (128, 8, E), BF16)
    ebc_bf = sbuf("ebc_bf", (E, E * 128), BF16)
    gbb = sbuf("gbb", (128, 4, E), F32)
    hmask = sbuf("hmask", (CP, HL), F32)
    growf_s = sbuf("growf_s", (1, 4 * E), F32)
    s_row = [sbuf(f"s_row{h}", (1, TILE), F32) for h in (0, 1)]
    s_rec = [sbuf(f"s_rec{h}", (1, TILE), F32) for h in (0, 1)]
    r_tmp2 = s_row[0]
    g_eg = sbuf("g_eg", (128, 4, E), F32)
    g_or = sbuf("g_or", (128, 4, E), F32)
    g_bi = sbuf("g_bi", (128, 4, E), F32)
    g_s1 = sbuf("g_s1", (128, 4, 15), F32)
    g_s2 = sbuf("g_s2", (128, 4, 14), F32)
    g_s3 = sbuf("g_s3", (128, 4, 13), F32)
    g_G = sbuf("g_G", (128, 4, 4), F32)
    g_r2 = sbuf("g_r2", (128, 4, 4), F32)
    g_x = sbuf("g_x", (128, 4, 3), F32)
    g_cnt = sbuf("g_cnt", (128, 4, 4), F32)
    g_pen = sbuf("g_pen", (128, 4, 4), F32)
    g_sm = sbuf("g_sm", (128, 4, E), F32)
    g_mx8 = sbuf("g_mx8", (128, 8), F32)
    g_ss = sbuf("g_ss", (128, 4), F32)
    g_rs = sbuf("g_rs", (128, 4), F32)
    cb_bf = sbuf("cb_bf", (128, 4, E), BF16)
    lq_f = sbuf("lq_f", (128, 8, CP), F32)
    lk_f = sbuf("lk_f", (128, 8, CP), F32)
    sq24 = sbuf("sq24", (128, 8, CP), F32)
    n2q = sbuf("n2q", (128, 16), F32)
    n2k = sbuf("n2k", (128, 16), F32)
    rn_q = sbuf("rn_q", (128, 16), F32)
    rn_k = sbuf("rn_k", (128, 16), F32)
    prd = sbuf("prd", (128, 16), F32)
    prd2 = sbuf("prd2", (128, 16), F32)
    prdk = sbuf("prdk", (128, 16), F32)
    prd2k = sbuf("prd2k", (128, 16), F32)
    sq24k = sbuf("sq24k", (128, 8, CP), F32)
    lq24 = [sbuf(f"lq24_{i}", (128, 8, CP), BF16) for i in (0, 1)]
    jlk24 = [sbuf(f"jlk24_{i}", (128, 8, CP), BF16) for i in (0, 1)]
    xprev = sbuf("xprev", (128, 8, 128), BF16)

    OVL = 110 * 1024
    ovl_back = nc.alloc_sbuf_tensor("ovl_back", [128, OVL], mybir.dt.uint8)
    ovl_base = nc.lookup_mloc(ovl_back).addr

    def oalloc(name, shape, dtype, off):
        h = nc.virtual_sbuf_tensor(name, list(shape), dtype)
        nc._tensor(h.name, list(h.shape), h.dtype, type="SB")
        mloc = nc.lookup_mloc(h)
        mloc.allocated = True
        mloc.addr = ovl_base + off
        nbytes = int(np.prod(shape[1:])) * mybir.dt.size(dtype)
        assert off % 32 == 0 and off + nbytes <= OVL, (name, off, nbytes)
        sb[name] = h
        return h

    KB = 1024
    xn_bf = oalloc("xn_bf", (128, 8, NT), BF16, 0)
    w1b = [oalloc(f"w1b{i}", (128, 8, EI), BF16, 32 * KB + 8 * KB * i) for i in (0, 1)]
    w3b = [oalloc(f"w3b{i}", (128, 8, EI), BF16, 48 * KB + 8 * KB * i) for i in (0, 1)]
    w2b = [oalloc(f"w2b{i}", (128, 4, D), BF16, 64 * KB + 8 * KB * i) for i in (0, 1)]
    h_bf = [oalloc(f"h_bf{i}", (128, 4, TILE), BF16, 80 * KB + 4 * KB * i) for i in (0, 1)]
    sil_t = [oalloc(f"sil_t{i}", (128, TILE), F32, 88 * KB + 2 * KB * i) for i in (0, 1)]
    sq_t2 = [oalloc(f"sq_t{i}", (128, TILE), F32, 92 * KB + 2 * KB * i)
             for i in (0, 1)]
    t1_t = oalloc("t1_t", (128, TILE), F32, 106 * KB)
    cbT_bf = oalloc("cbT_bf", (E, NT), BF16, 96 * KB)
    jlkt = oalloc("jlkt", (CP, T), BF16, 0)
    lqt = oalloc("lqt", (CP, T), BF16, 16 * KB)
    vaug = oalloc("vaug", (128, NKB, HL * 65), BF16, 32 * KB)
    attg = oalloc("attg", (128, 8, NT), BF16, 0)
    xnseg = [oalloc(f"xnseg{i}", (128, 8, TILE), BF16, 66 * KB + 8 * KB * i)
             for i in (0, 1)]
    wo_bf = oalloc("wo_bf", (128, 8, D), BF16, 82 * KB)
    wvqk_bf = oalloc("wvqk_bf", (128, 8, WVQK), BF16, 98 * KB)
    wqb_bf = oalloc("wqb_bf", (128, 8, 16), BF16, 101 * KB)
    exp_t = [[oalloc(f"exp_t{h}_{i}", (128, 1024), BF16,
                     (102 * KB if h == 0 else 55552) + 2 * KB * i)
              for i in (0, 1)] for h in (0, 1)]
    lqm = [[oalloc(f"lqm{h}_{i}", (CP, TILE), BF16,
                   (106 + 2 * h + i) * KB) for i in (0, 1)] for h in (0, 1)]
    r_mean = oalloc("r_mean", (1, TILE), F32, 80 * KB)
    r_tmp = oalloc("r_tmp", (1, TILE), F32, 82 * KB)
    qk12 = [oalloc(f"qk12{i}", (128, 8, 32), F32, 59648 + 1 * KB * i)
            for i in (0, 1, 2)]
    tmp64 = [oalloc(f"tmp64{h}", (64, TILE), F32, 62720 + 2 * KB * h)
             for h in (0, 1)]
    attno = [oalloc(f"attno{i}", (128, TILE), BF16, 53504 + 1 * KB * i)
             for i in (0, 1)]

    pA = nc.alloc_psum_tensor("pA", [128, 1024], F32)
    pB = nc.alloc_psum_tensor("pB", [128, 1024], F32)
    pC = nc.alloc_psum_tensor("pC", [128, 512], F32)
    pD = nc.alloc_psum_tensor("pD", [128, 512], F32)
    pE_ = nc.alloc_psum_tensor("pE", [128, 512], F32)
    pbf = nc.alloc_psum_tensor("pbf", [128, 1024], BF16)

    semn = ["wld", "xld", "st", "pe", "act", "dve", "pool", "cc1", "cc2"]
    sems = {n: nc.alloc_semaphore(n) for n in semn}
    S = Sch()
    pidc = {}

    def get_pid(s):
        if "row" not in pidc:
            pid = s.partition_id()
            pidc["b"] = pid // 4
            pidc["row"] = (pid // 4) * 1024
            pidc["col"] = (pid - (pid // 4) * 4) * NT
        return pidc


    def PE(fn, wait=()):
        return S.on("pe", fn, wait=wait, inc="pe")

    def ACT(fn, wait=()):
        return S.on("act", fn, wait=wait, inc="act")

    def DVE(fn, wait=()):
        return S.on("dve", fn, wait=wait, inc="dve")

    def POOL(fn, wait=()):
        return S.on("pool", fn, wait=wait, inc="pool")

    def SPD(fn, wait=()):
        return S.on("sp", fn, wait=wait, inc="xld", incn=16)

    def WLD(fn, wait=()):
        return S.on("pool", fn, wait=wait, inc="wld", incn=16)

    def STD(fn, wait=()):
        return S.on("sp", fn, wait=wait, inc="st", incn=16)

    def mmgroup(items, wait=()):
        n = len(items)
        tok = None
        for i, (o, l, r) in enumerate(items):
            tok = S.on(
                "pe",
                (lambda t, o=o, l=l, r=r, st_=(i == 0), sp_=(i == n - 1):
                 t.matmul(o, l, r, start=st_, stop=sp_)),
                wait=wait if i == 0 else (),
                inc="pe" if i == n - 1 else None)
        return tok

    def _first(s):
        get_pid(s)
        return s.dma_start(bcol[:, :], dt["bcol_h"][:, :])
    SPD(_first)
    SPD(lambda s: s.dma_start(hmask[:, :], dt["hmask_h"][:, :]))
    SPD(lambda s: s.dma_start(growf_s[:, :], dt["growf"][:, :]))
    for dc in range(8):
        SPD(lambda s, dc=dc: s.dma_start(X[:, dc, :],
                                         dt["xin"][dc * 128:(dc + 1) * 128, :]))
    tok_xload = S.now("xld")

    WLD(lambda g: g.dma_start(id_bf[:, :], dt["ident"][:, :]))
    WLD(lambda g: g.dma_start(tri_bf[:, :], dt["trih"][:, :]))
    WLD(lambda g: g.dma_start(brow_bf[:, :], dt["brow_h"][:, :]))
    WLD(lambda g: g.dma_start(be2_bf[:, :], dt["be2m"][:, :]))
    WLD(lambda g: g.dma_start(ebc_bf[:, :], dt["ebc_h"][:, :]))
    WLD(lambda g: g.dma_start(gwT_bf[:, :, :],
                              dt["gwT"][:, :].rearrange("(k p) e -> p k e", p=128)))
    tok_const = S.now("wld")

    DVE(lambda v: v.memset(ones_bf[:, :], 1.0))
    DVE(lambda v: v.memset(onesr_f[:, :], 1.0))
    tok_ones = DVE(lambda v: v.memset(onesc_f[:, :], 1.0))

    PE(lambda t: t.matmul(pC[:, 0:64], onesr_f[:, :], growf_s[:, :],
                          start=True, stop=True),
       wait=[tok_xload, tok_ones])
    tok_gbb = ACT(lambda a: a.copy(gbb[:, :, :], pC[:, 0:64]), wait=[S.now("pe")])

    def emit_ln():
        prev = [S.now("act"), S.now("dve"), S.now("pe")]
        prev_bm = prev_br = prev_xn = None
        xn_toks = []
        for tt in range(NTILE):
            cols = slice(tt * TILE, (tt + 1) * TILE)
            tok_sum = mmgroup(
                [(pD[0:1, :], onesc_f[:, :], X[:, dc, cols]) for dc in range(8)],
                wait=prev)
            tok_mm = None
            tok_mm2q = None
            for dc in range(8):
                sp_ = sq_t2[dc % 2]
                tok_sq = DVE(lambda v, dc=dc, cols=cols, sp_=sp_:
                             v.tensor_mul(sp_[:, :], X[:, dc, cols], X[:, dc, cols]),
                             wait=([tok_mm2q] if dc >= 2 else prev))
                tok_mm = S.on("pe",
                              (lambda t, st_=(dc == 0), sp2_=(dc == 7), sp_=sp_:
                               t.matmul(pE_[0:1, :], onesc_f[:, :], sp_[:, :],
                                        start=st_, stop=sp2_)),
                              wait=[tok_sq], inc="pe")
                if dc % 2 == 1:
                    tok_mm2q = tok_mm
            tok_mean = ACT(lambda a: a.activation(r_mean[:, :], pD[0:1, :], AF.Copy,
                                                  scale=1.0 / D),
                           wait=[tok_sum, prev_bm])
            tok_msq = ACT(lambda a: a.activation(r_tmp[:, :], pE_[0:1, :], AF.Copy,
                                                 scale=1.0 / D),
                          wait=[tok_mm, prev_br])
            tok_m2 = DVE(lambda v: v.tensor_mul(r_tmp2[:, :], r_mean[:, :],
                                                r_mean[:, :]), wait=[tok_mean])
            tok_var = DVE(lambda v: v.scalar_tensor_tensor(
                r_tmp[:, :], r_tmp[:, :], LN_EPS, r_tmp2[:, :],
                ALU.add, ALU.subtract), wait=[tok_msq, tok_m2])
            tok_lnv = ACT(lambda a: a.activation(r_tmp[:, :], r_tmp[:, :], AF.Ln),
                          wait=[tok_var])
            tok_rstd = ACT(lambda a: a.activation(r_tmp[:, :], r_tmp[:, :], AF.Exp,
                                                  scale=-0.5), wait=[tok_lnv])
            tok_nm = DVE(lambda v: v.tensor_scalar_mul(r_mean[:, :], r_mean[:, :],
                                                       -1.0), wait=[tok_var])
            prev_bm = PE(lambda t: t.matmul(pA[:, 0:512], onesr_f[:, :],
                                            r_mean[:, :], start=True, stop=True),
                         wait=[tok_nm, prev_xn])
            prev_br = PE(lambda t: t.matmul(pA[:, 512:1024], onesr_f[:, :],
                                            r_tmp[:, :], start=True, stop=True),
                         wait=[tok_rstd])
            for dc in range(8):
                DVE(lambda v, dc=dc, cols=cols:
                    v.tensor_add(t1_t[:, :], X[:, dc, cols], pA[:, 0:512]),
                    wait=[prev_bm, prev_br] if dc == 0 else ())
                prev_xn = DVE(lambda v, dc=dc, cols=cols:
                              v.tensor_mul(xn_bf[:, dc, cols], t1_t[:, :],
                                           pA[:, 512:1024]))
            prev = [prev_xn, prev_bm, prev_br, tok_rstd]
            xn_toks.append(prev_xn)
        return xn_toks

    def emit_gate(tok_xn):
        tok_tr_rd = None
        tok_pg_rd = None
        cbT_toks = []
        for tt in range(NTILE):
            toks_pg = []
            for b4 in range(4):
                bc = slice(tt * TILE + b4 * 128, tt * TILE + b4 * 128 + 128)
                toks_pg.append(mmgroup(
                    [(pC[:, 16 * b4:16 * b4 + 16], xn_bf[:, kc, bc],
                      gwT_bf[:, kc, :]) for kc in range(8)],
                    wait=[tok_xn[tt], tok_const, tok_pg_rd] if b4 == 0 else ()))
            tok_pg = toks_pg[-1]
            tok_exp = ACT(lambda a: a.activation(g_eg[:, :, :], pC[:, 0:64],
                                                 AF.Exp), wait=[tok_pg])
            tok_pg_rd = tok_exp
            tk = DVE(lambda v: v.tensor_reduce(g_ss[:, :],
                                               fv(g_eg, [(E, 4), (1, E)]),
                                               AX.X, ALU.add), wait=[tok_exp])
            tk = DVE(lambda v: v.reciprocal(g_rs[:, :], g_ss[:, :]), wait=[tk])
            for b4 in range(4):
                tk = DVE(lambda v, b4=b4: v.tensor_scalar_mul(
                    g_or[:, b4, :], g_eg[:, b4, :], g_rs[:, b4:b4 + 1]),
                    wait=[tk])
            tk = DVE(lambda v: v.tensor_tensor(g_bi[:, :, :], g_or[:, :, :],
                                               gbb[:, :, :], ALU.add),
                     wait=[tk, tok_gbb])
            tk = DVE(lambda v: v.tensor_add(fv(g_s1, [(15, 4), (1, 15)]),
                                            fv(g_bi, [(E, 4), (1, 15)]),
                                            fv(g_bi, [(E, 4), (1, 15)], 1)),
                     wait=[tk])
            tk = DVE(lambda v: v.tensor_add(fv(g_s2, [(14, 4), (1, 14)]),
                                            fv(g_bi, [(E, 4), (1, 14)]),
                                            fv(g_bi, [(E, 4), (1, 14)], 2)),
                     wait=[tk])
            tk = DVE(lambda v: v.tensor_add(fv(g_s3, [(13, 4), (1, 13)]),
                                            fv(g_bi, [(E, 4), (1, 13)]),
                                            fv(g_bi, [(E, 4), (1, 13)], 3)),
                     wait=[tk])
            tk = DVE(lambda v: v.tensor_reduce(
                g_G[:, :, :], fv(g_s1, [(15, 4), (4, 4), (1, 3)]),
                AX.X, ALU.max), wait=[tk])
            tk = DVE(lambda v: v.tensor_reduce(
                g_r2[:, :, :], fv(g_s2, [(14, 4), (4, 4), (1, 2)]),
                AX.X, ALU.max), wait=[tk])
            tk = DVE(lambda v: v.tensor_max(g_G[:, :, :], g_G[:, :, :],
                                            g_r2[:, :, :]), wait=[tk])
            tk = DVE(lambda v: v.tensor_max(g_G[:, :, :], g_G[:, :, :],
                                            fv(g_s3, [(13, 4), (4, 4)])),
                     wait=[tk])
            tk = DVE(lambda v: v.memset(g_cnt[:, :, :], 0.0), wait=[tk])
            for s_ in (1, 2, 3):
                tk = DVE(lambda v, s_=s_: v.tensor_tensor(
                    fv(g_x, [(3, 4), (1, 4 - s_)]),
                    fv(g_G, [(4, 4), (1, 4 - s_)], s_),
                    fv(g_G, [(4, 4), (1, 4 - s_)]), ALU.is_gt), wait=[tk])
                tk = DVE(lambda v, s_=s_: v.tensor_add(
                    fv(g_cnt, [(4, 4), (1, 4 - s_)]),
                    fv(g_cnt, [(4, 4), (1, 4 - s_)]),
                    fv(g_x, [(3, 4), (1, 4 - s_)])), wait=[tk])
                tk = DVE(lambda v, s_=s_: v.tensor_scalar(
                    fv(g_x, [(3, 4), (1, 4 - s_)]),
                    fv(g_x, [(3, 4), (1, 4 - s_)]),
                    -1.0, 1.0, ALU.mult, ALU.add), wait=[tk])
                tk = DVE(lambda v, s_=s_: v.tensor_add(
                    fv(g_cnt, [(4, 4), (1, 4 - s_)], s_),
                    fv(g_cnt, [(4, 4), (1, 4 - s_)], s_),
                    fv(g_x, [(3, 4), (1, 4 - s_)])), wait=[tk])
            tk = DVE(lambda v: v.tensor_scalar(g_pen[:, :, :], g_cnt[:, :, :],
                                               1.5, -1e30, ALU.is_ge, ALU.mult),
                     wait=[tk])
            for j in range(4):
                tk = DVE(lambda v, j=j: v.tensor_add(
                    fv(g_sm, [(E, 4), (4, 4)], j),
                    fv(g_bi, [(E, 4), (4, 4)], j),
                    g_pen[:, :, :]), wait=[tk])
            for b4 in range(4):
                tk = DVE(lambda v, b4=b4: v.max(g_mx8[:, :], g_sm[:, b4, :]),
                         wait=[tk])
                tk = DVE(lambda v, b4=b4: v.tensor_single_scalar(
                    g_sm[:, b4, :], g_sm[:, b4, :], g_mx8[:, 1:2], ALU.is_ge),
                    wait=[tk])
            tok_cb = DVE(lambda v: v.tensor_mul(cb_bf[:, :, :], g_sm[:, :, :],
                                                g_or[:, :, :]), wait=[tk])
            for b4 in range(4):
                bc = slice(tt * TILE + b4 * 128, tt * TILE + b4 * 128 + 128)
                tok_tr = PE(lambda t, b4=b4: t.transpose(
                    pbf[0:E, 128 * b4:128 * b4 + 128], cb_bf[:, b4, :],
                    id_bf[:, :]),
                    wait=[tok_cb, tok_tr_rd, tok_const] if b4 == 0 else ())
            tok_trl = S.now("pe")
            tok_cbT = DVE(lambda v, tt=tt: v.tensor_copy(
                cbT_bf[:, tt * TILE:(tt + 1) * TILE], pbf[0:E, 0:512]),
                wait=[tok_trl])
            tok_tr_rd = tok_cbT
            cbT_toks.append(tok_cbT)
        return cbT_toks

    wstate = {"last_use": [None, None]}

    def emit_wload(e):
        par = e % 2
        wa = wstate["last_use"][par]
        wa = wa if isinstance(wa, list) else [wa]
        if e < E:
            WLD(lambda g, e=e, par=par: g.dma_start(
                w1b[par][:, :, :],
                dt["we1"][e, :, :].rearrange("(k p) i -> p k i", p=128)), wait=wa)
            WLD(lambda g, e=e, par=par: g.dma_start(
                w3b[par][:, :, :],
                dt["we3"][e, :, :].rearrange("(k p) i -> p k i", p=128)))
            WLD(lambda g, e=e, par=par: g.dma_start(
                w2b[par][:, :, :],
                dt["we2"][e, :, :].rearrange("(k p) d -> p k d", p=128)))
        else:
            s_ = e - E
            WLD(lambda g, s_=s_, par=par: g.dma_start(
                w1b[par][:, :, :],
                dt["ws1"][:, s_ * EI:(s_ + 1) * EI]
                .rearrange("(k p) i -> p k i", p=128)), wait=wa)
            WLD(lambda g, s_=s_, par=par: g.dma_start(
                w3b[par][:, :, :],
                dt["ws3"][:, s_ * EI:(s_ + 1) * EI]
                .rearrange("(k p) i -> p k i", p=128)))
            WLD(lambda g, s_=s_, par=par: g.dma_start(
                w2b[par][:, 0:4, :],
                dt["ws2"][s_ * EI:(s_ + 1) * EI, :]
                .rearrange("(k p) d -> p k d", p=128)))
        return S.now("wld")

    def emit_experts(tok_cbT):
        fence = [S.now("act"), S.now("dve")]
        est = {"sil_rd": [fence, fence], "p13_rd": {0: fence, 1: fence},
               "h_rd": [fence, fence], "p2_rd": {0: fence, 1: fence},
               "be_rd": fence}
        if wstate.get("pre"):
            wtok = wstate.pop("pre")
        else:
            wtok = [emit_wload(0), emit_wload(1)]
        pend = None

        def emit_p2(par, h_par, tok_h, cols, e):
            tok_p2l = None
            for dc in range(8):
                dcp = dc % 2
                pP2 = pD if dcp == 0 else pE_
                items = [(pP2[:, :], w2b[par][:, kc, dc * 128:(dc + 1) * 128],
                          h_bf[h_par][:, kc, :]) for kc in range(4)]
                if e == 0:
                    items.append((pP2[:, :], be2_bf[:, dc * 128:(dc + 1) * 128],
                                  cbT_bf[:, cols]))
                if e == E + 1:
                    items.append((pP2[:, :],
                                  brow_bf[:, 256 + dc * 128:256 + dc * 128 + 128],
                                  ones_bf[:, :]))
                tok_p2 = mmgroup(items, wait=[tok_h] + est["p2_rd"][dcp])
                tok_ev = DVE(lambda v, dc=dc, cols=cols, pP2=pP2:
                             v.scalar_tensor_tensor(X[:, dc, cols], pP2[:, :],
                                                    0.5, X[:, dc, cols],
                                                    ALU.mult, ALU.add),
                             wait=[tok_p2])
                est["p2_rd"][dcp] = [tok_ev]
                tok_p2l = tok_p2
            est["h_rd"][h_par] = [tok_p2l]

        for e in range(NE):
            par = e % 2
            mywtok = wtok[par]
            for tt in range(NTILE):
                cols = slice(tt * TILE, (tt + 1) * TILE)
                if e < E:
                    tok_be = PE(lambda t, e=e, cols=cols:
                                t.matmul(pC[:, :], ebc_bf[:, e * 128:(e + 1) * 128],
                                         cbT_bf[:, cols], start=True, stop=True),
                                wait=[tok_cbT[tt]] + est["be_rd"])
                h_par = (e * NTILE + tt) % 2
                tok_h = None
                for ic in range(4):
                    icp = ic % 2
                    pP1 = pA[:, icp * 512:(icp + 1) * 512]
                    pP3 = pB[:, icp * 512:(icp + 1) * 512]
                    mmgroup([(pP1, w1b[par][:, kc, ic * 128:(ic + 1) * 128],
                              xn_bf[:, kc, cols]) for kc in range(8)],
                            wait=[mywtok] + est["p13_rd"][icp])
                    tok_p13 = mmgroup([(pP3, w3b[par][:, kc, ic * 128:(ic + 1) * 128],
                                        xn_bf[:, kc, cols]) for kc in range(8)])
                    bix = (4 * e + ic) if e < E else (128 + 4 * (e - E) + ic)
                    b3x = (64 + 4 * e + ic) if e < E else (136 + 4 * (e - E) + ic)
                    tok_sil = ACT(lambda a, bix=bix, icp=icp, pP1=pP1:
                                  a.activation(sil_t[icp][:, :], pP1, AF.Silu,
                                               bias=bcol[:, bix:bix + 1]),
                                  wait=[tok_p13] + est["sil_rd"][icp])
                    if e < E:
                        DVE(lambda v, b3x=b3x, icp=icp, pP3=pP3:
                            v.scalar_tensor_tensor(t1_t[:, :], pP3,
                                                   bcol[:, b3x:b3x + 1],
                                                   sil_t[icp][:, :],
                                                   ALU.add, ALU.mult),
                            wait=[tok_sil] + est["h_rd"][h_par])
                        tok_h = DVE(lambda v, ic=ic, h_par=h_par:
                                    v.tensor_mul(h_bf[h_par][:, ic, :], t1_t[:, :],
                                                 pC[:, :]),
                                    wait=[tok_be])
                    else:
                        tok_h = DVE(lambda v, b3x=b3x, icp=icp, ic=ic, h_par=h_par,
                                    pP3=pP3:
                                    v.scalar_tensor_tensor(h_bf[h_par][:, ic, :],
                                                           pP3,
                                                           bcol[:, b3x:b3x + 1],
                                                           sil_t[icp][:, :],
                                                           ALU.add, ALU.mult),
                                    wait=[tok_sil] + est["h_rd"][h_par])
                    est["sil_rd"][icp] = [tok_h]
                    est["p13_rd"][icp] = [tok_sil, tok_h]
                if e < E:
                    est["be_rd"] = [tok_h]
                if pend is not None:
                    emit_p2(*pend)
                    if pend[4] != e and pend[4] + 2 < NE:
                        wstate["last_use"][pend[0]] = S.now("pe")
                        wtok[pend[0]] = emit_wload(pend[4] + 2)
                pend = (par, h_par, tok_h, cols, e)
        emit_p2(*pend)

    def emit_ffn_pass():
        tok_xn = emit_ln()
        tok_cbT = emit_gate(tok_xn)
        emit_experts(tok_cbT)

    def emit_attention():
        tok_xn = emit_ln()
        for dc in range(8):
            SPD(lambda s, dc=dc: s.dma_start(agx_in[dc * 128:(dc + 1) * 128, :],
                                             xn_bf[:, dc, :]),
                wait=[tok_xn[-1]])
        tok_cc1a = S.on("pool",
                        lambda g: g.collective_compute(
                            "AllGather", ALU.bypass,
                            replica_groups=[list(range(NCORE))],
                            ins=[agx_in[0:D // 2, :].opt()],
                            outs=[agx_outA.ap().opt()]),
                        wait=[S.now("xld")], inc="cc1")
        tok_cc1b = S.on("pool",
                        lambda g: g.collective_compute(
                            "AllGather", ALU.bypass,
                            replica_groups=[list(range(NCORE))],
                            ins=[agx_in[D // 2:D, :].opt()],
                            outs=[agx_outB.ap().opt()]),
                        wait=[S.now("xld")], inc="cc1")
        wa = [S.now("pe"), S.now("dve"), S.now("act")]
        WLD(lambda g: g.dma_start(
            wvqk_bf[:, :, :], dt["wvqk"][:, :].rearrange("(k p) v -> p k v", p=128)),
            wait=wa)
        WLD(lambda g: g.dma_start(
            wqb_bf[:, :, :], dt["wqb"][:, :].rearrange("(k p) v -> p k v", p=128)))
        WLD(lambda g: g.dma_start(
            wo_bf[:, :, :], dt["wo"][:, :].rearrange("(k p) v -> p k v", p=128)))
        tok_wat = S.now("wld")

        tok_attno_rd = [None, None]
        tok_cc2 = {}
        for sq in range(B):
            tok_vones = DVE(
                lambda v: v.memset(
                    vaug[:, :, :].rearrange("p k (h c) -> p k h c", c=65)[:, :, :, 64],
                    1.0),
                wait=wa + [S.now("pe")])
            seg_ld = [None, None]
            seg_rd = [None, None]
            seg_dve_rd = [None, None]
            pvk_rd = {0: [S.now("act"), S.now("dve")],
                      1: [S.now("act"), S.now("dve")]}
            qk_rd = [[], [], []]
            tr_rd = None
            pend_tr = None

            def flush_tr():
                nonlocal pend_tr, tr_rd
                if pend_tr is None:
                    return
                pr_, pi_, tok_chain = pend_tr
                for jj in range(8):
                    PE(lambda t, jj=jj, pr_=pr_: t.transpose(
                        pbf[0:CP, 128 * jj:128 * jj + 128],
                        lq24[pr_][:, jj, :], id_bf[:, :]),
                       wait=(tok_chain + [tok_const, tr_rd]) if jj == 0 else ())
                tok_q8 = S.now("pe")
                c1 = DVE(lambda v, pi_=pi_: v.tensor_copy(
                    lqt[:, pi_ * 1024:(pi_ + 1) * 1024], pbf[0:CP, :]),
                    wait=[tok_q8])
                for jj in range(8):
                    PE(lambda t, jj=jj, pr_=pr_: t.transpose(
                        pbf[0:CP, 128 * jj:128 * jj + 128],
                        jlk24[pr_][:, jj, :], id_bf[:, :]),
                       wait=[c1] if jj == 0 else ())
                tok_k8 = S.now("pe")
                tr_rd = DVE(lambda v, pi_=pi_: v.tensor_copy(
                    jlkt[:, pi_ * 1024:(pi_ + 1) * 1024], pbf[0:CP, :]),
                    wait=[tok_k8])
                pend_tr = None

            for seg in range(NSEG):
                par = seg % 2
                pr = (seg // 2) % 2
                ri = 4 * sq + seg // 4
                co = (seg % 4) * TILE
                seg_ldA = SPD(
                    lambda s, ri=ri, co=co, par=par: s.dma_start(
                        xnseg[par][:, 0:4, :],
                        agx_outA[512 * ri:512 * (ri + 1), co:co + TILE]
                        .rearrange("(k p) t -> p k t", p=128)),
                    wait=[tok_cc1a, seg_rd[par], seg_dve_rd[par]])
                seg_ldB = SPD(
                    lambda s, ri=ri, co=co, par=par: s.dma_start(
                        xnseg[par][:, 4:8, :],
                        agx_outB[512 * ri:512 * (ri + 1), co:co + TILE]
                        .rearrange("(k p) t -> p k t", p=128)),
                    wait=[tok_cc1b])
                seg_ld[par] = seg_ldB
                tok_xp = None
                for j in range(4):
                    kb = 4 * seg + j
                    jj = (seg % 2) * 4 + j
                    jc = slice(j * 128, (j + 1) * 128)
                    pv = pC if kb % 2 == 0 else pD
                    if j == 0:
                        stw = [seg_ldA, seg_ldB, seg_ld[1 - par], S.now("pe")]
                        for kc in range(8):
                            DVE(lambda v, kc=kc, par=par:
                                v.tensor_copy(xprev[:, kc, 1:128],
                                              xnseg[par][:, kc, 0:127]),
                                wait=stw if kc == 0 else ())
                            if seg == 0:
                                DVE(lambda v, kc=kc: v.memset(xprev[:, kc, 0:1],
                                                              0.0))
                            else:
                                DVE(lambda v, kc=kc, par=par:
                                    v.tensor_copy(xprev[:, kc, 0:1],
                                                  xnseg[1 - par][:, kc,
                                                                 TILE - 1:TILE]))
                        tok_xp = S.now("dve")
                        seg_dve_rd[1 - par] = tok_xp
                    itemsA = [(pv[:, 0:WVQK], xnseg[par][:, kc, jc],
                               wvqk_bf[:, kc, :]) for kc in range(4)]
                    itemsB = [(pv[:, 0:WVQK], xnseg[par][:, kc, jc],
                               wvqk_bf[:, kc, :]) for kc in range(4, 8)]
                    for kc in range(8):
                        lhs = (xprev[:, kc, :] if j == 0 else
                               xnseg[par][:, kc, j * 128 - 1:(j + 1) * 128 - 1])
                        (itemsA if kc < 4 else itemsB).append(
                            (pv[:, 128:144], lhs, wqb_bf[:, kc, :]))
                    itemsB.append((pv[:, 0:128], ones_bf[:, 0:128],
                                   brow_bf[:, 0:128]))
                    nA, nB = len(itemsA), len(itemsB)
                    tok_pv = None
                    for ii, (o_, l_, r_) in enumerate(itemsA + itemsB):
                        wl = ()
                        if ii == 0:
                            wl = [seg_ldA, tok_wat, tok_vones,
                                  tok_xp if j == 0 else None] + pvk_rd[kb % 2]
                        elif ii == nA:
                            wl = [seg_ldB]
                        tok_pv = S.on(
                            "pe",
                            (lambda t, o_=o_, l_=l_, r_=r_, st_=(ii == 0),
                             sp_=(ii == nA + nB - 1):
                             t.matmul(o_, l_, r_, start=st_, stop=sp_)),
                            wait=wl,
                            inc="pe" if ii == nA + nB - 1 else None)
                    ev = []
                    for hh in range(HL):
                        ev.append(ACT(lambda a, hh=hh, kb=kb, pv=pv:
                                      a.copy(vaug[:, kb, 65 * hh:65 * hh + 64],
                                             pv[:, 64 * hh:64 * hh + 64]),
                                      wait=[tok_pv]))
                    ev.append(ACT(lambda a, jj=jj, pr=pr, pv=pv:
                                  a.copy(qk12[pr][:, jj, :], pv[:, 128:160]),
                                  wait=[tok_pv] + (qk_rd[pr] if jj == 0 else [])))
                    pvk_rd[kb % 2] = ev
                seg_rd[par] = S.now("pe")
                if seg % 2 == 0:
                    continue
                flush_tr()
                tok_qk = S.now("act")
                norm_toks = []
                qk_rd[pr] = []
                for side in ("q", "k"):
                    ENG = POOL if side == "q" else DVE
                    colbase = 0 if side == "q" else 16
                    Ldst = lq_f if side == "q" else lk_f
                    sqd = sq24 if side == "q" else sq24k
                    n2d = n2q if side == "q" else n2k
                    rnd = rn_q if side == "q" else rn_k
                    pa_ = prd if side == "q" else prdk
                    pb_ = prd2 if side == "q" else prd2k
                    tk = [tok_qk]
                    for p, (i, jx) in enumerate(_PAIRS):
                        t1_ = ENG(lambda g, i=i, jx=jx, pr=pr, cb_=colbase, pa_=pa_:
                                  g.tensor_mul(pa_[:, :],
                                               fv(qk12[pr], [(32, 8), (4, 2)],
                                                  cb_ + i),
                                               fv(qk12[pr], [(32, 8), (4, 2)],
                                                  cb_ + 8 + jx)),
                                  wait=tk if isinstance(tk, list) else [tk])
                        t2_ = ENG(lambda g, i=i, jx=jx, pr=pr, cb_=colbase, pb_=pb_:
                                  g.tensor_mul(pb_[:, :],
                                               fv(qk12[pr], [(32, 8), (4, 2)],
                                                  cb_ + jx),
                                               fv(qk12[pr], [(32, 8), (4, 2)],
                                                  cb_ + 8 + i)),
                                  wait=[t1_])
                        tk = ENG(lambda g, p=p, Ldst=Ldst, pa_=pa_, pb_=pb_:
                                 g.tensor_sub(fv(Ldst, [(CP, 8), (6, 2)], p),
                                              pa_[:, :], pb_[:, :]),
                                 wait=[t2_])
                    qk_rd[pr].append(tk)
                    tk = ENG(lambda g, Ldst=Ldst, sqd=sqd:
                             g.tensor_mul(sqd[:, :, :], Ldst[:, :, :],
                                          Ldst[:, :, :]), wait=[tk])
                    tk = DVE(lambda v, n2d=n2d, sqd=sqd:
                             v.tensor_reduce(n2d[:, :],
                                             fv(sqd, [(CP, 8), (6, 2), (1, 6)]),
                                             AX.X, ALU.add), wait=[tk])
                    tok_n2 = DVE(lambda v, n2d=n2d:
                                 v.tensor_scalar_max(n2d[:, :], n2d[:, :], 1e-24),
                                 wait=[tk])
                    tok_lnr = ACT(lambda a, rnd=rnd, n2d=n2d:
                                  a.activation(rnd[:, :], n2d[:, :], AF.Ln),
                                  wait=[tok_n2])
                    tok_rn = ACT(lambda a, rnd=rnd:
                                 a.activation(rnd[:, :], rnd[:, :], AF.Exp,
                                              scale=-0.5), wait=[tok_lnr])
                    if side == "q":
                        tkn = None
                        for p in range(6):
                            tkn = DVE(lambda v, p=p, pr=pr: v.scalar_tensor_tensor(
                                fv(lq24[pr % 2], [(CP, 8), (6, 2)], p),
                                fv(lq_f, [(CP, 8), (6, 2)], p), 1.0,
                                fv(rn_q, [(2, 8), (1, 2)]), ALU.mult, ALU.mult),
                                wait=[tok_rn] if p == 0 else [tkn])
                        norm_toks.append(tkn)
                    else:
                        tkn = None
                        for c_ in range(6):
                            tkn = DVE(lambda v, c_=c_, pr=pr: v.scalar_tensor_tensor(
                                fv(jlk24[pr % 2], [(CP, 8), (6, 2)], c_),
                                fv(lk_f, [(CP, 8), (6, 2)], 5 - c_), _JSIGN[c_],
                                fv(rn_k, [(2, 8), (1, 2)]), ALU.mult, ALU.mult),
                                wait=[tok_rn] if c_ == 0 else [tkn])
                        norm_toks.append(tkn)
                tk = norm_toks
                pend_tr = (pr % 2, seg // 2, tk)
            flush_tr()
            tok_proj = [S.now("pe"), S.now("dve"), S.now("act"), S.now("pool")]

            if stage <= 3:
                return

            scst = {"exp_rd": {(h, i): [] for h in (0, 1) for i in (0, 1)},
                    "pacc_rd": {0: [], 1: []}, "pbc_rd": [],
                    "t64_rd": {0: [], 1: []}, "rc": {0: None, 1: None},
                    "lqm_tok": {}, "lqm_rd": {}}
            pend_ev = None

            def emit_lqm(qt):
                for h in (0, 1):
                    buf = lqm[h][qt % 2]
                    scst["lqm_tok"][(qt, h)] = DVE(
                        lambda v, qt=qt, h=h, buf=buf: v.tensor_scalar_mul(
                            buf[:, :], lqt[:, qt * TILE:(qt + 1) * TILE],
                            hmask[:, h:h + 1]),
                        wait=tok_proj + scst["lqm_rd"].get((qt - 2, h), []))

            def flush_ev():
                nonlocal pend_ev
                if pend_ev is None:
                    return
                qt_, at_, rcs, t64s, aw_ = pend_ev
                bc0 = PE(lambda t: t.matmul(pE_[0:64, :], onesr_f[:, 0:64],
                                            s_rec[0][:, :], start=True, stop=True),
                         wait=[rcs[0]] + scst["pbc_rd"])
                bc1 = PE(lambda t: t.matmul(pE_[64:128, :], onesr_f[:, 0:64],
                                            s_rec[1][:, :], start=True, stop=True),
                         wait=[rcs[1]])
                at0 = DVE(lambda v, at_=at_:
                          v.tensor_mul(attno[at_][0:64, :], tmp64[0][:, :],
                                       pE_[0:64, :]),
                          wait=[bc1, t64s[0]] + scst["t64_rd"][0] + aw_)
                at1 = DVE(lambda v, at_=at_:
                          v.tensor_mul(attno[at_][64:128, :], tmp64[1][:, :],
                                       pE_[64:128, :]),
                          wait=[t64s[1]] + scst["t64_rd"][1])
                SPD(lambda s, sq=sq, qt_=qt_, at_=at_: s.dma_start(
                    aga_in[128 * sq:128 * sq + 128,
                           qt_ * TILE:(qt_ + 1) * TILE],
                    attno[at_][:, :]), wait=[at1])
                tok_attno_rd[at_] = S.now("xld")
                scst["pbc_rd"] = [at1]
                scst["t64_rd"] = {0: [at0], 1: [at1]}
                pend_ev = None

            for qt in range(NSEG):
                at = qt % 2
                aw = [tok_attno_rd[at]]
                if (qt, 0) not in scst["lqm_tok"]:
                    emit_lqm(qt)
                tok_exp = {}
                last_mm1 = {}
                last_mm2 = {0: None, 1: None}

                def emit_mm2(kp):
                    for h in (0, 1):
                        pacc = pC if h == 0 else pD
                        pp = kp % 2
                        for sub in range(2):
                            kb = 2 * kp + sub
                            i = kb - 4 * qt
                            off = 128 * i if i > 0 else 0
                            last_mm2[h] = S.on(
                                "pe",
                                (lambda t, pacc=pacc, h=h, pp=pp, sub=sub,
                                 off=off, kb=kb, qt=qt:
                                 t.matmul(pacc[0:65, off:TILE],
                                          vaug[:, kb, 65 * h:65 * h + 65],
                                          exp_t[h][pp][:, sub * 512 + off:
                                                       sub * 512 + 512],
                                          start=(kb == 0),
                                          stop=(kb == 4 * qt + 3))),
                                wait=(tok_exp[(kp, h)] +
                                      (scst["pacc_rd"][h] if kb == 0 else []))
                                if sub == 0 else (),
                                inc="pe")
                        scst["exp_rd"][(h, pp)] = [last_mm2[h]]

                for kp in range(2 * qt + 2):
                    pp = kp % 2
                    for sub in range(2):
                        kb = 2 * kp + sub
                        i = kb - 4 * qt
                        off = 128 * i if i > 0 else 0
                        for h in (0, 1):
                            pS = pA if h == 0 else pB
                            psw = (tok_exp[(kp - 1, h)][0:1]
                                   if (kp - 1, h) in tok_exp else [])
                            last_mm1[h] = S.on(
                                "pe",
                                (lambda t, pS=pS, h=h, sub=sub, off=off, kb=kb,
                                 qt=qt:
                                 t.matmul(pS[:, sub * 512 + off:sub * 512 + 512],
                                          jlkt[:, kb * 128:(kb + 1) * 128],
                                          lqm[h][qt % 2][:, off:TILE],
                                          start=True, stop=True)),
                                wait=([scst["lqm_tok"][(qt, h)]] + psw +
                                      (scst["pacc_rd"][h] if kp == 0 else []))
                                if sub == 0 else (),
                                inc="pe")
                    if kp == 1:
                        flush_ev()
                        if qt + 1 < NSEG:
                            emit_lqm(qt + 1)
                    for h in (0, 1):
                        pS = pA if h == 0 else pB
                        t_exp = ACT(
                            lambda a, h=h, pp=pp, pS=pS:
                            a.activation(exp_t[h][pp][:, :], pS[:, :], AF.Exp,
                                         scale=SCALE),
                            wait=[last_mm1[h]] + scst["exp_rd"][(h, pp)])
                        trtoks = []
                        for sub in range(2):
                            kb = 2 * kp + sub
                            i = kb - 4 * qt
                            if 0 <= i < 4:
                                trtoks.append(DVE(
                                    lambda v, h=h, pp=pp, sub=sub, i=i:
                                    v.tensor_mul(
                                        exp_t[h][pp][:, sub * 512 + 128 * i:
                                                     sub * 512 + 128 * i + 128],
                                        exp_t[h][pp][:, sub * 512 + 128 * i:
                                                     sub * 512 + 128 * i + 128],
                                        tri_bf[:, :]),
                                    wait=[t_exp]))
                        tok_exp[(kp, h)] = [t_exp] + trtoks
                    if kp > 0:
                        emit_mm2(kp - 1)
                emit_mm2(2 * qt + 1)
                rcs = {}
                t64s = {}
                for h in (0, 1):
                    pacc = pC if h == 0 else pD
                    tok_sr = ACT(lambda a, pacc=pacc, h=h:
                                 a.copy(s_row[h][:, :], pacc[64:65, :]),
                                 wait=[last_mm2[h], scst["rc"][h]])
                    rcs[h] = DVE(lambda v, h=h:
                                 v.reciprocal(s_rec[h][:, :], s_row[h][:, :]),
                                 wait=[tok_sr])
                    scst["rc"][h] = rcs[h]
                    t64s[h] = ACT(lambda a, pacc=pacc, h=h:
                                  a.copy(tmp64[h][:, :], pacc[0:64, :]),
                                  wait=[last_mm2[h]] + scst["t64_rd"][h])
                    scst["pacc_rd"][h] = [t64s[h], tok_sr]
                    scst["lqm_rd"][(qt, h)] = [last_mm2[h]]
                pend_ev = (qt, at, rcs, t64s, aw)
            flush_ev()
            tok_cc2[sq] = S.on("pool",
                               lambda g, sq=sq: g.collective_compute(
                                   "AllGather", ALU.bypass,
                                   replica_groups=[list(range(NCORE))],
                                   ins=[aga_in[128 * sq:128 * sq + 128, :].opt()],
                                   outs=[aga_out[1024 * sq:1024 * (sq + 1), :].opt()]),
                               wait=[S.now("xld")], inc="cc2")

        if stage <= 4:
            return
        attg_w = [tok_cc2[0], tok_cc2[1], S.now("pe"), S.now("dve"), S.now("act")]
        for dc in range(8):
            def rd(s, dc=dc):
                p = get_pid(s)
                return s.dma_start(
                    attg[:, dc, :],
                    aga_out[bass.ds(p["row"] + dc * 128, 128),
                            bass.ds(p["col"], NT)])
            SPD(rd, wait=attg_w)
        tok_attg = S.now("xld")
        pw_rd = {0: [], 1: []}
        for tt in range(NTILE):
            cols = slice(tt * TILE, (tt + 1) * TILE)
            for dc in range(8):
                dcp = dc % 2
                pw = pC if dcp == 0 else pD
                items = [(pw[:, :], wo_bf[:, kc, dc * 128:(dc + 1) * 128],
                          attg[:, kc, cols]) for kc in range(8)]
                tok_pw = mmgroup(items, wait=[tok_attg] + pw_rd[dcp])
                pw_rd[dcp] = [DVE(lambda v, dc=dc, cols=cols, pw=pw:
                                  v.scalar_tensor_tensor(X[:, dc, cols], pw[:, :],
                                                         bcol[:, 144 + dc:145 + dc],
                                                         X[:, dc, cols],
                                                         ALU.add, ALU.add),
                                  wait=[tok_pw])]

    if stage == 0:
        emit_ln()
    elif stage == 1:
        emit_gate(emit_ln())
    else:
        emit_ffn_pass()
        if stage >= 3:
            emit_attention()
            if stage >= 6:
                emit_ffn_pass()

    dbg_t = {}
    if dbg:
        S.barrier()
        for name in dbg:
            th = sb[name]
            shp = list(th.shape)
            nfree = int(np.prod(shp[1:]))
            dbg_t[name] = nc.dram_tensor("dbg_" + name, [shp[0], nfree], F32,
                                         kind="ExternalOutput")
            src = bass.AP(th, 0, [list(th[:].ap[0]), [1, nfree]])
            WLD(lambda g, name=name, src=src: g.dma_start(dbg_t[name][:, :], src))
        S.on("pool", None, wait=[S.now("wld")])

    S.barrier()
    for dc in range(8):
        STD(lambda s, dc=dc: s.dma_start(xout[dc * 128:(dc + 1) * 128, :],
                                         X[:, dc, :]))
    S.on("sp", None, wait=[S.now("st")])

    engmap = {"pe": "tensor", "act": "scalar", "dve": "vector",
              "pool": "gpsimd", "sp": "sync"}
    with nc.Block() as blk:
        for engname in Sch.ENGS:
            items = S.prog[engname]

            def body(eng, items=items):
                for fn, waits, inc in items:
                    for s_, v in waits:
                        eng.wait_ge(sems[s_], v)
                    if fn is not None:
                        ins = fn(eng)
                        if inc is not None:
                            ins.then_inc(sems[inc[0]], inc[1])
            getattr(blk, engmap[engname])(body)
    return nc


def _stage_inputs(inputs):
    f32 = np.float32
    x = np.asarray(inputs["L0_attn_x"], f32)
    xf = np.ascontiguousarray(x.reshape(B * T, D))
    gwT = np.ascontiguousarray(np.asarray(inputs["L0_ffn_gate_w"], f32).T)
    we1 = np.ascontiguousarray(np.asarray(inputs["L0_ffn_We1"], f32))
    we3 = np.ascontiguousarray(np.asarray(inputs["L0_ffn_We3"], f32))
    we2 = np.ascontiguousarray(np.asarray(inputs["L0_ffn_We2"], f32))
    ws1 = np.ascontiguousarray(np.asarray(inputs["L0_ffn_ws1"], f32))
    ws3 = np.ascontiguousarray(np.asarray(inputs["L0_ffn_ws3"], f32))
    ws2 = np.ascontiguousarray(np.asarray(inputs["L0_ffn_ws2"], f32))
    be1 = np.asarray(inputs["L0_ffn_be1"], f32)
    be3 = np.asarray(inputs["L0_ffn_be3"], f32)
    be2 = np.ascontiguousarray(np.asarray(inputs["L0_ffn_be2"], f32))
    bs1 = np.asarray(inputs["L0_ffn_bs1"], f32)
    bs3 = np.asarray(inputs["L0_ffn_bs3"], f32)
    bs2 = np.asarray(inputs["L0_ffn_bs2"], f32)
    gate_b = np.asarray(inputs["L0_ffn_gate_b"], f32)
    w1q = np.asarray(inputs["L0_attn_W1_q"], f32)
    w2q = np.asarray(inputs["L0_attn_W2_q"], f32)
    w1k = np.asarray(inputs["L0_attn_W1_k"], f32)
    w2k = np.asarray(inputs["L0_attn_W2_k"], f32)
    wv = np.asarray(inputs["L0_attn_W_v"], f32)
    bv = np.asarray(inputs["L0_attn_b_v"], f32)
    wo = np.ascontiguousarray(np.asarray(inputs["L0_attn_W_o"], f32))
    bo = np.asarray(inputs["L0_attn_b_o"], f32)

    bcol_h = np.zeros((128, NBCOL), f32)
    for e in range(E):
        for ic in range(4):
            bcol_h[:, 4 * e + ic] = be1[e, 128 * ic:128 * (ic + 1)]
            bcol_h[:, 64 + 4 * e + ic] = be3[e, 128 * ic:128 * (ic + 1)]
    for s_ in range(2):
        for ic in range(4):
            bcol_h[:, 128 + 4 * s_ + ic] = \
                bs1[512 * s_ + 128 * ic:512 * s_ + 128 * (ic + 1)]
            bcol_h[:, 136 + 4 * s_ + ic] = \
                bs3[512 * s_ + 128 * ic:512 * s_ + 128 * (ic + 1)]
    for dc in range(8):
        bcol_h[:, 144 + dc] = bo[128 * dc:128 * (dc + 1)]

    ident = np.eye(128, dtype=f32)
    trih = np.triu(np.ones((128, 128), f32))
    hmask_h = np.zeros((CP, HL), f32)
    for j in range(HL):
        hmask_h[6 * j:6 * j + 6, j] = 1.0
    ebc_h = np.zeros((E, E * 128), f32)
    for e in range(E):
        ebc_h[e, 128 * e:128 * (e + 1)] = 1.0
    growf = np.ascontiguousarray(np.tile(gate_b, 4).reshape(1, 4 * E))

    in_maps = []
    for c in range(NCORE):
        qs = slice(8 * c, 8 * c + 8)
        vs = slice(128 * c, 128 * (c + 1))
        brow_h = np.zeros((1, NBROW), f32)
        brow_h[0, 0:128] = bv[vs]
        brow_h[0, 256:1280] = bs2
        brow_h[0, 1280:2304] = bo
        wvqk = np.concatenate(
            [wv[:, vs], w1q[:D, qs], w2q[:D, qs], w1k[:, qs], w2k[:, qs]], axis=1)
        wqb = np.concatenate([w1q[D:, qs], w2q[D:, qs]], axis=1)
        in_maps.append({
            "xin": np.ascontiguousarray(xf[NT * c:NT * (c + 1)].T),
            "we1": we1, "we3": we3, "we2": we2,
            "ws1": ws1, "ws3": ws3, "ws2": ws2,
            "gwT": gwT, "be2m": be2, "bcol_h": bcol_h,
            "brow_h": brow_h, "growf": growf,
            "wvqk": np.ascontiguousarray(wvqk),
            "wqb": np.ascontiguousarray(wqb),
            "wo": wo,
            "ident": ident, "trih": trih, "hmask_h": hmask_h, "ebc_h": ebc_h,
        })
    return in_maps


_NC_CACHE = {}


def kernel(**inputs):
    if "nc" not in _NC_CACHE:
        _NC_CACHE["nc"] = build_nc()
    nc = _NC_CACHE["nc"]
    in_maps = _stage_inputs(inputs)
    res = run_bass_kernel_spmd(nc, in_maps, core_ids=list(range(NCORE)))
    out = np.empty((B * T, D), np.float32)
    for c in range(NCORE):
        out[NT * c:NT * (c + 1)] = res.results[c]["xout"].T
    return out.reshape(B, T, D)


# version 16
# speedup vs baseline: 1.0862x; 1.0862x over previous
import numpy as np
import ml_dtypes
from contextlib import ExitStack
import concourse.bass as bass
import concourse.mybir as mybir
from concourse.bass_utils import run_bass_kernel_spmd

F32 = mybir.dt.float32
BF16 = mybir.dt.bfloat16
AF = mybir.ActivationFunctionType
ALU = mybir.AluOpType
AX = mybir.AxisListType

NCORE = 8
B, T, D = 2, 8192, 1024
HL = 2
CP = HL * 6
NT = 2048
NBLK = NT // 128
TILE = 512
NTILE = NT // TILE
NKB = T // 128
NSEG = T // TILE
E = 16
EI = 512
NE = 18
SCALE = 6.0 ** -0.5
LN_EPS = 1e-5

NBCOL = 152
NBROW = 2304
WVQK = 160

_PAIRS = [(0, 1), (0, 2), (0, 3), (1, 2), (1, 3), (2, 3)]
_JSIGN = [1.0, -1.0, 1.0, 1.0, -1.0, 1.0]


class Sch:
    ENGS = ("pe", "act", "dve", "pool", "sp")

    def __init__(self):
        self.prog = {e: [] for e in self.ENGS}
        self.cnt = {}
        self.lastwait = {e: {} for e in self.ENGS}

    def on(self, eng, fn, wait=(), inc=None, incn=1):
        wl = []
        for tok in wait:
            if tok is None:
                continue
            s, v = tok
            if v > self.lastwait[eng].get(s, 0):
                wl.append((s, v))
                self.lastwait[eng][s] = v
        res = None
        if inc is not None:
            self.cnt[inc] = self.cnt.get(inc, 0) + incn
            res = (inc, self.cnt[inc])
        self.prog[eng].append((fn, tuple(wl), (inc, incn) if inc else None))
        return res

    def now(self, sem):
        return (sem, self.cnt.get(sem, 0))

    def barrier(self):
        snap = [(s, c) for s, c in self.cnt.items() if c > 0]
        for e in self.ENGS:
            self.on(e, None, wait=snap)


def fv(tsr, dims, off=0):
    base = tsr[:]
    return bass.AP(base.tensor, off, [list(base.ap[0])] +
                   [[s0, c0] for s0, c0 in dims])


def build_nc(stage=6, dbg=()):
    nc = bass.Bass("TRN2", target_bir_lowering=False, debug=False)
    dt = {}

    def din(name, shape):
        dt[name] = nc.dram_tensor(name, list(shape), F32, kind="ExternalInput")

    din("xin", (D, NT))
    din("we1", (E, D, EI))
    din("we3", (E, D, EI))
    din("we2", (E, EI, D))
    din("ws1", (D, D))
    din("ws3", (D, D))
    din("ws2", (D, D))
    din("gwT", (D, E))
    din("be2m", (E, D))
    din("bcol_h", (128, NBCOL))
    din("brow_h", (1, NBROW))
    din("growf", (1, 4 * E))
    din("wvqk", (D, WVQK))
    din("wqb", (D, 16))
    din("wo", (D, D))
    din("ident", (128, 128))
    din("trih", (128, 128))
    din("hmask_h", (CP, HL))
    din("ebc_h", (E, E * 128))

    xout = nc.dram_tensor("xout", [D, NT], F32, kind="ExternalOutput")
    agx_in = nc.dram_tensor("agx_in", [D, NT], BF16, kind="Internal")
    agx_out = nc.dram_tensor("agx_out", [NCORE * D, NT], BF16, kind="Internal")
    aga_in = nc.dram_tensor("aga_in", [B * 128, T], BF16, kind="Internal")
    aga_out = nc.dram_tensor("aga_out", [B * NCORE * 128, T], BF16, kind="Internal")

    sb = {}

    def sbuf(name, shape, dtype):
        sb[name] = nc.alloc_sbuf_tensor(name, list(shape), dtype)
        return sb[name]

    X = sbuf("X", (128, 8, NT), F32)
    id_bf = sbuf("id_bf", (128, 128), BF16)
    tri_bf = sbuf("tri_bf", (128, 128), BF16)
    ones_bf = sbuf("ones_bf", (1, TILE), BF16)
    onesr_f = sbuf("onesr_f", (1, 128), F32)
    onesc_f = sbuf("onesc_f", (128, 1), F32)
    bcol = sbuf("bcol", (128, NBCOL), F32)
    brow_bf = sbuf("brow_bf", (1, NBROW), BF16)
    be2_bf = sbuf("be2_bf", (E, D), BF16)
    gwT_bf = sbuf("gwT_bf", (128, 8, E), BF16)
    ebc_bf = sbuf("ebc_bf", (E, E * 128), BF16)
    gbb = sbuf("gbb", (128, 4, E), F32)
    hmask = sbuf("hmask", (CP, HL), F32)
    growf_s = sbuf("growf_s", (1, 4 * E), F32)
    s_row = [sbuf(f"s_row{h}", (1, TILE), F32) for h in (0, 1)]
    s_rec = [sbuf(f"s_rec{h}", (1, TILE), F32) for h in (0, 1)]
    r_tmp2 = s_row[0]
    g_eg = sbuf("g_eg", (128, 4, E), F32)
    g_or = sbuf("g_or", (128, 4, E), F32)
    g_bi = sbuf("g_bi", (128, 4, E), F32)
    g_s1 = sbuf("g_s1", (128, 4, 15), F32)
    g_s2 = sbuf("g_s2", (128, 4, 14), F32)
    g_s3 = sbuf("g_s3", (128, 4, 13), F32)
    g_G = sbuf("g_G", (128, 4, 4), F32)
    g_r2 = sbuf("g_r2", (128, 4, 4), F32)
    g_x = sbuf("g_x", (128, 4, 3), F32)
    g_cnt = sbuf("g_cnt", (128, 4, 4), F32)
    g_pen = sbuf("g_pen", (128, 4, 4), F32)
    g_sm = sbuf("g_sm", (128, 4, E), F32)
    g_mx8 = sbuf("g_mx8", (128, 8), F32)
    g_ss = sbuf("g_ss", (128, 4), F32)
    g_rs = sbuf("g_rs", (128, 4), F32)
    cb_bf = sbuf("cb_bf", (128, 4, E), BF16)
    lq_f = sbuf("lq_f", (128, 8, CP), F32)
    lk_f = sbuf("lk_f", (128, 8, CP), F32)
    sq24 = sbuf("sq24", (128, 8, CP), F32)
    n2q = sbuf("n2q", (128, 16), F32)
    n2k = sbuf("n2k", (128, 16), F32)
    rn_q = sbuf("rn_q", (128, 16), F32)
    rn_k = sbuf("rn_k", (128, 16), F32)
    prd = sbuf("prd", (128, 16), F32)
    prd2 = sbuf("prd2", (128, 16), F32)
    prdk = sbuf("prdk", (128, 16), F32)
    prd2k = sbuf("prd2k", (128, 16), F32)
    sq24k = sbuf("sq24k", (128, 8, CP), F32)
    lq24 = [sbuf(f"lq24_{i}", (128, 8, CP), BF16) for i in (0, 1)]
    jlk24 = [sbuf(f"jlk24_{i}", (128, 8, CP), BF16) for i in (0, 1)]
    xprev = sbuf("xprev", (128, 8, 128), BF16)

    OVL = 110 * 1024
    ovl_back = nc.alloc_sbuf_tensor("ovl_back", [128, OVL], mybir.dt.uint8)
    ovl_base = nc.lookup_mloc(ovl_back).addr

    def oalloc(name, shape, dtype, off):
        h = nc.virtual_sbuf_tensor(name, list(shape), dtype)
        nc._tensor(h.name, list(h.shape), h.dtype, type="SB")
        mloc = nc.lookup_mloc(h)
        mloc.allocated = True
        mloc.addr = ovl_base + off
        nbytes = int(np.prod(shape[1:])) * mybir.dt.size(dtype)
        assert off % 32 == 0 and off + nbytes <= OVL, (name, off, nbytes)
        sb[name] = h
        return h

    KB = 1024
    xn_bf = oalloc("xn_bf", (128, 8, NT), BF16, 0)
    w1b = [oalloc(f"w1b{i}", (128, 8, EI), BF16, 32 * KB + 8 * KB * i) for i in (0, 1)]
    w3b = [oalloc(f"w3b{i}", (128, 8, EI), BF16, 48 * KB + 8 * KB * i) for i in (0, 1)]
    w2b = [oalloc(f"w2b{i}", (128, 4, D), BF16, 64 * KB + 8 * KB * i) for i in (0, 1)]
    h_bf = [oalloc(f"h_bf{i}", (128, 4, TILE), BF16, 80 * KB + 4 * KB * i) for i in (0, 1)]
    sil_t = [oalloc(f"sil_t{i}", (128, TILE), F32, 88 * KB + 2 * KB * i) for i in (0, 1)]
    sq_t2 = [oalloc(f"sq_t{i}", (128, TILE), F32, 92 * KB + 2 * KB * i)
             for i in (0, 1)]
    t1_t = oalloc("t1_t", (128, TILE), F32, 106 * KB)
    cbT_bf = oalloc("cbT_bf", (E, NT), BF16, 96 * KB)
    jlkt = oalloc("jlkt", (CP, T), BF16, 0)
    lqt = oalloc("lqt", (CP, T), BF16, 16 * KB)
    vaug = oalloc("vaug", (128, NKB, HL * 65), BF16, 32 * KB)
    attg = oalloc("attg", (128, 8, NT), BF16, 0)
    xnseg = [oalloc(f"xnseg{i}", (128, 8, TILE), BF16, 66 * KB + 8 * KB * i)
             for i in (0, 1)]
    wo_bf = oalloc("wo_bf", (128, 8, D), BF16, 82 * KB)
    wvqk_bf = oalloc("wvqk_bf", (128, 8, WVQK), BF16, 98 * KB)
    wqb_bf = oalloc("wqb_bf", (128, 8, 16), BF16, 101 * KB)
    exp_t = [[oalloc(f"exp_t{h}_{i}", (128, 1024), BF16,
                     (102 * KB if h == 0 else 55552) + 2 * KB * i)
              for i in (0, 1)] for h in (0, 1)]
    lqm = [[oalloc(f"lqm{h}_{i}", (CP, TILE), BF16,
                   (106 + 2 * h + i) * KB) for i in (0, 1)] for h in (0, 1)]
    r_mean = oalloc("r_mean", (1, TILE), F32, 80 * KB)
    r_tmp = oalloc("r_tmp", (1, TILE), F32, 82 * KB)
    qk12 = [oalloc(f"qk12{i}", (128, 8, 32), F32, 59648 + 1 * KB * i)
            for i in (0, 1, 2)]
    tmp64 = [oalloc(f"tmp64{h}", (64, TILE), F32, 62720 + 2 * KB * h)
             for h in (0, 1)]
    attno = [oalloc(f"attno{i}", (128, TILE), BF16, 53504 + 1 * KB * i)
             for i in (0, 1)]

    pA = nc.alloc_psum_tensor("pA", [128, 1024], F32)
    pB = nc.alloc_psum_tensor("pB", [128, 1024], F32)
    pC = nc.alloc_psum_tensor("pC", [128, 512], F32)
    pD = nc.alloc_psum_tensor("pD", [128, 512], F32)
    pE_ = nc.alloc_psum_tensor("pE", [128, 512], F32)
    pbf = nc.alloc_psum_tensor("pbf", [128, 1024], BF16)

    semn = ["wld", "xld", "st", "pe", "act", "dve", "pool", "cc1", "cc2"]
    sems = {n: nc.alloc_semaphore(n) for n in semn}
    S = Sch()
    pidc = {}

    def get_pid(s):
        if "row" not in pidc:
            pid = s.partition_id()
            pidc["b"] = pid // 4
            pidc["row"] = (pid // 4) * 1024
            pidc["col"] = (pid - (pid // 4) * 4) * NT
        return pidc


    def PE(fn, wait=()):
        return S.on("pe", fn, wait=wait, inc="pe")

    def ACT(fn, wait=()):
        return S.on("act", fn, wait=wait, inc="act")

    def DVE(fn, wait=()):
        return S.on("dve", fn, wait=wait, inc="dve")

    def POOL(fn, wait=()):
        return S.on("pool", fn, wait=wait, inc="pool")

    def SPD(fn, wait=()):
        return S.on("sp", fn, wait=wait, inc="xld", incn=16)

    def WLD(fn, wait=()):
        return S.on("pool", fn, wait=wait, inc="wld", incn=16)

    def STD(fn, wait=()):
        return S.on("sp", fn, wait=wait, inc="st", incn=16)

    def mmgroup(items, wait=()):
        n = len(items)
        tok = None
        for i, (o, l, r) in enumerate(items):
            tok = S.on(
                "pe",
                (lambda t, o=o, l=l, r=r, st_=(i == 0), sp_=(i == n - 1):
                 t.matmul(o, l, r, start=st_, stop=sp_)),
                wait=wait if i == 0 else (),
                inc="pe" if i == n - 1 else None)
        return tok

    def _first(s):
        get_pid(s)
        return s.dma_start(bcol[:, :], dt["bcol_h"][:, :])
    SPD(_first)
    SPD(lambda s: s.dma_start(hmask[:, :], dt["hmask_h"][:, :]))
    SPD(lambda s: s.dma_start(growf_s[:, :], dt["growf"][:, :]))
    for dc in range(8):
        SPD(lambda s, dc=dc: s.dma_start(X[:, dc, :],
                                         dt["xin"][dc * 128:(dc + 1) * 128, :]))
    tok_xload = S.now("xld")

    WLD(lambda g: g.dma_start(id_bf[:, :], dt["ident"][:, :]))
    WLD(lambda g: g.dma_start(tri_bf[:, :], dt["trih"][:, :]))
    WLD(lambda g: g.dma_start(brow_bf[:, :], dt["brow_h"][:, :]))
    WLD(lambda g: g.dma_start(be2_bf[:, :], dt["be2m"][:, :]))
    WLD(lambda g: g.dma_start(ebc_bf[:, :], dt["ebc_h"][:, :]))
    WLD(lambda g: g.dma_start(gwT_bf[:, :, :],
                              dt["gwT"][:, :].rearrange("(k p) e -> p k e", p=128)))
    tok_const = S.now("wld")

    DVE(lambda v: v.memset(ones_bf[:, :], 1.0))
    DVE(lambda v: v.memset(onesr_f[:, :], 1.0))
    tok_ones = DVE(lambda v: v.memset(onesc_f[:, :], 1.0))

    PE(lambda t: t.matmul(pC[:, 0:64], onesr_f[:, :], growf_s[:, :],
                          start=True, stop=True),
       wait=[tok_xload, tok_ones])
    tok_gbb = ACT(lambda a: a.copy(gbb[:, :, :], pC[:, 0:64]), wait=[S.now("pe")])

    def emit_ln():
        prev = [S.now("act"), S.now("dve"), S.now("pe")]
        prev_bm = prev_br = prev_xn = None
        xn_toks = []
        for tt in range(NTILE):
            cols = slice(tt * TILE, (tt + 1) * TILE)
            tok_sum = mmgroup(
                [(pD[0:1, :], onesc_f[:, :], X[:, dc, cols]) for dc in range(8)],
                wait=prev)
            tok_mm = None
            tok_mm2q = None
            for dc in range(8):
                sp_ = sq_t2[dc % 2]
                tok_sq = DVE(lambda v, dc=dc, cols=cols, sp_=sp_:
                             v.tensor_mul(sp_[:, :], X[:, dc, cols], X[:, dc, cols]),
                             wait=([tok_mm2q] if dc >= 2 else prev))
                tok_mm = S.on("pe",
                              (lambda t, st_=(dc == 0), sp2_=(dc == 7), sp_=sp_:
                               t.matmul(pE_[0:1, :], onesc_f[:, :], sp_[:, :],
                                        start=st_, stop=sp2_)),
                              wait=[tok_sq], inc="pe")
                if dc % 2 == 1:
                    tok_mm2q = tok_mm
            tok_mean = ACT(lambda a: a.activation(r_mean[:, :], pD[0:1, :], AF.Copy,
                                                  scale=1.0 / D),
                           wait=[tok_sum, prev_bm])
            tok_msq = ACT(lambda a: a.activation(r_tmp[:, :], pE_[0:1, :], AF.Copy,
                                                 scale=1.0 / D),
                          wait=[tok_mm, prev_br])
            tok_m2 = DVE(lambda v: v.tensor_mul(r_tmp2[:, :], r_mean[:, :],
                                                r_mean[:, :]), wait=[tok_mean])
            tok_var = DVE(lambda v: v.scalar_tensor_tensor(
                r_tmp[:, :], r_tmp[:, :], LN_EPS, r_tmp2[:, :],
                ALU.add, ALU.subtract), wait=[tok_msq, tok_m2])
            tok_lnv = ACT(lambda a: a.activation(r_tmp[:, :], r_tmp[:, :], AF.Ln),
                          wait=[tok_var])
            tok_rstd = ACT(lambda a: a.activation(r_tmp[:, :], r_tmp[:, :], AF.Exp,
                                                  scale=-0.5), wait=[tok_lnv])
            tok_nm = DVE(lambda v: v.tensor_scalar_mul(r_mean[:, :], r_mean[:, :],
                                                       -1.0), wait=[tok_var])
            prev_bm = PE(lambda t: t.matmul(pA[:, 0:512], onesr_f[:, :],
                                            r_mean[:, :], start=True, stop=True),
                         wait=[tok_nm, prev_xn])
            prev_br = PE(lambda t: t.matmul(pA[:, 512:1024], onesr_f[:, :],
                                            r_tmp[:, :], start=True, stop=True),
                         wait=[tok_rstd])
            for dc in range(8):
                DVE(lambda v, dc=dc, cols=cols:
                    v.tensor_add(t1_t[:, :], X[:, dc, cols], pA[:, 0:512]),
                    wait=[prev_bm, prev_br] if dc == 0 else ())
                prev_xn = DVE(lambda v, dc=dc, cols=cols:
                              v.tensor_mul(xn_bf[:, dc, cols], t1_t[:, :],
                                           pA[:, 512:1024]))
            prev = [prev_xn, prev_bm, prev_br, tok_rstd]
            xn_toks.append(prev_xn)
        return xn_toks

    def emit_gate(tok_xn):
        tok_tr_rd = None
        tok_pg_rd = None
        cbT_toks = []
        for tt in range(NTILE):
            toks_pg = []
            for b4 in range(4):
                bc = slice(tt * TILE + b4 * 128, tt * TILE + b4 * 128 + 128)
                toks_pg.append(mmgroup(
                    [(pC[:, 16 * b4:16 * b4 + 16], xn_bf[:, kc, bc],
                      gwT_bf[:, kc, :]) for kc in range(8)],
                    wait=[tok_xn[tt], tok_const, tok_pg_rd] if b4 == 0 else ()))
            tok_pg = toks_pg[-1]
            tok_exp = ACT(lambda a: a.activation(g_eg[:, :, :], pC[:, 0:64],
                                                 AF.Exp), wait=[tok_pg])
            tok_pg_rd = tok_exp
            tk = DVE(lambda v: v.tensor_reduce(g_ss[:, :],
                                               fv(g_eg, [(E, 4), (1, E)]),
                                               AX.X, ALU.add), wait=[tok_exp])
            tk = DVE(lambda v: v.reciprocal(g_rs[:, :], g_ss[:, :]), wait=[tk])
            for b4 in range(4):
                tk = DVE(lambda v, b4=b4: v.tensor_scalar_mul(
                    g_or[:, b4, :], g_eg[:, b4, :], g_rs[:, b4:b4 + 1]),
                    wait=[tk])
            tk = DVE(lambda v: v.tensor_tensor(g_bi[:, :, :], g_or[:, :, :],
                                               gbb[:, :, :], ALU.add),
                     wait=[tk, tok_gbb])
            tk = DVE(lambda v: v.tensor_add(fv(g_s1, [(15, 4), (1, 15)]),
                                            fv(g_bi, [(E, 4), (1, 15)]),
                                            fv(g_bi, [(E, 4), (1, 15)], 1)),
                     wait=[tk])
            tk = DVE(lambda v: v.tensor_add(fv(g_s2, [(14, 4), (1, 14)]),
                                            fv(g_bi, [(E, 4), (1, 14)]),
                                            fv(g_bi, [(E, 4), (1, 14)], 2)),
                     wait=[tk])
            tk = DVE(lambda v: v.tensor_add(fv(g_s3, [(13, 4), (1, 13)]),
                                            fv(g_bi, [(E, 4), (1, 13)]),
                                            fv(g_bi, [(E, 4), (1, 13)], 3)),
                     wait=[tk])
            tk = DVE(lambda v: v.tensor_reduce(
                g_G[:, :, :], fv(g_s1, [(15, 4), (4, 4), (1, 3)]),
                AX.X, ALU.max), wait=[tk])
            tk = DVE(lambda v: v.tensor_reduce(
                g_r2[:, :, :], fv(g_s2, [(14, 4), (4, 4), (1, 2)]),
                AX.X, ALU.max), wait=[tk])
            tk = DVE(lambda v: v.tensor_max(g_G[:, :, :], g_G[:, :, :],
                                            g_r2[:, :, :]), wait=[tk])
            tk = DVE(lambda v: v.tensor_max(g_G[:, :, :], g_G[:, :, :],
                                            fv(g_s3, [(13, 4), (4, 4)])),
                     wait=[tk])
            tk = DVE(lambda v: v.memset(g_cnt[:, :, :], 0.0), wait=[tk])
            for s_ in (1, 2, 3):
                tk = DVE(lambda v, s_=s_: v.tensor_tensor(
                    fv(g_x, [(3, 4), (1, 4 - s_)]),
                    fv(g_G, [(4, 4), (1, 4 - s_)], s_),
                    fv(g_G, [(4, 4), (1, 4 - s_)]), ALU.is_gt), wait=[tk])
                tk = DVE(lambda v, s_=s_: v.tensor_add(
                    fv(g_cnt, [(4, 4), (1, 4 - s_)]),
                    fv(g_cnt, [(4, 4), (1, 4 - s_)]),
                    fv(g_x, [(3, 4), (1, 4 - s_)])), wait=[tk])
                tk = DVE(lambda v, s_=s_: v.tensor_scalar(
                    fv(g_x, [(3, 4), (1, 4 - s_)]),
                    fv(g_x, [(3, 4), (1, 4 - s_)]),
                    -1.0, 1.0, ALU.mult, ALU.add), wait=[tk])
                tk = DVE(lambda v, s_=s_: v.tensor_add(
                    fv(g_cnt, [(4, 4), (1, 4 - s_)], s_),
                    fv(g_cnt, [(4, 4), (1, 4 - s_)], s_),
                    fv(g_x, [(3, 4), (1, 4 - s_)])), wait=[tk])
            tk = DVE(lambda v: v.tensor_scalar(g_pen[:, :, :], g_cnt[:, :, :],
                                               1.5, -1e30, ALU.is_ge, ALU.mult),
                     wait=[tk])
            for j in range(4):
                tk = DVE(lambda v, j=j: v.tensor_add(
                    fv(g_sm, [(E, 4), (4, 4)], j),
                    fv(g_bi, [(E, 4), (4, 4)], j),
                    g_pen[:, :, :]), wait=[tk])
            for b4 in range(4):
                tk = DVE(lambda v, b4=b4: v.max(g_mx8[:, :], g_sm[:, b4, :]),
                         wait=[tk])
                tk = DVE(lambda v, b4=b4: v.tensor_single_scalar(
                    g_sm[:, b4, :], g_sm[:, b4, :], g_mx8[:, 1:2], ALU.is_ge),
                    wait=[tk])
            tok_cb = DVE(lambda v: v.tensor_mul(cb_bf[:, :, :], g_sm[:, :, :],
                                                g_or[:, :, :]), wait=[tk])
            for b4 in range(4):
                bc = slice(tt * TILE + b4 * 128, tt * TILE + b4 * 128 + 128)
                tok_tr = PE(lambda t, b4=b4: t.transpose(
                    pbf[0:E, 128 * b4:128 * b4 + 128], cb_bf[:, b4, :],
                    id_bf[:, :]),
                    wait=[tok_cb, tok_tr_rd, tok_const] if b4 == 0 else ())
            tok_trl = S.now("pe")
            tok_cbT = DVE(lambda v, tt=tt: v.tensor_copy(
                cbT_bf[:, tt * TILE:(tt + 1) * TILE], pbf[0:E, 0:512]),
                wait=[tok_trl])
            tok_tr_rd = tok_cbT
            cbT_toks.append(tok_cbT)
        return cbT_toks

    wstate = {"last_use": [None, None]}

    def emit_wload(e):
        par = e % 2
        wa = wstate["last_use"][par]
        wa = wa if isinstance(wa, list) else [wa]
        if e < E:
            WLD(lambda g, e=e, par=par: g.dma_start(
                w1b[par][:, :, :],
                dt["we1"][e, :, :].rearrange("(k p) i -> p k i", p=128)), wait=wa)
            WLD(lambda g, e=e, par=par: g.dma_start(
                w3b[par][:, :, :],
                dt["we3"][e, :, :].rearrange("(k p) i -> p k i", p=128)))
            WLD(lambda g, e=e, par=par: g.dma_start(
                w2b[par][:, :, :],
                dt["we2"][e, :, :].rearrange("(k p) d -> p k d", p=128)))
        else:
            s_ = e - E
            WLD(lambda g, s_=s_, par=par: g.dma_start(
                w1b[par][:, :, :],
                dt["ws1"][:, s_ * EI:(s_ + 1) * EI]
                .rearrange("(k p) i -> p k i", p=128)), wait=wa)
            WLD(lambda g, s_=s_, par=par: g.dma_start(
                w3b[par][:, :, :],
                dt["ws3"][:, s_ * EI:(s_ + 1) * EI]
                .rearrange("(k p) i -> p k i", p=128)))
            WLD(lambda g, s_=s_, par=par: g.dma_start(
                w2b[par][:, 0:4, :],
                dt["ws2"][s_ * EI:(s_ + 1) * EI, :]
                .rearrange("(k p) d -> p k d", p=128)))
        return S.now("wld")

    def emit_experts(tok_cbT):
        fence = [S.now("act"), S.now("dve")]
        est = {"sil_rd": [fence, fence], "p13_rd": {0: fence, 1: fence},
               "h_rd": [fence, fence], "p2_rd": {0: fence, 1: fence},
               "be_rd": fence}
        if wstate.get("pre"):
            wtok = wstate.pop("pre")
        else:
            wtok = [emit_wload(0), emit_wload(1)]
        pend = None

        def emit_p2(par, h_par, tok_h, cols, e):
            tok_p2l = None
            for dc in range(8):
                dcp = dc % 2
                pP2 = pD if dcp == 0 else pE_
                items = [(pP2[:, :], w2b[par][:, kc, dc * 128:(dc + 1) * 128],
                          h_bf[h_par][:, kc, :]) for kc in range(4)]
                if e == 0:
                    items.append((pP2[:, :], be2_bf[:, dc * 128:(dc + 1) * 128],
                                  cbT_bf[:, cols]))
                if e == E + 1:
                    items.append((pP2[:, :],
                                  brow_bf[:, 256 + dc * 128:256 + dc * 128 + 128],
                                  ones_bf[:, :]))
                tok_p2 = mmgroup(items, wait=[tok_h] + est["p2_rd"][dcp])
                tok_ev = DVE(lambda v, dc=dc, cols=cols, pP2=pP2:
                             v.scalar_tensor_tensor(X[:, dc, cols], pP2[:, :],
                                                    0.5, X[:, dc, cols],
                                                    ALU.mult, ALU.add),
                             wait=[tok_p2])
                est["p2_rd"][dcp] = [tok_ev]
                tok_p2l = tok_p2
            est["h_rd"][h_par] = [tok_p2l]

        for e in range(NE):
            par = e % 2
            mywtok = wtok[par]
            for tt in range(NTILE):
                cols = slice(tt * TILE, (tt + 1) * TILE)
                if e < E:
                    tok_be = PE(lambda t, e=e, cols=cols:
                                t.matmul(pC[:, :], ebc_bf[:, e * 128:(e + 1) * 128],
                                         cbT_bf[:, cols], start=True, stop=True),
                                wait=[tok_cbT[tt]] + est["be_rd"])
                h_par = (e * NTILE + tt) % 2
                tok_h = None
                for ic in range(4):
                    icp = ic % 2
                    pP1 = pA[:, icp * 512:(icp + 1) * 512]
                    pP3 = pB[:, icp * 512:(icp + 1) * 512]
                    mmgroup([(pP1, w1b[par][:, kc, ic * 128:(ic + 1) * 128],
                              xn_bf[:, kc, cols]) for kc in range(8)],
                            wait=[mywtok] + est["p13_rd"][icp])
                    tok_p13 = mmgroup([(pP3, w3b[par][:, kc, ic * 128:(ic + 1) * 128],
                                        xn_bf[:, kc, cols]) for kc in range(8)])
                    bix = (4 * e + ic) if e < E else (128 + 4 * (e - E) + ic)
                    b3x = (64 + 4 * e + ic) if e < E else (136 + 4 * (e - E) + ic)
                    tok_sil = ACT(lambda a, bix=bix, icp=icp, pP1=pP1:
                                  a.activation(sil_t[icp][:, :], pP1, AF.Silu,
                                               bias=bcol[:, bix:bix + 1]),
                                  wait=[tok_p13] + est["sil_rd"][icp])
                    if e < E:
                        DVE(lambda v, b3x=b3x, icp=icp, pP3=pP3:
                            v.scalar_tensor_tensor(t1_t[:, :], pP3,
                                                   bcol[:, b3x:b3x + 1],
                                                   sil_t[icp][:, :],
                                                   ALU.add, ALU.mult),
                            wait=[tok_sil] + est["h_rd"][h_par])
                        tok_h = DVE(lambda v, ic=ic, h_par=h_par:
                                    v.tensor_mul(h_bf[h_par][:, ic, :], t1_t[:, :],
                                                 pC[:, :]),
                                    wait=[tok_be])
                    else:
                        tok_h = DVE(lambda v, b3x=b3x, icp=icp, ic=ic, h_par=h_par,
                                    pP3=pP3:
                                    v.scalar_tensor_tensor(h_bf[h_par][:, ic, :],
                                                           pP3,
                                                           bcol[:, b3x:b3x + 1],
                                                           sil_t[icp][:, :],
                                                           ALU.add, ALU.mult),
                                    wait=[tok_sil] + est["h_rd"][h_par])
                    est["sil_rd"][icp] = [tok_h]
                    est["p13_rd"][icp] = [tok_sil, tok_h]
                if e < E:
                    est["be_rd"] = [tok_h]
                if pend is not None:
                    emit_p2(*pend)
                    if pend[4] != e and pend[4] + 2 < NE:
                        wstate["last_use"][pend[0]] = S.now("pe")
                        wtok[pend[0]] = emit_wload(pend[4] + 2)
                pend = (par, h_par, tok_h, cols, e)
        emit_p2(*pend)

    def emit_ffn_pass():
        tok_xn = emit_ln()
        tok_cbT = emit_gate(tok_xn)
        emit_experts(tok_cbT)

    def emit_attention():
        tok_xn = emit_ln()
        for dc in range(8):
            SPD(lambda s, dc=dc: s.dma_start(agx_in[dc * 128:(dc + 1) * 128, :],
                                             xn_bf[:, dc, :]),
                wait=[tok_xn[-1]])
        tok_cc1 = S.on("pool",
                       lambda g: g.collective_compute(
                           "AllGather", ALU.bypass,
                           replica_groups=[list(range(NCORE))],
                           ins=[agx_in.ap().opt()], outs=[agx_out.ap().opt()]),
                       wait=[S.now("xld")], inc="cc1")
        wa = [S.now("pe"), S.now("dve"), S.now("act")]
        WLD(lambda g: g.dma_start(
            wvqk_bf[:, :, :], dt["wvqk"][:, :].rearrange("(k p) v -> p k v", p=128)),
            wait=wa)
        WLD(lambda g: g.dma_start(
            wqb_bf[:, :, :], dt["wqb"][:, :].rearrange("(k p) v -> p k v", p=128)))
        WLD(lambda g: g.dma_start(
            wo_bf[:, :, :], dt["wo"][:, :].rearrange("(k p) v -> p k v", p=128)))
        tok_wat = S.now("wld")

        tok_attno_rd = [None, None]
        tok_cc2 = {}
        for sq in range(B):
            tok_vones = DVE(
                lambda v: v.memset(
                    vaug[:, :, :].rearrange("p k (h c) -> p k h c", c=65)[:, :, :, 64],
                    1.0),
                wait=wa + [S.now("pe")])
            seg_ld = [None, None]
            seg_rd = [None, None]
            seg_dve_rd = [None, None]
            pE_rd = {"t": [S.now("act"), S.now("dve")]}
            qk_rd = [[], [], []]
            tr_rd = None
            pend_tr = None
            flush_lq = {}
            flush_jl = {}
            vtok = {}

            def flush_tr():
                nonlocal pend_tr, tr_rd
                if pend_tr is None:
                    return
                pr_, pi_, tok_chain = pend_tr
                for jj in range(8):
                    PE(lambda t, jj=jj, pr_=pr_: t.transpose(
                        pbf[0:CP, 128 * jj:128 * jj + 128],
                        lq24[pr_][:, jj, :], id_bf[:, :]),
                       wait=(tok_chain + [tok_const, tr_rd]) if jj == 0 else ())
                tok_q8 = S.now("pe")
                c1 = DVE(lambda v, pi_=pi_: v.tensor_copy(
                    lqt[:, pi_ * 1024:(pi_ + 1) * 1024], pbf[0:CP, :]),
                    wait=[tok_q8])
                flush_lq[pi_] = c1
                for jj in range(8):
                    PE(lambda t, jj=jj, pr_=pr_: t.transpose(
                        pbf[0:CP, 128 * jj:128 * jj + 128],
                        jlk24[pr_][:, jj, :], id_bf[:, :]),
                       wait=[c1] if jj == 0 else ())
                tok_k8 = S.now("pe")
                tr_rd = DVE(lambda v, pi_=pi_: v.tensor_copy(
                    jlkt[:, pi_ * 1024:(pi_ + 1) * 1024], pbf[0:CP, :]),
                    wait=[tok_k8])
                flush_jl[pi_] = tr_rd
                pend_tr = None

            scst = {"exp_rd": {(h, i): [] for h in (0, 1) for i in (0, 1)},
                    "pacc_rd": {0: [], 1: []},
                    "t64_rd": {0: [], 1: []}, "rc": {0: None, 1: None},
                    "lqm_tok": {}, "lqm_rd": {}}
            pend_ev = None

            def emit_lqm(qt):
                for h in (0, 1):
                    buf = lqm[h][qt % 2]
                    scst["lqm_tok"][(qt, h)] = DVE(
                        lambda v, qt=qt, h=h, buf=buf: v.tensor_scalar_mul(
                            buf[:, :], lqt[:, qt * TILE:(qt + 1) * TILE],
                            hmask[:, h:h + 1]),
                        wait=[flush_lq[qt // 2]] +
                        scst["lqm_rd"].get((qt - 2, h), []))

            def flush_ev():
                nonlocal pend_ev
                if pend_ev is None:
                    return
                qt_, at_, rcs, t64s, aw_ = pend_ev
                bc0 = PE(lambda t: t.matmul(pE_[0:64, :], onesr_f[:, 0:64],
                                            s_rec[0][:, :], start=True, stop=True),
                         wait=[rcs[0]] + pE_rd["t"])
                bc1 = PE(lambda t: t.matmul(pE_[64:128, :], onesr_f[:, 0:64],
                                            s_rec[1][:, :], start=True, stop=True),
                         wait=[rcs[1]])
                at0 = DVE(lambda v, at_=at_:
                          v.tensor_mul(attno[at_][0:64, :], tmp64[0][:, :],
                                       pE_[0:64, :]),
                          wait=[bc1, t64s[0]] + scst["t64_rd"][0] + aw_)
                at1 = DVE(lambda v, at_=at_:
                          v.tensor_mul(attno[at_][64:128, :], tmp64[1][:, :],
                                       pE_[64:128, :]),
                          wait=[t64s[1]] + scst["t64_rd"][1])
                SPD(lambda s, sq=sq, qt_=qt_, at_=at_: s.dma_start(
                    aga_in[128 * sq:128 * sq + 128,
                           qt_ * TILE:(qt_ + 1) * TILE],
                    attno[at_][:, :]), wait=[at1])
                tok_attno_rd[at_] = S.now("xld")
                pE_rd["t"] = [at0, at1]
                scst["t64_rd"] = {0: [at0], 1: [at1]}
                pend_ev = None

            def emit_scores_qt(qt):
                nonlocal pend_ev
                at = qt % 2
                aw = [tok_attno_rd[at]]
                if (qt, 0) not in scst["lqm_tok"]:
                    emit_lqm(qt)
                tok_exp = {}
                last_mm1 = {}
                last_mm2 = {0: None, 1: None}

                def emit_mm2(kp):
                    for h in (0, 1):
                        pacc = pC if h == 0 else pD
                        pp = kp % 2
                        for sub in range(2):
                            kb = 2 * kp + sub
                            i = kb - 4 * qt
                            off = 128 * i if i > 0 else 0
                            last_mm2[h] = S.on(
                                "pe",
                                (lambda t, pacc=pacc, h=h, pp=pp, sub=sub,
                                 off=off, kb=kb, qt=qt:
                                 t.matmul(pacc[0:65, off:TILE],
                                          vaug[:, kb, 65 * h:65 * h + 65],
                                          exp_t[h][pp][:, sub * 512 + off:
                                                       sub * 512 + 512],
                                          start=(kb == 0),
                                          stop=(kb == 4 * qt + 3))),
                                wait=(tok_exp[(kp, h)] +
                                      ([vtok[qt]] + scst["pacc_rd"][h]
                                       if kb == 0 else []))
                                if sub == 0 else (),
                                inc="pe")
                        scst["exp_rd"][(h, pp)] = [last_mm2[h]]

                for kp in range(2 * qt + 2):
                    pp = kp % 2
                    for sub in range(2):
                        kb = 2 * kp + sub
                        i = kb - 4 * qt
                        off = 128 * i if i > 0 else 0
                        for h in (0, 1):
                            pS = pA if h == 0 else pB
                            psw = (tok_exp[(kp - 1, h)][0:1]
                                   if (kp - 1, h) in tok_exp else [])
                            last_mm1[h] = S.on(
                                "pe",
                                (lambda t, pS=pS, h=h, sub=sub, off=off, kb=kb,
                                 qt=qt:
                                 t.matmul(pS[:, sub * 512 + off:sub * 512 + 512],
                                          jlkt[:, kb * 128:(kb + 1) * 128],
                                          lqm[h][qt % 2][:, off:TILE],
                                          start=True, stop=True)),
                                wait=([scst["lqm_tok"][(qt, h)],
                                       flush_jl[qt // 2]] + psw +
                                      (scst["pacc_rd"][h] if kp == 0 else []))
                                if sub == 0 else (),
                                inc="pe")
                    if kp == 1:
                        flush_ev()
                        if qt + 1 < NSEG:
                            emit_lqm(qt + 1)
                    for h in (0, 1):
                        pS = pA if h == 0 else pB
                        t_exp = ACT(
                            lambda a, h=h, pp=pp, pS=pS:
                            a.activation(exp_t[h][pp][:, :], pS[:, :], AF.Exp,
                                         scale=SCALE),
                            wait=[last_mm1[h]] + scst["exp_rd"][(h, pp)])
                        trtoks = []
                        for sub in range(2):
                            kb = 2 * kp + sub
                            i = kb - 4 * qt
                            if 0 <= i < 4:
                                trtoks.append(DVE(
                                    lambda v, h=h, pp=pp, sub=sub, i=i:
                                    v.tensor_mul(
                                        exp_t[h][pp][:, sub * 512 + 128 * i:
                                                     sub * 512 + 128 * i + 128],
                                        exp_t[h][pp][:, sub * 512 + 128 * i:
                                                     sub * 512 + 128 * i + 128],
                                        tri_bf[:, :]),
                                    wait=[t_exp]))
                        tok_exp[(kp, h)] = [t_exp] + trtoks
                    if kp > 0:
                        emit_mm2(kp - 1)
                emit_mm2(2 * qt + 1)
                rcs = {}
                t64s = {}
                for h in (0, 1):
                    pacc = pC if h == 0 else pD
                    tok_sr = ACT(lambda a, pacc=pacc, h=h:
                                 a.copy(s_row[h][:, :], pacc[64:65, :]),
                                 wait=[last_mm2[h], scst["rc"][h]])
                    rcs[h] = DVE(lambda v, h=h:
                                 v.reciprocal(s_rec[h][:, :], s_row[h][:, :]),
                                 wait=[tok_sr])
                    scst["rc"][h] = rcs[h]
                    t64s[h] = ACT(lambda a, pacc=pacc, h=h:
                                  a.copy(tmp64[h][:, :], pacc[0:64, :]),
                                  wait=[last_mm2[h]] + scst["t64_rd"][h])
                    scst["pacc_rd"][h] = [t64s[h], tok_sr]
                    scst["lqm_rd"][(qt, h)] = [last_mm2[h]]
                pend_ev = (qt, at, rcs, t64s, aw)

            for seg in range(NSEG):
                par = seg % 2
                pr = (seg // 2) % 3
                ri = 4 * sq + seg // 4
                co = (seg % 4) * TILE
                seg_ld[par] = SPD(
                    lambda s, ri=ri, co=co, par=par: s.dma_start(
                        xnseg[par][:, :, :],
                        agx_out[1024 * ri:1024 * (ri + 1), co:co + TILE]
                        .rearrange("(k p) t -> p k t", p=128)),
                    wait=[tok_cc1, seg_rd[par], seg_dve_rd[par]])
                tok_xp = None
                for j in range(4):
                    kb = 4 * seg + j
                    jj = (seg % 2) * 4 + j
                    jc = slice(j * 128, (j + 1) * 128)
                    if j == 0:
                        stw = [seg_ld[par], seg_ld[1 - par], S.now("pe")]
                        for kc in range(8):
                            DVE(lambda v, kc=kc, par=par:
                                v.tensor_copy(xprev[:, kc, 1:128],
                                              xnseg[par][:, kc, 0:127]),
                                wait=stw if kc == 0 else ())
                            if seg == 0:
                                DVE(lambda v, kc=kc: v.memset(xprev[:, kc, 0:1],
                                                              0.0))
                            else:
                                DVE(lambda v, kc=kc, par=par:
                                    v.tensor_copy(xprev[:, kc, 0:1],
                                                  xnseg[1 - par][:, kc,
                                                                 TILE - 1:TILE]))
                        tok_xp = S.now("dve")
                        seg_dve_rd[1 - par] = tok_xp
                    items = [(pE_[:, 0:WVQK], xnseg[par][:, kc, jc],
                              wvqk_bf[:, kc, :]) for kc in range(8)]
                    for kc in range(8):
                        lhs = (xprev[:, kc, :] if j == 0 else
                               xnseg[par][:, kc, j * 128 - 1:(j + 1) * 128 - 1])
                        items.append((pE_[:, 128:144], lhs, wqb_bf[:, kc, :]))
                    items.append((pE_[:, 0:128], ones_bf[:, 0:128],
                                  brow_bf[:, 0:128]))
                    tok_pv = mmgroup(
                        items,
                        wait=[seg_ld[par], tok_wat, tok_vones,
                              tok_xp if j == 0 else None] + pE_rd["t"])
                    ev = []
                    for hh in range(HL):
                        ev.append(ACT(lambda a, hh=hh, kb=kb:
                                      a.copy(vaug[:, kb, 65 * hh:65 * hh + 64],
                                             pE_[:, 64 * hh:64 * hh + 64]),
                                      wait=[tok_pv]))
                    ev.append(ACT(lambda a, jj=jj, pr=pr:
                                  a.copy(qk12[pr][:, jj, :], pE_[:, 128:160]),
                                  wait=[tok_pv] + (qk_rd[pr] if jj == 0 else [])))
                    pE_rd["t"] = ev
                seg_rd[par] = S.now("pe")
                vtok[seg] = S.now("act")
                if seg % 2 == 1:
                    flush_tr()
                    tok_qk = S.now("act")
                    norm_toks = []
                    qk_rd[pr] = []
                    for side in ("q", "k"):
                        ENG = POOL if side == "q" else DVE
                        colbase = 0 if side == "q" else 16
                        Ldst = lq_f if side == "q" else lk_f
                        sqd = sq24 if side == "q" else sq24k
                        n2d = n2q if side == "q" else n2k
                        rnd = rn_q if side == "q" else rn_k
                        pa_ = prd if side == "q" else prdk
                        pb_ = prd2 if side == "q" else prd2k
                        tk = [tok_qk]
                        for p, (i, jx) in enumerate(_PAIRS):
                            t1_ = ENG(lambda g, i=i, jx=jx, pr=pr, cb_=colbase,
                                      pa_=pa_:
                                      g.tensor_mul(pa_[:, :],
                                                   fv(qk12[pr], [(32, 8), (4, 2)],
                                                      cb_ + i),
                                                   fv(qk12[pr], [(32, 8), (4, 2)],
                                                      cb_ + 8 + jx)),
                                      wait=tk if isinstance(tk, list) else [tk])
                            t2_ = ENG(lambda g, i=i, jx=jx, pr=pr, cb_=colbase,
                                      pb_=pb_:
                                      g.tensor_mul(pb_[:, :],
                                                   fv(qk12[pr], [(32, 8), (4, 2)],
                                                      cb_ + jx),
                                                   fv(qk12[pr], [(32, 8), (4, 2)],
                                                      cb_ + 8 + i)),
                                      wait=[t1_])
                            tk = ENG(lambda g, p=p, Ldst=Ldst, pa_=pa_, pb_=pb_:
                                     g.tensor_sub(fv(Ldst, [(CP, 8), (6, 2)], p),
                                                  pa_[:, :], pb_[:, :]),
                                     wait=[t2_])
                        qk_rd[pr].append(tk)
                        tk = ENG(lambda g, Ldst=Ldst, sqd=sqd:
                                 g.tensor_mul(sqd[:, :, :], Ldst[:, :, :],
                                              Ldst[:, :, :]), wait=[tk])
                        tk = DVE(lambda v, n2d=n2d, sqd=sqd:
                                 v.tensor_reduce(n2d[:, :],
                                                 fv(sqd, [(CP, 8), (6, 2), (1, 6)]),
                                                 AX.X, ALU.add), wait=[tk])
                        tok_n2 = DVE(lambda v, n2d=n2d:
                                     v.tensor_scalar_max(n2d[:, :], n2d[:, :],
                                                         1e-24), wait=[tk])
                        tok_lnr = ACT(lambda a, rnd=rnd, n2d=n2d:
                                      a.activation(rnd[:, :], n2d[:, :], AF.Ln),
                                      wait=[tok_n2])
                        tok_rn = ACT(lambda a, rnd=rnd:
                                     a.activation(rnd[:, :], rnd[:, :], AF.Exp,
                                                  scale=-0.5), wait=[tok_lnr])
                        if side == "q":
                            tkn = None
                            for p in range(6):
                                tkn = DVE(lambda v, p=p, pr=pr:
                                          v.scalar_tensor_tensor(
                                              fv(lq24[pr % 2], [(CP, 8), (6, 2)], p),
                                              fv(lq_f, [(CP, 8), (6, 2)], p), 1.0,
                                              fv(rn_q, [(2, 8), (1, 2)]),
                                              ALU.mult, ALU.mult),
                                          wait=[tok_rn] if p == 0 else [tkn])
                            norm_toks.append(tkn)
                        else:
                            tkn = None
                            for c_ in range(6):
                                tkn = DVE(lambda v, c_=c_, pr=pr:
                                          v.scalar_tensor_tensor(
                                              fv(jlk24[pr % 2], [(CP, 8), (6, 2)],
                                                 c_),
                                              fv(lk_f, [(CP, 8), (6, 2)], 5 - c_),
                                              _JSIGN[c_],
                                              fv(rn_k, [(2, 8), (1, 2)]),
                                              ALU.mult, ALU.mult),
                                          wait=[tok_rn] if c_ == 0 else [tkn])
                            norm_toks.append(tkn)
                    pend_tr = (pr % 2, seg // 2, norm_toks)
                if stage >= 4 and seg >= 4:
                    emit_scores_qt(seg - 4)
            flush_tr()
            if stage <= 3:
                return
            for qt in range(NSEG - 4, NSEG):
                emit_scores_qt(qt)
            flush_ev()
            tok_cc2[sq] = S.on("pool",
                               lambda g, sq=sq: g.collective_compute(
                                   "AllGather", ALU.bypass,
                                   replica_groups=[list(range(NCORE))],
                                   ins=[aga_in[128 * sq:128 * sq + 128, :].opt()],
                                   outs=[aga_out[1024 * sq:1024 * (sq + 1), :].opt()]),
                               wait=[S.now("xld")], inc="cc2")

        if stage <= 4:
            return
        attg_w = [tok_cc2[0], tok_cc2[1], S.now("pe"), S.now("dve"), S.now("act")]
        for dc in range(8):
            def rd(s, dc=dc):
                p = get_pid(s)
                return s.dma_start(
                    attg[:, dc, :],
                    aga_out[bass.ds(p["row"] + dc * 128, 128),
                            bass.ds(p["col"], NT)])
            SPD(rd, wait=attg_w)
        tok_attg = S.now("xld")
        pw_rd = {0: [], 1: []}
        for tt in range(NTILE):
            cols = slice(tt * TILE, (tt + 1) * TILE)
            for dc in range(8):
                dcp = dc % 2
                pw = pC if dcp == 0 else pD
                items = [(pw[:, :], wo_bf[:, kc, dc * 128:(dc + 1) * 128],
                          attg[:, kc, cols]) for kc in range(8)]
                tok_pw = mmgroup(items, wait=[tok_attg] + pw_rd[dcp])
                pw_rd[dcp] = [DVE(lambda v, dc=dc, cols=cols, pw=pw:
                                  v.scalar_tensor_tensor(X[:, dc, cols], pw[:, :],
                                                         bcol[:, 144 + dc:145 + dc],
                                                         X[:, dc, cols],
                                                         ALU.add, ALU.add),
                                  wait=[tok_pw])]

    if stage == 0:
        emit_ln()
    elif stage == 1:
        emit_gate(emit_ln())
    else:
        emit_ffn_pass()
        if stage >= 3:
            S.barrier()
            emit_attention()
            if stage >= 6:
                S.barrier()
                emit_ffn_pass()

    dbg_t = {}
    if dbg:
        S.barrier()
        for name in dbg:
            th = sb[name]
            shp = list(th.shape)
            nfree = int(np.prod(shp[1:]))
            dbg_t[name] = nc.dram_tensor("dbg_" + name, [shp[0], nfree], F32,
                                         kind="ExternalOutput")
            src = bass.AP(th, 0, [list(th[:].ap[0]), [1, nfree]])
            WLD(lambda g, name=name, src=src: g.dma_start(dbg_t[name][:, :], src))
        S.on("pool", None, wait=[S.now("wld")])

    S.barrier()
    for dc in range(8):
        STD(lambda s, dc=dc: s.dma_start(xout[dc * 128:(dc + 1) * 128, :],
                                         X[:, dc, :]))
    S.on("sp", None, wait=[S.now("st")])

    engmap = {"pe": "tensor", "act": "scalar", "dve": "vector",
              "pool": "gpsimd", "sp": "sync"}
    with nc.Block() as blk:
        for engname in Sch.ENGS:
            items = S.prog[engname]

            def body(eng, items=items):
                for fn, waits, inc in items:
                    for s_, v in waits:
                        eng.wait_ge(sems[s_], v)
                    if fn is not None:
                        ins = fn(eng)
                        if inc is not None:
                            ins.then_inc(sems[inc[0]], inc[1])
            getattr(blk, engmap[engname])(body)
    return nc


def _stage_inputs(inputs):
    f32 = np.float32
    x = np.asarray(inputs["L0_attn_x"], f32)
    xf = np.ascontiguousarray(x.reshape(B * T, D))
    gwT = np.ascontiguousarray(np.asarray(inputs["L0_ffn_gate_w"], f32).T)
    we1 = np.ascontiguousarray(np.asarray(inputs["L0_ffn_We1"], f32))
    we3 = np.ascontiguousarray(np.asarray(inputs["L0_ffn_We3"], f32))
    we2 = np.ascontiguousarray(np.asarray(inputs["L0_ffn_We2"], f32))
    ws1 = np.ascontiguousarray(np.asarray(inputs["L0_ffn_ws1"], f32))
    ws3 = np.ascontiguousarray(np.asarray(inputs["L0_ffn_ws3"], f32))
    ws2 = np.ascontiguousarray(np.asarray(inputs["L0_ffn_ws2"], f32))
    be1 = np.asarray(inputs["L0_ffn_be1"], f32)
    be3 = np.asarray(inputs["L0_ffn_be3"], f32)
    be2 = np.ascontiguousarray(np.asarray(inputs["L0_ffn_be2"], f32))
    bs1 = np.asarray(inputs["L0_ffn_bs1"], f32)
    bs3 = np.asarray(inputs["L0_ffn_bs3"], f32)
    bs2 = np.asarray(inputs["L0_ffn_bs2"], f32)
    gate_b = np.asarray(inputs["L0_ffn_gate_b"], f32)
    w1q = np.asarray(inputs["L0_attn_W1_q"], f32)
    w2q = np.asarray(inputs["L0_attn_W2_q"], f32)
    w1k = np.asarray(inputs["L0_attn_W1_k"], f32)
    w2k = np.asarray(inputs["L0_attn_W2_k"], f32)
    wv = np.asarray(inputs["L0_attn_W_v"], f32)
    bv = np.asarray(inputs["L0_attn_b_v"], f32)
    wo = np.ascontiguousarray(np.asarray(inputs["L0_attn_W_o"], f32))
    bo = np.asarray(inputs["L0_attn_b_o"], f32)

    bcol_h = np.zeros((128, NBCOL), f32)
    for e in range(E):
        for ic in range(4):
            bcol_h[:, 4 * e + ic] = be1[e, 128 * ic:128 * (ic + 1)]
            bcol_h[:, 64 + 4 * e + ic] = be3[e, 128 * ic:128 * (ic + 1)]
    for s_ in range(2):
        for ic in range(4):
            bcol_h[:, 128 + 4 * s_ + ic] = \
                bs1[512 * s_ + 128 * ic:512 * s_ + 128 * (ic + 1)]
            bcol_h[:, 136 + 4 * s_ + ic] = \
                bs3[512 * s_ + 128 * ic:512 * s_ + 128 * (ic + 1)]
    for dc in range(8):
        bcol_h[:, 144 + dc] = bo[128 * dc:128 * (dc + 1)]

    ident = np.eye(128, dtype=f32)
    trih = np.triu(np.ones((128, 128), f32))
    hmask_h = np.zeros((CP, HL), f32)
    for j in range(HL):
        hmask_h[6 * j:6 * j + 6, j] = 1.0
    ebc_h = np.zeros((E, E * 128), f32)
    for e in range(E):
        ebc_h[e, 128 * e:128 * (e + 1)] = 1.0
    growf = np.ascontiguousarray(np.tile(gate_b, 4).reshape(1, 4 * E))

    in_maps = []
    for c in range(NCORE):
        qs = slice(8 * c, 8 * c + 8)
        vs = slice(128 * c, 128 * (c + 1))
        brow_h = np.zeros((1, NBROW), f32)
        brow_h[0, 0:128] = bv[vs]
        brow_h[0, 256:1280] = bs2
        brow_h[0, 1280:2304] = bo
        wvqk = np.concatenate(
            [wv[:, vs], w1q[:D, qs], w2q[:D, qs], w1k[:, qs], w2k[:, qs]], axis=1)
        wqb = np.concatenate([w1q[D:, qs], w2q[D:, qs]], axis=1)
        in_maps.append({
            "xin": np.ascontiguousarray(xf[NT * c:NT * (c + 1)].T),
            "we1": we1, "we3": we3, "we2": we2,
            "ws1": ws1, "ws3": ws3, "ws2": ws2,
            "gwT": gwT, "be2m": be2, "bcol_h": bcol_h,
            "brow_h": brow_h, "growf": growf,
            "wvqk": np.ascontiguousarray(wvqk),
            "wqb": np.ascontiguousarray(wqb),
            "wo": wo,
            "ident": ident, "trih": trih, "hmask_h": hmask_h, "ebc_h": ebc_h,
        })
    return in_maps


_NC_CACHE = {}


def kernel(**inputs):
    if "nc" not in _NC_CACHE:
        _NC_CACHE["nc"] = build_nc()
    nc = _NC_CACHE["nc"]
    in_maps = _stage_inputs(inputs)
    res = run_bass_kernel_spmd(nc, in_maps, core_ids=list(range(NCORE)))
    out = np.empty((B * T, D), np.float32)
    for c in range(NCORE):
        out[NT * c:NT * (c + 1)] = res.results[c]["xout"].T
    return out.reshape(B, T, D)


# version 18
# speedup vs baseline: 1.0988x; 1.0116x over previous
import numpy as np
import ml_dtypes
from contextlib import ExitStack
import concourse.bass as bass
import concourse.mybir as mybir
from concourse.bass_utils import run_bass_kernel_spmd

F32 = mybir.dt.float32
BF16 = mybir.dt.bfloat16
AF = mybir.ActivationFunctionType
ALU = mybir.AluOpType
AX = mybir.AxisListType

NCORE = 8
B, T, D = 2, 8192, 1024
HL = 2
CP = HL * 6
NT = 2048
NBLK = NT // 128
TILE = 512
NTILE = NT // TILE
NKB = T // 128
NSEG = T // TILE
E = 16
EI = 512
NE = 18
SCALE = 6.0 ** -0.5
LN_EPS = 1e-5

NBCOL = 152
NBROW = 2304
WVQK = 160

_PAIRS = [(0, 1), (0, 2), (0, 3), (1, 2), (1, 3), (2, 3)]
_JSIGN = [1.0, -1.0, 1.0, 1.0, -1.0, 1.0]


class Sch:
    ENGS = ("pe", "act", "dve", "pool", "sp")

    def __init__(self):
        self.prog = {e: [] for e in self.ENGS}
        self.cnt = {}
        self.lastwait = {e: {} for e in self.ENGS}

    def on(self, eng, fn, wait=(), inc=None, incn=1):
        wl = []
        for tok in wait:
            if tok is None:
                continue
            s, v = tok
            if v > self.lastwait[eng].get(s, 0):
                wl.append((s, v))
                self.lastwait[eng][s] = v
        res = None
        if inc is not None:
            self.cnt[inc] = self.cnt.get(inc, 0) + incn
            res = (inc, self.cnt[inc])
        self.prog[eng].append((fn, tuple(wl), (inc, incn) if inc else None))
        return res

    def now(self, sem):
        return (sem, self.cnt.get(sem, 0))

    def barrier(self):
        snap = [(s, c) for s, c in self.cnt.items() if c > 0]
        for e in self.ENGS:
            self.on(e, None, wait=snap)


def fv(tsr, dims, off=0):
    base = tsr[:]
    return bass.AP(base.tensor, off, [list(base.ap[0])] +
                   [[s0, c0] for s0, c0 in dims])


def build_nc(stage=6, dbg=()):
    nc = bass.Bass("TRN2", target_bir_lowering=False, debug=False)
    dt = {}

    def din(name, shape):
        dt[name] = nc.dram_tensor(name, list(shape), F32, kind="ExternalInput")

    din("xin", (D, NT))
    din("we1", (E, D, EI))
    din("we3", (E, D, EI))
    din("we2", (E, EI, D))
    din("ws1", (D, D))
    din("ws3", (D, D))
    din("ws2", (D, D))
    din("gwT", (D, E))
    din("be2m", (E, D))
    din("bcol_h", (128, NBCOL))
    din("brow_h", (1, NBROW))
    din("growf", (1, 4 * E))
    din("wvqk", (D, WVQK))
    din("wqb", (D, 16))
    din("wo", (D, D))
    din("ident", (128, 128))
    din("trih", (128, 128))
    din("hmask_h", (CP, HL))
    din("ebc_h", (E, E * 128))

    xout = nc.dram_tensor("xout", [D, NT], F32, kind="ExternalOutput")
    HNT = NT // 2
    agx_in = [nc.dram_tensor(f"agx_in{i}", [D, HNT], BF16, kind="Internal")
              for i in (0, 1)]
    agx_out = [nc.dram_tensor(f"agx_out{i}", [NCORE * D, HNT], BF16,
                              kind="Internal") for i in (0, 1)]
    aga_in = nc.dram_tensor("aga_in", [B * 128, T], BF16, kind="Internal")
    aga_out = nc.dram_tensor("aga_out", [B * NCORE * 128, T], BF16, kind="Internal")

    sb = {}

    def sbuf(name, shape, dtype):
        sb[name] = nc.alloc_sbuf_tensor(name, list(shape), dtype)
        return sb[name]

    X = sbuf("X", (128, 8, NT), F32)
    id_bf = sbuf("id_bf", (128, 128), BF16)
    tri_bf = sbuf("tri_bf", (128, 128), BF16)
    ones_bf = sbuf("ones_bf", (1, TILE), BF16)
    onesr_f = sbuf("onesr_f", (1, 128), F32)
    onesc_f = sbuf("onesc_f", (128, 1), F32)
    bcol = sbuf("bcol", (128, NBCOL), F32)
    brow_bf = sbuf("brow_bf", (1, NBROW), BF16)
    be2_bf = sbuf("be2_bf", (E, D), BF16)
    gwT_bf = sbuf("gwT_bf", (128, 8, E), BF16)
    ebc_bf = sbuf("ebc_bf", (E, E * 128), BF16)
    gbb = sbuf("gbb", (128, 4, E), F32)
    hmask = sbuf("hmask", (CP, HL), F32)
    growf_s = sbuf("growf_s", (1, 4 * E), F32)
    s_row = [sbuf(f"s_row{h}", (1, TILE), F32) for h in (0, 1)]
    s_rec = [sbuf(f"s_rec{h}", (1, TILE), F32) for h in (0, 1)]
    r_tmp2 = s_row[0]
    g_eg = sbuf("g_eg", (128, 4, E), F32)
    g_or = sbuf("g_or", (128, 4, E), F32)
    g_bi = sbuf("g_bi", (128, 4, E), F32)
    g_s1 = sbuf("g_s1", (128, 4, 15), F32)
    g_s2 = sbuf("g_s2", (128, 4, 14), F32)
    g_s3 = sbuf("g_s3", (128, 4, 13), F32)
    g_G = sbuf("g_G", (128, 4, 4), F32)
    g_r2 = sbuf("g_r2", (128, 4, 4), F32)
    g_x = sbuf("g_x", (128, 4, 3), F32)
    g_cnt = sbuf("g_cnt", (128, 4, 4), F32)
    g_pen = sbuf("g_pen", (128, 4, 4), F32)
    g_sm = sbuf("g_sm", (128, 4, E), F32)
    g_mx8 = sbuf("g_mx8", (128, 8), F32)
    g_ss = sbuf("g_ss", (128, 4), F32)
    g_rs = sbuf("g_rs", (128, 4), F32)
    cb_bf = sbuf("cb_bf", (128, 4, E), BF16)
    lq_f = sbuf("lq_f", (128, 8, CP), F32)
    lk_f = sbuf("lk_f", (128, 8, CP), F32)
    sq24 = sbuf("sq24", (128, 8, CP), F32)
    n2q = sbuf("n2q", (128, 16), F32)
    n2k = sbuf("n2k", (128, 16), F32)
    rn_q = sbuf("rn_q", (128, 16), F32)
    rn_k = sbuf("rn_k", (128, 16), F32)
    prd = sbuf("prd", (128, 16), F32)
    prd2 = sbuf("prd2", (128, 16), F32)
    prdk = sbuf("prdk", (128, 16), F32)
    prd2k = sbuf("prd2k", (128, 16), F32)
    sq24k = sbuf("sq24k", (128, 8, CP), F32)
    lq24 = [sbuf(f"lq24_{i}", (128, 8, CP), BF16) for i in (0, 1)]
    jlk24 = [sbuf(f"jlk24_{i}", (128, 8, CP), BF16) for i in (0, 1)]
    xprev = sbuf("xprev", (128, 8, 128), BF16)

    OVL = 110 * 1024
    ovl_back = nc.alloc_sbuf_tensor("ovl_back", [128, OVL], mybir.dt.uint8)
    ovl_base = nc.lookup_mloc(ovl_back).addr

    def oalloc(name, shape, dtype, off):
        h = nc.virtual_sbuf_tensor(name, list(shape), dtype)
        nc._tensor(h.name, list(h.shape), h.dtype, type="SB")
        mloc = nc.lookup_mloc(h)
        mloc.allocated = True
        mloc.addr = ovl_base + off
        nbytes = int(np.prod(shape[1:])) * mybir.dt.size(dtype)
        assert off % 32 == 0 and off + nbytes <= OVL, (name, off, nbytes)
        sb[name] = h
        return h

    KB = 1024
    xn_bf = oalloc("xn_bf", (128, 8, NT), BF16, 0)
    w1b = [oalloc(f"w1b{i}", (128, 8, EI), BF16, 32 * KB + 8 * KB * i) for i in (0, 1)]
    w3b = [oalloc(f"w3b{i}", (128, 8, EI), BF16, 48 * KB + 8 * KB * i) for i in (0, 1)]
    w2b = [oalloc(f"w2b{i}", (128, 4, D), BF16, 64 * KB + 8 * KB * i) for i in (0, 1)]
    h_bf = [oalloc(f"h_bf{i}", (128, 4, TILE), BF16, 80 * KB + 4 * KB * i) for i in (0, 1)]
    sil_t = [oalloc(f"sil_t{i}", (128, TILE), F32, 88 * KB + 2 * KB * i) for i in (0, 1)]
    sq_t2 = [oalloc(f"sq_t{i}", (128, TILE), F32, 92 * KB + 2 * KB * i)
             for i in (0, 1)]
    t1_t = oalloc("t1_t", (128, TILE), F32, 106 * KB)
    cbT_bf = oalloc("cbT_bf", (E, NT), BF16, 96 * KB)
    jlkt = oalloc("jlkt", (CP, T), BF16, 0)
    lqt = oalloc("lqt", (CP, T), BF16, 16 * KB)
    vaug = oalloc("vaug", (128, NKB, HL * 65), BF16, 32 * KB)
    attg = oalloc("attg", (128, 8, NT), BF16, 0)
    xnseg = [oalloc(f"xnseg{i}", (128, 8, TILE), BF16, 66 * KB + 8 * KB * i)
             for i in (0, 1)]
    wo_bf = oalloc("wo_bf", (128, 8, D), BF16, 82 * KB)
    wvqk_bf = oalloc("wvqk_bf", (128, 8, WVQK), BF16, 98 * KB)
    wqb_bf = oalloc("wqb_bf", (128, 8, 16), BF16, 101 * KB)
    exp_t = [[oalloc(f"exp_t{h}_{i}", (128, 1024), BF16,
                     (102 * KB if h == 0 else 55552) + 2 * KB * i)
              for i in (0, 1)] for h in (0, 1)]
    lqm = [[oalloc(f"lqm{h}_{i}", (CP, TILE), BF16,
                   (106 + 2 * h + i) * KB) for i in (0, 1)] for h in (0, 1)]
    r_mean = oalloc("r_mean", (1, TILE), F32, 80 * KB)
    r_tmp = oalloc("r_tmp", (1, TILE), F32, 82 * KB)
    qk12 = [oalloc(f"qk12{i}", (128, 8, 32), F32, 59648 + 1 * KB * i)
            for i in (0, 1, 2)]
    tmp64 = [oalloc(f"tmp64{h}", (64, TILE), F32, 62720 + 2 * KB * h)
             for h in (0, 1)]
    attno = [oalloc(f"attno{i}", (128, TILE), BF16, 53504 + 1 * KB * i)
             for i in (0, 1)]

    pA = nc.alloc_psum_tensor("pA", [128, 1024], F32)
    pB = nc.alloc_psum_tensor("pB", [128, 1024], F32)
    pC = nc.alloc_psum_tensor("pC", [128, 512], F32)
    pD = nc.alloc_psum_tensor("pD", [128, 512], F32)
    pE_ = nc.alloc_psum_tensor("pE", [128, 512], F32)
    pbf = nc.alloc_psum_tensor("pbf", [128, 1024], BF16)

    semn = ["wld", "xld", "st", "pe", "act", "dve", "pool", "cc1", "cc2"]
    sems = {n: nc.alloc_semaphore(n) for n in semn}
    S = Sch()
    pidc = {}

    def get_pid(s):
        if "row" not in pidc:
            pid = s.partition_id()
            pidc["b"] = pid // 4
            pidc["row"] = (pid // 4) * 1024
            pidc["col"] = (pid - (pid // 4) * 4) * NT
        return pidc


    def PE(fn, wait=()):
        return S.on("pe", fn, wait=wait, inc="pe")

    def ACT(fn, wait=()):
        return S.on("act", fn, wait=wait, inc="act")

    def DVE(fn, wait=()):
        return S.on("dve", fn, wait=wait, inc="dve")

    def POOL(fn, wait=()):
        return S.on("pool", fn, wait=wait, inc="pool")

    def SPD(fn, wait=()):
        return S.on("sp", fn, wait=wait, inc="xld", incn=16)

    def WLD(fn, wait=()):
        return S.on("pool", fn, wait=wait, inc="wld", incn=16)

    def STD(fn, wait=()):
        return S.on("sp", fn, wait=wait, inc="st", incn=16)

    def mmgroup(items, wait=()):
        n = len(items)
        tok = None
        for i, (o, l, r) in enumerate(items):
            tok = S.on(
                "pe",
                (lambda t, o=o, l=l, r=r, st_=(i == 0), sp_=(i == n - 1):
                 t.matmul(o, l, r, start=st_, stop=sp_)),
                wait=wait if i == 0 else (),
                inc="pe" if i == n - 1 else None)
        return tok

    def _first(s):
        get_pid(s)
        return s.dma_start(bcol[:, :], dt["bcol_h"][:, :])
    SPD(_first)
    SPD(lambda s: s.dma_start(hmask[:, :], dt["hmask_h"][:, :]))
    SPD(lambda s: s.dma_start(growf_s[:, :], dt["growf"][:, :]))
    for dc in range(8):
        SPD(lambda s, dc=dc: s.dma_start(X[:, dc, :],
                                         dt["xin"][dc * 128:(dc + 1) * 128, :]))
    tok_xload = S.now("xld")

    WLD(lambda g: g.dma_start(id_bf[:, :], dt["ident"][:, :]))
    WLD(lambda g: g.dma_start(tri_bf[:, :], dt["trih"][:, :]))
    WLD(lambda g: g.dma_start(brow_bf[:, :], dt["brow_h"][:, :]))
    WLD(lambda g: g.dma_start(be2_bf[:, :], dt["be2m"][:, :]))
    WLD(lambda g: g.dma_start(ebc_bf[:, :], dt["ebc_h"][:, :]))
    WLD(lambda g: g.dma_start(gwT_bf[:, :, :],
                              dt["gwT"][:, :].rearrange("(k p) e -> p k e", p=128)))
    tok_const = S.now("wld")

    DVE(lambda v: v.memset(ones_bf[:, :], 1.0))
    DVE(lambda v: v.memset(onesr_f[:, :], 1.0))
    tok_ones = DVE(lambda v: v.memset(onesc_f[:, :], 1.0))

    PE(lambda t: t.matmul(pC[:, 0:64], onesr_f[:, :], growf_s[:, :],
                          start=True, stop=True),
       wait=[tok_xload, tok_ones])
    tok_gbb = ACT(lambda a: a.copy(gbb[:, :, :], pC[:, 0:64]), wait=[S.now("pe")])

    def emit_ln():
        prev = [S.now("act"), S.now("dve"), S.now("pe")]
        prev_bm = prev_br = prev_xn = None
        xn_toks = []
        for tt in range(NTILE):
            cols = slice(tt * TILE, (tt + 1) * TILE)
            tok_sum = mmgroup(
                [(pD[0:1, :], onesc_f[:, :], X[:, dc, cols]) for dc in range(8)],
                wait=prev)
            tok_mm = None
            tok_mm2q = None
            for dc in range(8):
                sp_ = sq_t2[dc % 2]
                tok_sq = DVE(lambda v, dc=dc, cols=cols, sp_=sp_:
                             v.tensor_mul(sp_[:, :], X[:, dc, cols], X[:, dc, cols]),
                             wait=([tok_mm2q] if dc >= 2 else prev))
                tok_mm = S.on("pe",
                              (lambda t, st_=(dc == 0), sp2_=(dc == 7), sp_=sp_:
                               t.matmul(pE_[0:1, :], onesc_f[:, :], sp_[:, :],
                                        start=st_, stop=sp2_)),
                              wait=[tok_sq], inc="pe")
                if dc % 2 == 1:
                    tok_mm2q = tok_mm
            tok_mean = ACT(lambda a: a.activation(r_mean[:, :], pD[0:1, :], AF.Copy,
                                                  scale=1.0 / D),
                           wait=[tok_sum, prev_bm])
            tok_msq = ACT(lambda a: a.activation(r_tmp[:, :], pE_[0:1, :], AF.Copy,
                                                 scale=1.0 / D),
                          wait=[tok_mm, prev_br])
            tok_m2 = DVE(lambda v: v.tensor_mul(r_tmp2[:, :], r_mean[:, :],
                                                r_mean[:, :]), wait=[tok_mean])
            tok_var = DVE(lambda v: v.scalar_tensor_tensor(
                r_tmp[:, :], r_tmp[:, :], LN_EPS, r_tmp2[:, :],
                ALU.add, ALU.subtract), wait=[tok_msq, tok_m2])
            tok_lnv = ACT(lambda a: a.activation(r_tmp[:, :], r_tmp[:, :], AF.Ln),
                          wait=[tok_var])
            tok_rstd = ACT(lambda a: a.activation(r_tmp[:, :], r_tmp[:, :], AF.Exp,
                                                  scale=-0.5), wait=[tok_lnv])
            tok_nm = DVE(lambda v: v.tensor_scalar_mul(r_mean[:, :], r_mean[:, :],
                                                       -1.0), wait=[tok_var])
            prev_bm = PE(lambda t: t.matmul(pA[:, 0:512], onesr_f[:, :],
                                            r_mean[:, :], start=True, stop=True),
                         wait=[tok_nm, prev_xn])
            prev_br = PE(lambda t: t.matmul(pA[:, 512:1024], onesr_f[:, :],
                                            r_tmp[:, :], start=True, stop=True),
                         wait=[tok_rstd])
            for dc in range(8):
                DVE(lambda v, dc=dc, cols=cols:
                    v.tensor_add(t1_t[:, :], X[:, dc, cols], pA[:, 0:512]),
                    wait=[prev_bm, prev_br] if dc == 0 else ())
                prev_xn = DVE(lambda v, dc=dc, cols=cols:
                              v.tensor_mul(xn_bf[:, dc, cols], t1_t[:, :],
                                           pA[:, 512:1024]))
            prev = [prev_xn, prev_bm, prev_br, tok_rstd]
            xn_toks.append(prev_xn)
        return xn_toks

    def emit_gate(tok_xn):
        tok_tr_rd = None
        tok_pg_rd = None
        cbT_toks = []
        for tt in range(NTILE):
            toks_pg = []
            for b4 in range(4):
                bc = slice(tt * TILE + b4 * 128, tt * TILE + b4 * 128 + 128)
                toks_pg.append(mmgroup(
                    [(pC[:, 16 * b4:16 * b4 + 16], xn_bf[:, kc, bc],
                      gwT_bf[:, kc, :]) for kc in range(8)],
                    wait=[tok_xn[tt], tok_const, tok_pg_rd] if b4 == 0 else ()))
            tok_pg = toks_pg[-1]
            tok_exp = ACT(lambda a: a.activation(g_eg[:, :, :], pC[:, 0:64],
                                                 AF.Exp), wait=[tok_pg])
            tok_pg_rd = tok_exp
            tk = DVE(lambda v: v.tensor_reduce(g_ss[:, :],
                                               fv(g_eg, [(E, 4), (1, E)]),
                                               AX.X, ALU.add), wait=[tok_exp])
            tk = DVE(lambda v: v.reciprocal(g_rs[:, :], g_ss[:, :]), wait=[tk])
            for b4 in range(4):
                tk = DVE(lambda v, b4=b4: v.tensor_scalar_mul(
                    g_or[:, b4, :], g_eg[:, b4, :], g_rs[:, b4:b4 + 1]),
                    wait=[tk])
            tk = DVE(lambda v: v.tensor_tensor(g_bi[:, :, :], g_or[:, :, :],
                                               gbb[:, :, :], ALU.add),
                     wait=[tk, tok_gbb])
            tk = DVE(lambda v: v.tensor_add(fv(g_s1, [(15, 4), (1, 15)]),
                                            fv(g_bi, [(E, 4), (1, 15)]),
                                            fv(g_bi, [(E, 4), (1, 15)], 1)),
                     wait=[tk])
            tk = DVE(lambda v: v.tensor_add(fv(g_s2, [(14, 4), (1, 14)]),
                                            fv(g_bi, [(E, 4), (1, 14)]),
                                            fv(g_bi, [(E, 4), (1, 14)], 2)),
                     wait=[tk])
            tk = DVE(lambda v: v.tensor_add(fv(g_s3, [(13, 4), (1, 13)]),
                                            fv(g_bi, [(E, 4), (1, 13)]),
                                            fv(g_bi, [(E, 4), (1, 13)], 3)),
                     wait=[tk])
            tk = DVE(lambda v: v.tensor_reduce(
                g_G[:, :, :], fv(g_s1, [(15, 4), (4, 4), (1, 3)]),
                AX.X, ALU.max), wait=[tk])
            tk = DVE(lambda v: v.tensor_reduce(
                g_r2[:, :, :], fv(g_s2, [(14, 4), (4, 4), (1, 2)]),
                AX.X, ALU.max), wait=[tk])
            tk = DVE(lambda v: v.tensor_max(g_G[:, :, :], g_G[:, :, :],
                                            g_r2[:, :, :]), wait=[tk])
            tk = DVE(lambda v: v.tensor_max(g_G[:, :, :], g_G[:, :, :],
                                            fv(g_s3, [(13, 4), (4, 4)])),
                     wait=[tk])
            tk = DVE(lambda v: v.memset(g_cnt[:, :, :], 0.0), wait=[tk])
            for s_ in (1, 2, 3):
                tk = DVE(lambda v, s_=s_: v.tensor_tensor(
                    fv(g_x, [(3, 4), (1, 4 - s_)]),
                    fv(g_G, [(4, 4), (1, 4 - s_)], s_),
                    fv(g_G, [(4, 4), (1, 4 - s_)]), ALU.is_gt), wait=[tk])
                tk = DVE(lambda v, s_=s_: v.tensor_add(
                    fv(g_cnt, [(4, 4), (1, 4 - s_)]),
                    fv(g_cnt, [(4, 4), (1, 4 - s_)]),
                    fv(g_x, [(3, 4), (1, 4 - s_)])), wait=[tk])
                tk = DVE(lambda v, s_=s_: v.tensor_scalar(
                    fv(g_x, [(3, 4), (1, 4 - s_)]),
                    fv(g_x, [(3, 4), (1, 4 - s_)]),
                    -1.0, 1.0, ALU.mult, ALU.add), wait=[tk])
                tk = DVE(lambda v, s_=s_: v.tensor_add(
                    fv(g_cnt, [(4, 4), (1, 4 - s_)], s_),
                    fv(g_cnt, [(4, 4), (1, 4 - s_)], s_),
                    fv(g_x, [(3, 4), (1, 4 - s_)])), wait=[tk])
            tk = DVE(lambda v: v.tensor_scalar(g_pen[:, :, :], g_cnt[:, :, :],
                                               1.5, -1e30, ALU.is_ge, ALU.mult),
                     wait=[tk])
            for j in range(4):
                tk = DVE(lambda v, j=j: v.tensor_add(
                    fv(g_sm, [(E, 4), (4, 4)], j),
                    fv(g_bi, [(E, 4), (4, 4)], j),
                    g_pen[:, :, :]), wait=[tk])
            for b4 in range(4):
                tk = DVE(lambda v, b4=b4: v.max(g_mx8[:, :], g_sm[:, b4, :]),
                         wait=[tk])
                tk = DVE(lambda v, b4=b4: v.tensor_single_scalar(
                    g_sm[:, b4, :], g_sm[:, b4, :], g_mx8[:, 1:2], ALU.is_ge),
                    wait=[tk])
            tok_cb = DVE(lambda v: v.tensor_mul(cb_bf[:, :, :], g_sm[:, :, :],
                                                g_or[:, :, :]), wait=[tk])
            for b4 in range(4):
                bc = slice(tt * TILE + b4 * 128, tt * TILE + b4 * 128 + 128)
                tok_tr = PE(lambda t, b4=b4: t.transpose(
                    pbf[0:E, 128 * b4:128 * b4 + 128], cb_bf[:, b4, :],
                    id_bf[:, :]),
                    wait=[tok_cb, tok_tr_rd, tok_const] if b4 == 0 else ())
            tok_trl = S.now("pe")
            tok_cbT = DVE(lambda v, tt=tt: v.tensor_copy(
                cbT_bf[:, tt * TILE:(tt + 1) * TILE], pbf[0:E, 0:512]),
                wait=[tok_trl])
            tok_tr_rd = tok_cbT
            cbT_toks.append(tok_cbT)
        return cbT_toks

    wstate = {"last_use": [None, None]}

    def emit_wload(e):
        par = e % 2
        wa = wstate["last_use"][par]
        wa = wa if isinstance(wa, list) else [wa]
        if e < E:
            WLD(lambda g, e=e, par=par: g.dma_start(
                w1b[par][:, :, :],
                dt["we1"][e, :, :].rearrange("(k p) i -> p k i", p=128)), wait=wa)
            WLD(lambda g, e=e, par=par: g.dma_start(
                w3b[par][:, :, :],
                dt["we3"][e, :, :].rearrange("(k p) i -> p k i", p=128)))
            WLD(lambda g, e=e, par=par: g.dma_start(
                w2b[par][:, :, :],
                dt["we2"][e, :, :].rearrange("(k p) d -> p k d", p=128)))
        else:
            s_ = e - E
            WLD(lambda g, s_=s_, par=par: g.dma_start(
                w1b[par][:, :, :],
                dt["ws1"][:, s_ * EI:(s_ + 1) * EI]
                .rearrange("(k p) i -> p k i", p=128)), wait=wa)
            WLD(lambda g, s_=s_, par=par: g.dma_start(
                w3b[par][:, :, :],
                dt["ws3"][:, s_ * EI:(s_ + 1) * EI]
                .rearrange("(k p) i -> p k i", p=128)))
            WLD(lambda g, s_=s_, par=par: g.dma_start(
                w2b[par][:, 0:4, :],
                dt["ws2"][s_ * EI:(s_ + 1) * EI, :]
                .rearrange("(k p) d -> p k d", p=128)))
        return S.now("wld")

    def emit_experts(tok_cbT):
        fence = [S.now("act"), S.now("dve")]
        est = {"sil_rd": [fence, fence], "p13_rd": {0: fence, 1: fence},
               "h_rd": [fence, fence], "p2_rd": {0: fence, 1: fence},
               "be_rd": fence}
        if wstate.get("pre"):
            wtok = wstate.pop("pre")
        else:
            wtok = [emit_wload(0), emit_wload(1)]
        pend = None

        def emit_p2(par, h_par, tok_h, cols, e):
            tok_p2l = None
            for dc in range(8):
                dcp = dc % 2
                pP2 = pD if dcp == 0 else pE_
                items = [(pP2[:, :], w2b[par][:, kc, dc * 128:(dc + 1) * 128],
                          h_bf[h_par][:, kc, :]) for kc in range(4)]
                if e == 0:
                    items.append((pP2[:, :], be2_bf[:, dc * 128:(dc + 1) * 128],
                                  cbT_bf[:, cols]))
                if e == E + 1:
                    items.append((pP2[:, :],
                                  brow_bf[:, 256 + dc * 128:256 + dc * 128 + 128],
                                  ones_bf[:, :]))
                tok_p2 = mmgroup(items, wait=[tok_h] + est["p2_rd"][dcp])
                tok_ev = DVE(lambda v, dc=dc, cols=cols, pP2=pP2:
                             v.scalar_tensor_tensor(X[:, dc, cols], pP2[:, :],
                                                    0.5, X[:, dc, cols],
                                                    ALU.mult, ALU.add),
                             wait=[tok_p2])
                est["p2_rd"][dcp] = [tok_ev]
                tok_p2l = tok_p2
            est["h_rd"][h_par] = [tok_p2l]

        for e in range(NE):
            par = e % 2
            mywtok = wtok[par]
            for tt in range(NTILE):
                cols = slice(tt * TILE, (tt + 1) * TILE)
                if e < E:
                    tok_be = PE(lambda t, e=e, cols=cols:
                                t.matmul(pC[:, :], ebc_bf[:, e * 128:(e + 1) * 128],
                                         cbT_bf[:, cols], start=True, stop=True),
                                wait=[tok_cbT[tt]] + est["be_rd"])
                h_par = (e * NTILE + tt) % 2
                tok_h = None
                for ic in range(4):
                    icp = ic % 2
                    pP1 = pA[:, icp * 512:(icp + 1) * 512]
                    pP3 = pB[:, icp * 512:(icp + 1) * 512]
                    mmgroup([(pP1, w1b[par][:, kc, ic * 128:(ic + 1) * 128],
                              xn_bf[:, kc, cols]) for kc in range(8)],
                            wait=[mywtok] + est["p13_rd"][icp])
                    tok_p13 = mmgroup([(pP3, w3b[par][:, kc, ic * 128:(ic + 1) * 128],
                                        xn_bf[:, kc, cols]) for kc in range(8)])
                    bix = (4 * e + ic) if e < E else (128 + 4 * (e - E) + ic)
                    b3x = (64 + 4 * e + ic) if e < E else (136 + 4 * (e - E) + ic)
                    tok_sil = ACT(lambda a, bix=bix, icp=icp, pP1=pP1:
                                  a.activation(sil_t[icp][:, :], pP1, AF.Silu,
                                               bias=bcol[:, bix:bix + 1]),
                                  wait=[tok_p13] + est["sil_rd"][icp])
                    if e < E:
                        DVE(lambda v, b3x=b3x, icp=icp, pP3=pP3:
                            v.scalar_tensor_tensor(t1_t[:, :], pP3,
                                                   bcol[:, b3x:b3x + 1],
                                                   sil_t[icp][:, :],
                                                   ALU.add, ALU.mult),
                            wait=[tok_sil] + est["h_rd"][h_par])
                        tok_h = DVE(lambda v, ic=ic, h_par=h_par:
                                    v.tensor_mul(h_bf[h_par][:, ic, :], t1_t[:, :],
                                                 pC[:, :]),
                                    wait=[tok_be])
                    else:
                        tok_h = DVE(lambda v, b3x=b3x, icp=icp, ic=ic, h_par=h_par,
                                    pP3=pP3:
                                    v.scalar_tensor_tensor(h_bf[h_par][:, ic, :],
                                                           pP3,
                                                           bcol[:, b3x:b3x + 1],
                                                           sil_t[icp][:, :],
                                                           ALU.add, ALU.mult),
                                    wait=[tok_sil] + est["h_rd"][h_par])
                    est["sil_rd"][icp] = [tok_h]
                    est["p13_rd"][icp] = [tok_sil, tok_h]
                if e < E:
                    est["be_rd"] = [tok_h]
                if pend is not None:
                    emit_p2(*pend)
                    if pend[4] != e and pend[4] + 2 < NE:
                        wstate["last_use"][pend[0]] = S.now("pe")
                        wtok[pend[0]] = emit_wload(pend[4] + 2)
                pend = (par, h_par, tok_h, cols, e)
        emit_p2(*pend)

    def emit_ffn_pass():
        tok_xn = emit_ln()
        tok_cbT = emit_gate(tok_xn)
        emit_experts(tok_cbT)

    def emit_attention():
        tok_xn = emit_ln()
        tok_cc1 = [None, None]
        for hi in (0, 1):
            for dc in range(8):
                SPD(lambda s, dc=dc, hi=hi: s.dma_start(
                    agx_in[hi][dc * 128:(dc + 1) * 128, :],
                    xn_bf[:, dc, hi * HNT:(hi + 1) * HNT]),
                    wait=[tok_xn[-1]] if (hi, dc) == (0, 0) else ())
            tok_cc1[hi] = S.on("pool",
                               lambda g, hi=hi: g.collective_compute(
                                   "AllGather", ALU.bypass,
                                   replica_groups=[list(range(NCORE))],
                                   ins=[agx_in[hi].ap().opt()],
                                   outs=[agx_out[hi].ap().opt()]),
                               wait=[S.now("xld")], inc="cc1")
        wa = [S.now("pe"), S.now("dve"), S.now("act")]
        WLD(lambda g: g.dma_start(
            wvqk_bf[:, :, :], dt["wvqk"][:, :].rearrange("(k p) v -> p k v", p=128)),
            wait=wa)
        WLD(lambda g: g.dma_start(
            wqb_bf[:, :, :], dt["wqb"][:, :].rearrange("(k p) v -> p k v", p=128)))
        WLD(lambda g: g.dma_start(
            wo_bf[:, :, :], dt["wo"][:, :].rearrange("(k p) v -> p k v", p=128)))
        tok_wat = S.now("wld")

        tok_attno_rd = [None, None]
        tok_cc2 = {}
        for sq in range(B):
            tok_vones = DVE(
                lambda v: v.memset(
                    vaug[:, :, :].rearrange("p k (h c) -> p k h c", c=65)[:, :, :, 64],
                    1.0),
                wait=wa + [S.now("pe")])
            seg_ld = [None, None]
            seg_rd = [None, None]
            seg_dve_rd = [None, None]
            pvk_rd = {0: [S.now("act"), S.now("dve")],
                      1: [S.now("act"), S.now("dve")]}
            qk_rd = [[], [], []]
            tr_rd = None
            pend_tr = None

            def flush_tr():
                nonlocal pend_tr, tr_rd
                if pend_tr is None:
                    return
                pr_, pi_, tok_chain = pend_tr
                for jj in range(8):
                    PE(lambda t, jj=jj, pr_=pr_: t.transpose(
                        pbf[0:CP, 128 * jj:128 * jj + 128],
                        lq24[pr_][:, jj, :], id_bf[:, :]),
                       wait=(tok_chain + [tok_const, tr_rd]) if jj == 0 else ())
                tok_q8 = S.now("pe")
                c1 = DVE(lambda v, pi_=pi_: v.tensor_copy(
                    lqt[:, pi_ * 1024:(pi_ + 1) * 1024], pbf[0:CP, :]),
                    wait=[tok_q8])
                for jj in range(8):
                    PE(lambda t, jj=jj, pr_=pr_: t.transpose(
                        pbf[0:CP, 128 * jj:128 * jj + 128],
                        jlk24[pr_][:, jj, :], id_bf[:, :]),
                       wait=[c1] if jj == 0 else ())
                tok_k8 = S.now("pe")
                tr_rd = DVE(lambda v, pi_=pi_: v.tensor_copy(
                    jlkt[:, pi_ * 1024:(pi_ + 1) * 1024], pbf[0:CP, :]),
                    wait=[tok_k8])
                pend_tr = None

            for seg in range(NSEG):
                par = seg % 2
                pr = (seg // 2) % 2
                ri = 4 * sq + seg // 4
                co = (seg % 4) * TILE
                hi = co // HNT
                hco = co % HNT
                for ks in range(4):
                    seg_ld[par] = SPD(
                        lambda s, ri=ri, hco=hco, par=par, hi=hi, ks=ks:
                        s.dma_start(
                            xnseg[par][:, 2 * ks:2 * ks + 2, :],
                            agx_out[hi][1024 * ri + 256 * ks:
                                        1024 * ri + 256 * (ks + 1),
                                        hco:hco + TILE]
                            .rearrange("(k p) t -> p k t", p=128)),
                        wait=[tok_cc1[hi], seg_rd[par], seg_dve_rd[par]]
                        if ks == 0 else ())
                tok_xp = None
                for j in range(4):
                    kb = 4 * seg + j
                    jj = (seg % 2) * 4 + j
                    jc = slice(j * 128, (j + 1) * 128)
                    pv = pC if kb % 2 == 0 else pD
                    if j == 0:
                        stw = [seg_ld[par], seg_ld[1 - par], S.now("pe")]
                        for kc in range(8):
                            DVE(lambda v, kc=kc, par=par:
                                v.tensor_copy(xprev[:, kc, 1:128],
                                              xnseg[par][:, kc, 0:127]),
                                wait=stw if kc == 0 else ())
                            if seg == 0:
                                DVE(lambda v, kc=kc: v.memset(xprev[:, kc, 0:1],
                                                              0.0))
                            else:
                                DVE(lambda v, kc=kc, par=par:
                                    v.tensor_copy(xprev[:, kc, 0:1],
                                                  xnseg[1 - par][:, kc,
                                                                 TILE - 1:TILE]))
                        tok_xp = S.now("dve")
                        seg_dve_rd[1 - par] = tok_xp
                    items = [(pv[:, 0:WVQK], xnseg[par][:, kc, jc],
                              wvqk_bf[:, kc, :]) for kc in range(8)]
                    for kc in range(8):
                        lhs = (xprev[:, kc, :] if j == 0 else
                               xnseg[par][:, kc, j * 128 - 1:(j + 1) * 128 - 1])
                        items.append((pv[:, 128:144], lhs, wqb_bf[:, kc, :]))
                    items.append((pv[:, 0:128], ones_bf[:, 0:128],
                                  brow_bf[:, 0:128]))
                    tok_pv = mmgroup(
                        items,
                        wait=[seg_ld[par], tok_wat, tok_vones,
                              tok_xp if j == 0 else None] + pvk_rd[kb % 2])
                    ev = []
                    for hh in range(HL):
                        ev.append(ACT(lambda a, hh=hh, kb=kb, pv=pv:
                                      a.copy(vaug[:, kb, 65 * hh:65 * hh + 64],
                                             pv[:, 64 * hh:64 * hh + 64]),
                                      wait=[tok_pv]))
                    ev.append(ACT(lambda a, jj=jj, pr=pr, pv=pv:
                                  a.copy(qk12[pr][:, jj, :], pv[:, 128:160]),
                                  wait=[tok_pv] + (qk_rd[pr] if jj == 0 else [])))
                    pvk_rd[kb % 2] = ev
                seg_rd[par] = S.now("pe")
                if seg % 2 == 0:
                    continue
                flush_tr()
                tok_qk = S.now("act")
                norm_toks = []
                qk_rd[pr] = []
                for side in ("q", "k"):
                    ENG = POOL if side == "q" else DVE
                    colbase = 0 if side == "q" else 16
                    Ldst = lq_f if side == "q" else lk_f
                    sqd = sq24 if side == "q" else sq24k
                    n2d = n2q if side == "q" else n2k
                    rnd = rn_q if side == "q" else rn_k
                    pa_ = prd if side == "q" else prdk
                    pb_ = prd2 if side == "q" else prd2k
                    tk = [tok_qk]
                    for p, (i, jx) in enumerate(_PAIRS):
                        t1_ = ENG(lambda g, i=i, jx=jx, pr=pr, cb_=colbase, pa_=pa_:
                                  g.tensor_mul(pa_[:, :],
                                               fv(qk12[pr], [(32, 8), (4, 2)],
                                                  cb_ + i),
                                               fv(qk12[pr], [(32, 8), (4, 2)],
                                                  cb_ + 8 + jx)),
                                  wait=tk if isinstance(tk, list) else [tk])
                        t2_ = ENG(lambda g, i=i, jx=jx, pr=pr, cb_=colbase, pb_=pb_:
                                  g.tensor_mul(pb_[:, :],
                                               fv(qk12[pr], [(32, 8), (4, 2)],
                                                  cb_ + jx),
                                               fv(qk12[pr], [(32, 8), (4, 2)],
                                                  cb_ + 8 + i)),
                                  wait=[t1_])
                        tk = ENG(lambda g, p=p, Ldst=Ldst, pa_=pa_, pb_=pb_:
                                 g.tensor_sub(fv(Ldst, [(CP, 8), (6, 2)], p),
                                              pa_[:, :], pb_[:, :]),
                                 wait=[t2_])
                    qk_rd[pr].append(tk)
                    tk = ENG(lambda g, Ldst=Ldst, sqd=sqd:
                             g.tensor_mul(sqd[:, :, :], Ldst[:, :, :],
                                          Ldst[:, :, :]), wait=[tk])
                    tk = DVE(lambda v, n2d=n2d, sqd=sqd:
                             v.tensor_reduce(n2d[:, :],
                                             fv(sqd, [(CP, 8), (6, 2), (1, 6)]),
                                             AX.X, ALU.add), wait=[tk])
                    tok_n2 = DVE(lambda v, n2d=n2d:
                                 v.tensor_scalar_max(n2d[:, :], n2d[:, :], 1e-24),
                                 wait=[tk])
                    tok_lnr = ACT(lambda a, rnd=rnd, n2d=n2d:
                                  a.activation(rnd[:, :], n2d[:, :], AF.Ln),
                                  wait=[tok_n2])
                    tok_rn = ACT(lambda a, rnd=rnd:
                                 a.activation(rnd[:, :], rnd[:, :], AF.Exp,
                                              scale=-0.5), wait=[tok_lnr])
                    if side == "q":
                        tkn = None
                        for p in range(6):
                            tkn = DVE(lambda v, p=p, pr=pr: v.scalar_tensor_tensor(
                                fv(lq24[pr % 2], [(CP, 8), (6, 2)], p),
                                fv(lq_f, [(CP, 8), (6, 2)], p), 1.0,
                                fv(rn_q, [(2, 8), (1, 2)]), ALU.mult, ALU.mult),
                                wait=[tok_rn] if p == 0 else [tkn])
                        norm_toks.append(tkn)
                    else:
                        tkn = None
                        for c_ in range(6):
                            tkn = DVE(lambda v, c_=c_, pr=pr: v.scalar_tensor_tensor(
                                fv(jlk24[pr % 2], [(CP, 8), (6, 2)], c_),
                                fv(lk_f, [(CP, 8), (6, 2)], 5 - c_), _JSIGN[c_],
                                fv(rn_k, [(2, 8), (1, 2)]), ALU.mult, ALU.mult),
                                wait=[tok_rn] if c_ == 0 else [tkn])
                        norm_toks.append(tkn)
                tk = norm_toks
                pend_tr = (pr % 2, seg // 2, tk)
            flush_tr()
            tok_proj = [S.now("pe"), S.now("dve"), S.now("act"), S.now("pool")]

            if stage <= 3:
                return

            scst = {"exp_rd": {(h, i): [] for h in (0, 1) for i in (0, 1)},
                    "pacc_rd": {0: [], 1: []}, "pbc_rd": [],
                    "t64_rd": {0: [], 1: []}, "rc": {0: None, 1: None},
                    "lqm_tok": {}, "lqm_rd": {}}
            pend_ev = None

            def emit_lqm(qt):
                for h in (0, 1):
                    buf = lqm[h][qt % 2]
                    scst["lqm_tok"][(qt, h)] = DVE(
                        lambda v, qt=qt, h=h, buf=buf: v.tensor_scalar_mul(
                            buf[:, :], lqt[:, qt * TILE:(qt + 1) * TILE],
                            hmask[:, h:h + 1]),
                        wait=tok_proj + scst["lqm_rd"].get((qt - 2, h), []))

            def flush_ev():
                nonlocal pend_ev
                if pend_ev is None:
                    return
                qt_, at_, rcs, t64s, aw_ = pend_ev
                bc0 = PE(lambda t: t.matmul(pE_[0:64, :], onesr_f[:, 0:64],
                                            s_rec[0][:, :], start=True, stop=True),
                         wait=[rcs[0]] + scst["pbc_rd"])
                bc1 = PE(lambda t: t.matmul(pE_[64:128, :], onesr_f[:, 0:64],
                                            s_rec[1][:, :], start=True, stop=True),
                         wait=[rcs[1]])
                at0 = DVE(lambda v, at_=at_:
                          v.tensor_mul(attno[at_][0:64, :], tmp64[0][:, :],
                                       pE_[0:64, :]),
                          wait=[bc1, t64s[0]] + scst["t64_rd"][0] + aw_)
                at1 = DVE(lambda v, at_=at_:
                          v.tensor_mul(attno[at_][64:128, :], tmp64[1][:, :],
                                       pE_[64:128, :]),
                          wait=[t64s[1]] + scst["t64_rd"][1])
                SPD(lambda s, sq=sq, qt_=qt_, at_=at_: s.dma_start(
                    aga_in[128 * sq:128 * sq + 128,
                           qt_ * TILE:(qt_ + 1) * TILE],
                    attno[at_][:, :]), wait=[at1])
                tok_attno_rd[at_] = S.now("xld")
                scst["pbc_rd"] = [at1]
                scst["t64_rd"] = {0: [at0], 1: [at1]}
                pend_ev = None

            for qt in range(NSEG):
                at = qt % 2
                aw = [tok_attno_rd[at]]
                if (qt, 0) not in scst["lqm_tok"]:
                    emit_lqm(qt)
                tok_exp = {}
                last_mm1 = {}
                last_mm2 = {0: None, 1: None}

                def emit_mm2(kp):
                    for h in (0, 1):
                        pacc = pC if h == 0 else pD
                        pp = kp % 2
                        for sub in range(2):
                            kb = 2 * kp + sub
                            i = kb - 4 * qt
                            off = 128 * i if i > 0 else 0
                            last_mm2[h] = S.on(
                                "pe",
                                (lambda t, pacc=pacc, h=h, pp=pp, sub=sub,
                                 off=off, kb=kb, qt=qt:
                                 t.matmul(pacc[0:65, off:TILE],
                                          vaug[:, kb, 65 * h:65 * h + 65],
                                          exp_t[h][pp][:, sub * 512 + off:
                                                       sub * 512 + 512],
                                          start=(kb == 0),
                                          stop=(kb == 4 * qt + 3))),
                                wait=(tok_exp[(kp, h)] +
                                      (scst["pacc_rd"][h] if kb == 0 else []))
                                if sub == 0 else (),
                                inc="pe")
                        scst["exp_rd"][(h, pp)] = [last_mm2[h]]

                for kp in range(2 * qt + 2):
                    pp = kp % 2
                    for sub in range(2):
                        kb = 2 * kp + sub
                        i = kb - 4 * qt
                        off = 128 * i if i > 0 else 0
                        for h in (0, 1):
                            pS = pA if h == 0 else pB
                            psw = (tok_exp[(kp - 1, h)][0:1]
                                   if (kp - 1, h) in tok_exp else [])
                            last_mm1[h] = S.on(
                                "pe",
                                (lambda t, pS=pS, h=h, sub=sub, off=off, kb=kb,
                                 qt=qt:
                                 t.matmul(pS[:, sub * 512 + off:sub * 512 + 512],
                                          jlkt[:, kb * 128:(kb + 1) * 128],
                                          lqm[h][qt % 2][:, off:TILE],
                                          start=True, stop=True)),
                                wait=([scst["lqm_tok"][(qt, h)]] + psw +
                                      (scst["pacc_rd"][h] if kp == 0 else []))
                                if sub == 0 else (),
                                inc="pe")
                    if kp == 1:
                        flush_ev()
                        if qt + 1 < NSEG:
                            emit_lqm(qt + 1)
                    for h in (0, 1):
                        pS = pA if h == 0 else pB
                        t_exp = ACT(
                            lambda a, h=h, pp=pp, pS=pS:
                            a.activation(exp_t[h][pp][:, :], pS[:, :], AF.Exp,
                                         scale=SCALE),
                            wait=[last_mm1[h]] + scst["exp_rd"][(h, pp)])
                        trtoks = []
                        for sub in range(2):
                            kb = 2 * kp + sub
                            i = kb - 4 * qt
                            if 0 <= i < 4:
                                trtoks.append(DVE(
                                    lambda v, h=h, pp=pp, sub=sub, i=i:
                                    v.tensor_mul(
                                        exp_t[h][pp][:, sub * 512 + 128 * i:
                                                     sub * 512 + 128 * i + 128],
                                        exp_t[h][pp][:, sub * 512 + 128 * i:
                                                     sub * 512 + 128 * i + 128],
                                        tri_bf[:, :]),
                                    wait=[t_exp]))
                        tok_exp[(kp, h)] = [t_exp] + trtoks
                    if kp > 0:
                        emit_mm2(kp - 1)
                emit_mm2(2 * qt + 1)
                rcs = {}
                t64s = {}
                for h in (0, 1):
                    pacc = pC if h == 0 else pD
                    tok_sr = ACT(lambda a, pacc=pacc, h=h:
                                 a.copy(s_row[h][:, :], pacc[64:65, :]),
                                 wait=[last_mm2[h], scst["rc"][h]])
                    rcs[h] = DVE(lambda v, h=h:
                                 v.reciprocal(s_rec[h][:, :], s_row[h][:, :]),
                                 wait=[tok_sr])
                    scst["rc"][h] = rcs[h]
                    t64s[h] = ACT(lambda a, pacc=pacc, h=h:
                                  a.copy(tmp64[h][:, :], pacc[0:64, :]),
                                  wait=[last_mm2[h]] + scst["t64_rd"][h])
                    scst["pacc_rd"][h] = [t64s[h], tok_sr]
                    scst["lqm_rd"][(qt, h)] = [last_mm2[h]]
                pend_ev = (qt, at, rcs, t64s, aw)
            flush_ev()
            tok_cc2[sq] = S.on("pool",
                               lambda g, sq=sq: g.collective_compute(
                                   "AllGather", ALU.bypass,
                                   replica_groups=[list(range(NCORE))],
                                   ins=[aga_in[128 * sq:128 * sq + 128, :].opt()],
                                   outs=[aga_out[1024 * sq:1024 * (sq + 1), :].opt()]),
                               wait=[S.now("xld")], inc="cc2")

        if stage <= 4:
            return
        wpre = [S.now("pe"), S.now("dve"), S.now("act"), S.now("pool"),
                S.now("xld")]
        wstate["last_use"] = [wpre, wpre]
        pre0 = emit_wload(0)
        attg_w = [tok_cc2[0], tok_cc2[1], S.now("pe"), S.now("dve"), S.now("act")]
        for dc in range(8):
            def rd(s, dc=dc):
                p = get_pid(s)
                return s.dma_start(
                    attg[:, dc, :],
                    aga_out[bass.ds(p["row"] + dc * 128, 128),
                            bass.ds(p["col"], NT)])
            SPD(rd, wait=attg_w)
        tok_attg = S.now("xld")
        wstate["last_use"][1] = wpre + [tok_attg]
        wstate["pre"] = [pre0, emit_wload(1)]
        pw_rd = {0: [], 1: []}
        for tt in range(NTILE):
            cols = slice(tt * TILE, (tt + 1) * TILE)
            for dc in range(8):
                dcp = dc % 2
                pw = pC if dcp == 0 else pD
                items = [(pw[:, :], wo_bf[:, kc, dc * 128:(dc + 1) * 128],
                          attg[:, kc, cols]) for kc in range(8)]
                tok_pw = mmgroup(items, wait=[tok_attg] + pw_rd[dcp])
                pw_rd[dcp] = [DVE(lambda v, dc=dc, cols=cols, pw=pw:
                                  v.scalar_tensor_tensor(X[:, dc, cols], pw[:, :],
                                                         bcol[:, 144 + dc:145 + dc],
                                                         X[:, dc, cols],
                                                         ALU.add, ALU.add),
                                  wait=[tok_pw])]

    if stage == 0:
        emit_ln()
    elif stage == 1:
        emit_gate(emit_ln())
    else:
        emit_ffn_pass()
        if stage >= 3:
            S.barrier()
            emit_attention()
            if stage >= 6:
                S.barrier()
                emit_ffn_pass()

    dbg_t = {}
    if dbg:
        S.barrier()
        for name in dbg:
            th = sb[name]
            shp = list(th.shape)
            nfree = int(np.prod(shp[1:]))
            dbg_t[name] = nc.dram_tensor("dbg_" + name, [shp[0], nfree], F32,
                                         kind="ExternalOutput")
            src = bass.AP(th, 0, [list(th[:].ap[0]), [1, nfree]])
            WLD(lambda g, name=name, src=src: g.dma_start(dbg_t[name][:, :], src))
        S.on("pool", None, wait=[S.now("wld")])

    S.barrier()
    for dc in range(8):
        STD(lambda s, dc=dc: s.dma_start(xout[dc * 128:(dc + 1) * 128, :],
                                         X[:, dc, :]))
    S.on("sp", None, wait=[S.now("st")])

    engmap = {"pe": "tensor", "act": "scalar", "dve": "vector",
              "pool": "gpsimd", "sp": "sync"}
    with nc.Block() as blk:
        for engname in Sch.ENGS:
            items = S.prog[engname]

            def body(eng, items=items):
                for fn, waits, inc in items:
                    for s_, v in waits:
                        eng.wait_ge(sems[s_], v)
                    if fn is not None:
                        ins = fn(eng)
                        if inc is not None:
                            ins.then_inc(sems[inc[0]], inc[1])
            getattr(blk, engmap[engname])(body)
    return nc


def _stage_inputs(inputs):
    f32 = np.float32
    x = np.asarray(inputs["L0_attn_x"], f32)
    xf = np.ascontiguousarray(x.reshape(B * T, D))
    gwT = np.ascontiguousarray(np.asarray(inputs["L0_ffn_gate_w"], f32).T)
    we1 = np.ascontiguousarray(np.asarray(inputs["L0_ffn_We1"], f32))
    we3 = np.ascontiguousarray(np.asarray(inputs["L0_ffn_We3"], f32))
    we2 = np.ascontiguousarray(np.asarray(inputs["L0_ffn_We2"], f32))
    ws1 = np.ascontiguousarray(np.asarray(inputs["L0_ffn_ws1"], f32))
    ws3 = np.ascontiguousarray(np.asarray(inputs["L0_ffn_ws3"], f32))
    ws2 = np.ascontiguousarray(np.asarray(inputs["L0_ffn_ws2"], f32))
    be1 = np.asarray(inputs["L0_ffn_be1"], f32)
    be3 = np.asarray(inputs["L0_ffn_be3"], f32)
    be2 = np.ascontiguousarray(np.asarray(inputs["L0_ffn_be2"], f32))
    bs1 = np.asarray(inputs["L0_ffn_bs1"], f32)
    bs3 = np.asarray(inputs["L0_ffn_bs3"], f32)
    bs2 = np.asarray(inputs["L0_ffn_bs2"], f32)
    gate_b = np.asarray(inputs["L0_ffn_gate_b"], f32)
    w1q = np.asarray(inputs["L0_attn_W1_q"], f32)
    w2q = np.asarray(inputs["L0_attn_W2_q"], f32)
    w1k = np.asarray(inputs["L0_attn_W1_k"], f32)
    w2k = np.asarray(inputs["L0_attn_W2_k"], f32)
    wv = np.asarray(inputs["L0_attn_W_v"], f32)
    bv = np.asarray(inputs["L0_attn_b_v"], f32)
    wo = np.ascontiguousarray(np.asarray(inputs["L0_attn_W_o"], f32))
    bo = np.asarray(inputs["L0_attn_b_o"], f32)

    bcol_h = np.zeros((128, NBCOL), f32)
    for e in range(E):
        for ic in range(4):
            bcol_h[:, 4 * e + ic] = be1[e, 128 * ic:128 * (ic + 1)]
            bcol_h[:, 64 + 4 * e + ic] = be3[e, 128 * ic:128 * (ic + 1)]
    for s_ in range(2):
        for ic in range(4):
            bcol_h[:, 128 + 4 * s_ + ic] = \
                bs1[512 * s_ + 128 * ic:512 * s_ + 128 * (ic + 1)]
            bcol_h[:, 136 + 4 * s_ + ic] = \
                bs3[512 * s_ + 128 * ic:512 * s_ + 128 * (ic + 1)]
    for dc in range(8):
        bcol_h[:, 144 + dc] = bo[128 * dc:128 * (dc + 1)]

    ident = np.eye(128, dtype=f32)
    trih = np.triu(np.ones((128, 128), f32))
    hmask_h = np.zeros((CP, HL), f32)
    for j in range(HL):
        hmask_h[6 * j:6 * j + 6, j] = 1.0
    ebc_h = np.zeros((E, E * 128), f32)
    for e in range(E):
        ebc_h[e, 128 * e:128 * (e + 1)] = 1.0
    growf = np.ascontiguousarray(np.tile(gate_b, 4).reshape(1, 4 * E))

    in_maps = []
    for c in range(NCORE):
        qs = slice(8 * c, 8 * c + 8)
        vs = slice(128 * c, 128 * (c + 1))
        brow_h = np.zeros((1, NBROW), f32)
        brow_h[0, 0:128] = bv[vs]
        brow_h[0, 256:1280] = bs2
        brow_h[0, 1280:2304] = bo
        wvqk = np.concatenate(
            [wv[:, vs], w1q[:D, qs], w2q[:D, qs], w1k[:, qs], w2k[:, qs]], axis=1)
        wqb = np.concatenate([w1q[D:, qs], w2q[D:, qs]], axis=1)
        in_maps.append({
            "xin": np.ascontiguousarray(xf[NT * c:NT * (c + 1)].T),
            "we1": we1, "we3": we3, "we2": we2,
            "ws1": ws1, "ws3": ws3, "ws2": ws2,
            "gwT": gwT, "be2m": be2, "bcol_h": bcol_h,
            "brow_h": brow_h, "growf": growf,
            "wvqk": np.ascontiguousarray(wvqk),
            "wqb": np.ascontiguousarray(wqb),
            "wo": wo,
            "ident": ident, "trih": trih, "hmask_h": hmask_h, "ebc_h": ebc_h,
        })
    return in_maps


_NC_CACHE = {}


def kernel(**inputs):
    if "nc" not in _NC_CACHE:
        _NC_CACHE["nc"] = build_nc()
    nc = _NC_CACHE["nc"]
    in_maps = _stage_inputs(inputs)
    res = run_bass_kernel_spmd(nc, in_maps, core_ids=list(range(NCORE)))
    out = np.empty((B * T, D), np.float32)
    for c in range(NCORE):
        out[NT * c:NT * (c + 1)] = res.results[c]["xout"].T
    return out.reshape(B, T, D)


# version 19 (final)
# speedup vs baseline: 1.1126x; 1.0125x over previous
import numpy as np
import ml_dtypes
from contextlib import ExitStack
import concourse.bass as bass
import concourse.mybir as mybir
from concourse.bass_utils import run_bass_kernel_spmd

F32 = mybir.dt.float32
BF16 = mybir.dt.bfloat16
AF = mybir.ActivationFunctionType
ALU = mybir.AluOpType
AX = mybir.AxisListType

NCORE = 8
B, T, D = 2, 8192, 1024
HL = 2
CP = HL * 6
NT = 2048
NBLK = NT // 128
TILE = 512
NTILE = NT // TILE
NKB = T // 128
NSEG = T // TILE
E = 16
EI = 512
NE = 18
SCALE = 6.0 ** -0.5
LN_EPS = 1e-5

NBCOL = 152
NBROW = 2304
WVQK = 160

_PAIRS = [(0, 1), (0, 2), (0, 3), (1, 2), (1, 3), (2, 3)]
_JSIGN = [1.0, -1.0, 1.0, 1.0, -1.0, 1.0]


class Sch:
    ENGS = ("pe", "act", "dve", "pool", "sp")

    def __init__(self):
        self.prog = {e: [] for e in self.ENGS}
        self.cnt = {}
        self.lastwait = {e: {} for e in self.ENGS}

    def on(self, eng, fn, wait=(), inc=None, incn=1):
        wl = []
        for tok in wait:
            if tok is None:
                continue
            s, v = tok
            if v > self.lastwait[eng].get(s, 0):
                wl.append((s, v))
                self.lastwait[eng][s] = v
        res = None
        if inc is not None:
            self.cnt[inc] = self.cnt.get(inc, 0) + incn
            res = (inc, self.cnt[inc])
        self.prog[eng].append((fn, tuple(wl), (inc, incn) if inc else None))
        return res

    def now(self, sem):
        return (sem, self.cnt.get(sem, 0))

    def barrier(self):
        snap = [(s, c) for s, c in self.cnt.items() if c > 0]
        for e in self.ENGS:
            self.on(e, None, wait=snap)


def fv(tsr, dims, off=0):
    base = tsr[:]
    return bass.AP(base.tensor, off, [list(base.ap[0])] +
                   [[s0, c0] for s0, c0 in dims])


def build_nc(stage=6, dbg=()):
    nc = bass.Bass("TRN2", target_bir_lowering=False, debug=False)
    dt = {}

    def din(name, shape):
        dt[name] = nc.dram_tensor(name, list(shape), F32, kind="ExternalInput")

    din("xin", (D, NT))
    din("we1", (E, D, EI))
    din("we3", (E, D, EI))
    din("we2", (E, EI, D))
    din("ws1", (D, D))
    din("ws3", (D, D))
    din("ws2", (D, D))
    din("gwT", (D, E))
    din("be2m", (E, D))
    din("bcol_h", (128, NBCOL))
    din("brow_h", (1, NBROW))
    din("growf", (1, 4 * E))
    din("wvqk", (D, WVQK))
    din("wqb", (D, 16))
    din("wo", (D, D))
    din("ident", (128, 128))
    din("trih", (128, 128))
    din("hmask_h", (CP, HL))
    din("ebc_h", (E, E * 128))

    xout = nc.dram_tensor("xout", [D, NT], F32, kind="ExternalOutput")
    HNT = NT // 2
    agx_in = [nc.dram_tensor(f"agx_in{i}", [D, HNT], BF16, kind="Internal")
              for i in (0, 1)]
    agx_out = [nc.dram_tensor(f"agx_out{i}", [NCORE * D, HNT], BF16,
                              kind="Internal") for i in (0, 1)]
    aga_in = nc.dram_tensor("aga_in", [B * 128, T], BF16, kind="Internal")
    aga_out = nc.dram_tensor("aga_out", [B * NCORE * 128, T], BF16, kind="Internal")

    sb = {}

    def sbuf(name, shape, dtype):
        sb[name] = nc.alloc_sbuf_tensor(name, list(shape), dtype)
        return sb[name]

    X = sbuf("X", (128, 8, NT), F32)
    id_bf = sbuf("id_bf", (128, 128), BF16)
    tri_bf = sbuf("tri_bf", (128, 128), BF16)
    ones_bf = sbuf("ones_bf", (1, TILE), BF16)
    onesr_f = sbuf("onesr_f", (1, 128), F32)
    onesc_f = sbuf("onesc_f", (128, 1), F32)
    bcol = sbuf("bcol", (128, NBCOL), F32)
    brow_bf = sbuf("brow_bf", (1, NBROW), BF16)
    be2_bf = sbuf("be2_bf", (E, D), BF16)
    gwT_bf = sbuf("gwT_bf", (128, 8, E), BF16)
    ebc_bf = sbuf("ebc_bf", (E, E * 128), BF16)
    gbb = sbuf("gbb", (128, 4, E), F32)
    hmask = sbuf("hmask", (CP, HL), F32)
    growf_s = sbuf("growf_s", (1, 4 * E), F32)
    s_row = [sbuf(f"s_row{h}", (1, TILE), F32) for h in (0, 1)]
    s_rec = [sbuf(f"s_rec{h}", (1, TILE), F32) for h in (0, 1)]
    r_tmp2 = s_row[0]
    g_eg = sbuf("g_eg", (128, 4, E), F32)
    g_or = sbuf("g_or", (128, 4, E), F32)
    g_bi = sbuf("g_bi", (128, 4, E), F32)
    g_s1 = sbuf("g_s1", (128, 4, 15), F32)
    g_s2 = sbuf("g_s2", (128, 4, 14), F32)
    g_s3 = sbuf("g_s3", (128, 4, 13), F32)
    g_G = sbuf("g_G", (128, 4, 4), F32)
    g_r2 = sbuf("g_r2", (128, 4, 4), F32)
    g_x = sbuf("g_x", (128, 4, 3), F32)
    g_cnt = sbuf("g_cnt", (128, 4, 4), F32)
    g_pen = sbuf("g_pen", (128, 4, 4), F32)
    g_sm = sbuf("g_sm", (128, 4, E), F32)
    g_mx8 = sbuf("g_mx8", (128, 8), F32)
    g_ss = sbuf("g_ss", (128, 4), F32)
    g_rs = sbuf("g_rs", (128, 4), F32)
    cb_bf = sbuf("cb_bf", (128, 4, E), BF16)
    lq_f = sbuf("lq_f", (128, 8, CP), F32)
    lk_f = sbuf("lk_f", (128, 8, CP), F32)
    sq24 = sbuf("sq24", (128, 8, CP), F32)
    n2q = sbuf("n2q", (128, 16), F32)
    n2k = sbuf("n2k", (128, 16), F32)
    rn_q = sbuf("rn_q", (128, 16), F32)
    rn_k = sbuf("rn_k", (128, 16), F32)
    prd = sbuf("prd", (128, 16), F32)
    prd2 = sbuf("prd2", (128, 16), F32)
    prdk = sbuf("prdk", (128, 16), F32)
    prd2k = sbuf("prd2k", (128, 16), F32)
    sq24k = sbuf("sq24k", (128, 8, CP), F32)
    lq24 = [sbuf(f"lq24_{i}", (128, 8, CP), BF16) for i in (0, 1)]
    jlk24 = [sbuf(f"jlk24_{i}", (128, 8, CP), BF16) for i in (0, 1)]
    xprev = sbuf("xprev", (128, 8, 128), BF16)

    OVL = 110 * 1024
    ovl_back = nc.alloc_sbuf_tensor("ovl_back", [128, OVL], mybir.dt.uint8)
    ovl_base = nc.lookup_mloc(ovl_back).addr

    def oalloc(name, shape, dtype, off):
        h = nc.virtual_sbuf_tensor(name, list(shape), dtype)
        nc._tensor(h.name, list(h.shape), h.dtype, type="SB")
        mloc = nc.lookup_mloc(h)
        mloc.allocated = True
        mloc.addr = ovl_base + off
        nbytes = int(np.prod(shape[1:])) * mybir.dt.size(dtype)
        assert off % 32 == 0 and off + nbytes <= OVL, (name, off, nbytes)
        sb[name] = h
        return h

    KB = 1024
    xn_bf = oalloc("xn_bf", (128, 8, NT), BF16, 0)
    w1b = [oalloc(f"w1b{i}", (128, 8, EI), BF16, 32 * KB + 8 * KB * i) for i in (0, 1)]
    w3b = [oalloc(f"w3b{i}", (128, 8, EI), BF16, 48 * KB + 8 * KB * i) for i in (0, 1)]
    w2b = [oalloc(f"w2b{i}", (128, 4, D), BF16, 64 * KB + 8 * KB * i) for i in (0, 1)]
    h_bf = [oalloc(f"h_bf{i}", (128, 4, TILE), BF16, 80 * KB + 4 * KB * i) for i in (0, 1)]
    sil_t = [oalloc(f"sil_t{i}", (128, TILE), F32, 88 * KB + 2 * KB * i) for i in (0, 1)]
    sq_t2 = [oalloc(f"sq_t{i}", (128, TILE), F32, 92 * KB + 2 * KB * i)
             for i in (0, 1)]
    t1_t = oalloc("t1_t", (128, TILE), F32, 106 * KB)
    cbT_bf = oalloc("cbT_bf", (E, NT), BF16, 96 * KB)
    jlkt = oalloc("jlkt", (CP, T), BF16, 0)
    lqt = oalloc("lqt", (CP, T), BF16, 16 * KB)
    vaug = oalloc("vaug", (128, NKB, HL * 65), BF16, 32 * KB)
    attg = oalloc("attg", (128, 8, NT), BF16, 0)
    xnseg = [oalloc(f"xnseg{i}", (128, 8, TILE), BF16, 66 * KB + 8 * KB * i)
             for i in (0, 1)]
    wo_bf = oalloc("wo_bf", (128, 8, D), BF16, 82 * KB)
    wvqk_bf = oalloc("wvqk_bf", (128, 8, WVQK), BF16, 98 * KB)
    wqb_bf = oalloc("wqb_bf", (128, 8, 16), BF16, 101 * KB)
    exp_t = [[oalloc(f"exp_t{h}_{i}", (128, 1024), BF16,
                     (102 * KB if h == 0 else 55552) + 2 * KB * i)
              for i in (0, 1)] for h in (0, 1)]
    lqm = [[oalloc(f"lqm{h}_{i}", (CP, TILE), BF16,
                   (106 + 2 * h + i) * KB) for i in (0, 1)] for h in (0, 1)]
    r_mean = oalloc("r_mean", (1, TILE), F32, 80 * KB)
    r_tmp = oalloc("r_tmp", (1, TILE), F32, 82 * KB)
    qk12 = [oalloc(f"qk12{i}", (128, 8, 32), F32, 59648 + 1 * KB * i)
            for i in (0, 1, 2)]
    tmp64 = [oalloc(f"tmp64{h}", (64, TILE), F32, 62720 + 2 * KB * h)
             for h in (0, 1)]
    attno = [oalloc(f"attno{i}", (128, TILE), BF16, 53504 + 1 * KB * i)
             for i in (0, 1)]

    pA = nc.alloc_psum_tensor("pA", [128, 1024], F32)
    pB = nc.alloc_psum_tensor("pB", [128, 1024], F32)
    pC = nc.alloc_psum_tensor("pC", [128, 512], F32)
    pD = nc.alloc_psum_tensor("pD", [128, 512], F32)
    pE_ = nc.alloc_psum_tensor("pE", [128, 512], F32)
    pbf = nc.alloc_psum_tensor("pbf", [128, 1024], BF16)

    semn = ["wld", "xld", "st", "pe", "act", "dve", "pool", "cc1", "cc2"]
    sems = {n: nc.alloc_semaphore(n) for n in semn}
    S = Sch()
    pidc = {}

    def get_pid(s):
        if "row" not in pidc:
            pid = s.partition_id()
            pidc["b"] = pid // 4
            pidc["row"] = (pid // 4) * 1024
            pidc["col"] = (pid - (pid // 4) * 4) * NT
        return pidc


    def PE(fn, wait=()):
        return S.on("pe", fn, wait=wait, inc="pe")

    def ACT(fn, wait=()):
        return S.on("act", fn, wait=wait, inc="act")

    def DVE(fn, wait=()):
        return S.on("dve", fn, wait=wait, inc="dve")

    def POOL(fn, wait=()):
        return S.on("pool", fn, wait=wait, inc="pool")

    def SPD(fn, wait=()):
        return S.on("sp", fn, wait=wait, inc="xld", incn=16)

    def WLD(fn, wait=()):
        return S.on("pool", fn, wait=wait, inc="wld", incn=16)

    def STD(fn, wait=()):
        return S.on("sp", fn, wait=wait, inc="st", incn=16)

    def mmgroup(items, wait=()):
        n = len(items)
        tok = None
        for i, (o, l, r) in enumerate(items):
            tok = S.on(
                "pe",
                (lambda t, o=o, l=l, r=r, st_=(i == 0), sp_=(i == n - 1):
                 t.matmul(o, l, r, start=st_, stop=sp_)),
                wait=wait if i == 0 else (),
                inc="pe" if i == n - 1 else None)
        return tok

    def _first(s):
        get_pid(s)
        return s.dma_start(bcol[:, :], dt["bcol_h"][:, :])
    SPD(_first)
    SPD(lambda s: s.dma_start(hmask[:, :], dt["hmask_h"][:, :]))
    SPD(lambda s: s.dma_start(growf_s[:, :], dt["growf"][:, :]))
    for dc in range(8):
        SPD(lambda s, dc=dc: s.dma_start(X[:, dc, :],
                                         dt["xin"][dc * 128:(dc + 1) * 128, :]))
    tok_xload = S.now("xld")

    WLD(lambda g: g.dma_start(id_bf[:, :], dt["ident"][:, :]))
    WLD(lambda g: g.dma_start(tri_bf[:, :], dt["trih"][:, :]))
    WLD(lambda g: g.dma_start(brow_bf[:, :], dt["brow_h"][:, :]))
    WLD(lambda g: g.dma_start(be2_bf[:, :], dt["be2m"][:, :]))
    WLD(lambda g: g.dma_start(ebc_bf[:, :], dt["ebc_h"][:, :]))
    WLD(lambda g: g.dma_start(gwT_bf[:, :, :],
                              dt["gwT"][:, :].rearrange("(k p) e -> p k e", p=128)))
    tok_const = S.now("wld")

    DVE(lambda v: v.memset(ones_bf[:, :], 1.0))
    DVE(lambda v: v.memset(onesr_f[:, :], 1.0))
    tok_ones = DVE(lambda v: v.memset(onesc_f[:, :], 1.0))

    PE(lambda t: t.matmul(pC[:, 0:64], onesr_f[:, :], growf_s[:, :],
                          start=True, stop=True),
       wait=[tok_xload, tok_ones])
    tok_gbb = ACT(lambda a: a.copy(gbb[:, :, :], pC[:, 0:64]), wait=[S.now("pe")])

    def emit_ln():
        prev = [S.now("act"), S.now("dve"), S.now("pe")]
        prev_bm = prev_br = prev_xn = None
        xn_toks = []
        for tt in range(NTILE):
            cols = slice(tt * TILE, (tt + 1) * TILE)
            tok_sum = mmgroup(
                [(pD[0:1, :], onesc_f[:, :], X[:, dc, cols]) for dc in range(8)],
                wait=prev)
            tok_mm = None
            tok_mm2q = None
            for dc in range(8):
                sp_ = sq_t2[dc % 2]
                tok_sq = DVE(lambda v, dc=dc, cols=cols, sp_=sp_:
                             v.tensor_mul(sp_[:, :], X[:, dc, cols], X[:, dc, cols]),
                             wait=([tok_mm2q] if dc >= 2 else prev))
                tok_mm = S.on("pe",
                              (lambda t, st_=(dc == 0), sp2_=(dc == 7), sp_=sp_:
                               t.matmul(pE_[0:1, :], onesc_f[:, :], sp_[:, :],
                                        start=st_, stop=sp2_)),
                              wait=[tok_sq], inc="pe")
                if dc % 2 == 1:
                    tok_mm2q = tok_mm
            tok_mean = ACT(lambda a: a.activation(r_mean[:, :], pD[0:1, :], AF.Copy,
                                                  scale=1.0 / D),
                           wait=[tok_sum, prev_bm])
            tok_msq = ACT(lambda a: a.activation(r_tmp[:, :], pE_[0:1, :], AF.Copy,
                                                 scale=1.0 / D),
                          wait=[tok_mm, prev_br])
            tok_m2 = DVE(lambda v: v.tensor_mul(r_tmp2[:, :], r_mean[:, :],
                                                r_mean[:, :]), wait=[tok_mean])
            tok_var = DVE(lambda v: v.scalar_tensor_tensor(
                r_tmp[:, :], r_tmp[:, :], LN_EPS, r_tmp2[:, :],
                ALU.add, ALU.subtract), wait=[tok_msq, tok_m2])
            tok_lnv = ACT(lambda a: a.activation(r_tmp[:, :], r_tmp[:, :], AF.Ln),
                          wait=[tok_var])
            tok_rstd = ACT(lambda a: a.activation(r_tmp[:, :], r_tmp[:, :], AF.Exp,
                                                  scale=-0.5), wait=[tok_lnv])
            tok_nm = DVE(lambda v: v.tensor_scalar_mul(r_mean[:, :], r_mean[:, :],
                                                       -1.0), wait=[tok_var])
            prev_bm = PE(lambda t: t.matmul(pA[:, 0:512], onesr_f[:, :],
                                            r_mean[:, :], start=True, stop=True),
                         wait=[tok_nm, prev_xn])
            prev_br = PE(lambda t: t.matmul(pA[:, 512:1024], onesr_f[:, :],
                                            r_tmp[:, :], start=True, stop=True),
                         wait=[tok_rstd])
            for dc in range(8):
                DVE(lambda v, dc=dc, cols=cols:
                    v.tensor_add(t1_t[:, :], X[:, dc, cols], pA[:, 0:512]),
                    wait=[prev_bm, prev_br] if dc == 0 else ())
                prev_xn = DVE(lambda v, dc=dc, cols=cols:
                              v.tensor_mul(xn_bf[:, dc, cols], t1_t[:, :],
                                           pA[:, 512:1024]))
            prev = [prev_xn, prev_bm, prev_br, tok_rstd]
            xn_toks.append(prev_xn)
        return xn_toks

    def emit_gate(tok_xn):
        tok_tr_rd = None
        tok_pg_rd = None
        cbT_toks = []
        for tt in range(NTILE):
            toks_pg = []
            for b4 in range(4):
                bc = slice(tt * TILE + b4 * 128, tt * TILE + b4 * 128 + 128)
                toks_pg.append(mmgroup(
                    [(pC[:, 16 * b4:16 * b4 + 16], xn_bf[:, kc, bc],
                      gwT_bf[:, kc, :]) for kc in range(8)],
                    wait=[tok_xn[tt], tok_const, tok_pg_rd] if b4 == 0 else ()))
            tok_pg = toks_pg[-1]
            tok_exp = ACT(lambda a: a.activation(g_eg[:, :, :], pC[:, 0:64],
                                                 AF.Exp), wait=[tok_pg])
            tok_pg_rd = tok_exp
            tk = DVE(lambda v: v.tensor_reduce(g_ss[:, :],
                                               fv(g_eg, [(E, 4), (1, E)]),
                                               AX.X, ALU.add), wait=[tok_exp])
            tk = DVE(lambda v: v.reciprocal(g_rs[:, :], g_ss[:, :]), wait=[tk])
            for b4 in range(4):
                tk = DVE(lambda v, b4=b4: v.tensor_scalar_mul(
                    g_or[:, b4, :], g_eg[:, b4, :], g_rs[:, b4:b4 + 1]),
                    wait=[tk])
            tk = DVE(lambda v: v.tensor_tensor(g_bi[:, :, :], g_or[:, :, :],
                                               gbb[:, :, :], ALU.add),
                     wait=[tk, tok_gbb])
            tk = DVE(lambda v: v.tensor_add(fv(g_s1, [(15, 4), (1, 15)]),
                                            fv(g_bi, [(E, 4), (1, 15)]),
                                            fv(g_bi, [(E, 4), (1, 15)], 1)),
                     wait=[tk])
            tk = DVE(lambda v: v.tensor_add(fv(g_s2, [(14, 4), (1, 14)]),
                                            fv(g_bi, [(E, 4), (1, 14)]),
                                            fv(g_bi, [(E, 4), (1, 14)], 2)),
                     wait=[tk])
            tk = DVE(lambda v: v.tensor_add(fv(g_s3, [(13, 4), (1, 13)]),
                                            fv(g_bi, [(E, 4), (1, 13)]),
                                            fv(g_bi, [(E, 4), (1, 13)], 3)),
                     wait=[tk])
            tk = DVE(lambda v: v.tensor_reduce(
                g_G[:, :, :], fv(g_s1, [(15, 4), (4, 4), (1, 3)]),
                AX.X, ALU.max), wait=[tk])
            tk = DVE(lambda v: v.tensor_reduce(
                g_r2[:, :, :], fv(g_s2, [(14, 4), (4, 4), (1, 2)]),
                AX.X, ALU.max), wait=[tk])
            tk = DVE(lambda v: v.tensor_max(g_G[:, :, :], g_G[:, :, :],
                                            g_r2[:, :, :]), wait=[tk])
            tk = DVE(lambda v: v.tensor_max(g_G[:, :, :], g_G[:, :, :],
                                            fv(g_s3, [(13, 4), (4, 4)])),
                     wait=[tk])
            tk = DVE(lambda v: v.memset(g_cnt[:, :, :], 0.0), wait=[tk])
            for s_ in (1, 2, 3):
                tk = DVE(lambda v, s_=s_: v.tensor_tensor(
                    fv(g_x, [(3, 4), (1, 4 - s_)]),
                    fv(g_G, [(4, 4), (1, 4 - s_)], s_),
                    fv(g_G, [(4, 4), (1, 4 - s_)]), ALU.is_gt), wait=[tk])
                tk = DVE(lambda v, s_=s_: v.tensor_add(
                    fv(g_cnt, [(4, 4), (1, 4 - s_)]),
                    fv(g_cnt, [(4, 4), (1, 4 - s_)]),
                    fv(g_x, [(3, 4), (1, 4 - s_)])), wait=[tk])
                tk = DVE(lambda v, s_=s_: v.tensor_scalar(
                    fv(g_x, [(3, 4), (1, 4 - s_)]),
                    fv(g_x, [(3, 4), (1, 4 - s_)]),
                    -1.0, 1.0, ALU.mult, ALU.add), wait=[tk])
                tk = DVE(lambda v, s_=s_: v.tensor_add(
                    fv(g_cnt, [(4, 4), (1, 4 - s_)], s_),
                    fv(g_cnt, [(4, 4), (1, 4 - s_)], s_),
                    fv(g_x, [(3, 4), (1, 4 - s_)])), wait=[tk])
            tk = DVE(lambda v: v.tensor_scalar(g_pen[:, :, :], g_cnt[:, :, :],
                                               1.5, -1e30, ALU.is_ge, ALU.mult),
                     wait=[tk])
            for j in range(4):
                tk = DVE(lambda v, j=j: v.tensor_add(
                    fv(g_sm, [(E, 4), (4, 4)], j),
                    fv(g_bi, [(E, 4), (4, 4)], j),
                    g_pen[:, :, :]), wait=[tk])
            for b4 in range(4):
                tk = DVE(lambda v, b4=b4: v.max(g_mx8[:, :], g_sm[:, b4, :]),
                         wait=[tk])
                tk = DVE(lambda v, b4=b4: v.tensor_single_scalar(
                    g_sm[:, b4, :], g_sm[:, b4, :], g_mx8[:, 1:2], ALU.is_ge),
                    wait=[tk])
            tok_cb = DVE(lambda v: v.tensor_mul(cb_bf[:, :, :], g_sm[:, :, :],
                                                g_or[:, :, :]), wait=[tk])
            for b4 in range(4):
                bc = slice(tt * TILE + b4 * 128, tt * TILE + b4 * 128 + 128)
                tok_tr = PE(lambda t, b4=b4: t.transpose(
                    pbf[0:E, 128 * b4:128 * b4 + 128], cb_bf[:, b4, :],
                    id_bf[:, :]),
                    wait=[tok_cb, tok_tr_rd, tok_const] if b4 == 0 else ())
            tok_trl = S.now("pe")
            tok_cbT = DVE(lambda v, tt=tt: v.tensor_copy(
                cbT_bf[:, tt * TILE:(tt + 1) * TILE], pbf[0:E, 0:512]),
                wait=[tok_trl])
            tok_tr_rd = tok_cbT
            cbT_toks.append(tok_cbT)
        return cbT_toks

    wstate = {"last_use": [None, None]}

    def emit_wload(e):
        par = e % 2
        wa = wstate["last_use"][par]
        wa = wa if isinstance(wa, list) else [wa]
        if e < E:
            WLD(lambda g, e=e, par=par: g.dma_start(
                w1b[par][:, :, :],
                dt["we1"][e, :, :].rearrange("(k p) i -> p k i", p=128)), wait=wa)
            WLD(lambda g, e=e, par=par: g.dma_start(
                w3b[par][:, :, :],
                dt["we3"][e, :, :].rearrange("(k p) i -> p k i", p=128)))
            WLD(lambda g, e=e, par=par: g.dma_start(
                w2b[par][:, :, :],
                dt["we2"][e, :, :].rearrange("(k p) d -> p k d", p=128)))
        else:
            s_ = e - E
            WLD(lambda g, s_=s_, par=par: g.dma_start(
                w1b[par][:, :, :],
                dt["ws1"][:, s_ * EI:(s_ + 1) * EI]
                .rearrange("(k p) i -> p k i", p=128)), wait=wa)
            WLD(lambda g, s_=s_, par=par: g.dma_start(
                w3b[par][:, :, :],
                dt["ws3"][:, s_ * EI:(s_ + 1) * EI]
                .rearrange("(k p) i -> p k i", p=128)))
            WLD(lambda g, s_=s_, par=par: g.dma_start(
                w2b[par][:, 0:4, :],
                dt["ws2"][s_ * EI:(s_ + 1) * EI, :]
                .rearrange("(k p) d -> p k d", p=128)))
        return S.now("wld")

    def emit_experts(tok_cbT):
        fence = [S.now("act"), S.now("dve")]
        est = {"sil_rd": [fence, fence], "p13_rd": {0: fence, 1: fence},
               "h_rd": [fence, fence], "p2_rd": {0: fence, 1: fence},
               "be_rd": fence}
        if wstate.get("pre"):
            wtok = wstate.pop("pre")
        else:
            wtok = [emit_wload(0), emit_wload(1)]
        pend = None

        def emit_p2(par, h_par, tok_h, cols, e):
            tok_p2l = None
            for dc in range(8):
                dcp = dc % 2
                pP2 = pD if dcp == 0 else pE_
                items = [(pP2[:, :], w2b[par][:, kc, dc * 128:(dc + 1) * 128],
                          h_bf[h_par][:, kc, :]) for kc in range(4)]
                if e == 0:
                    items.append((pP2[:, :], be2_bf[:, dc * 128:(dc + 1) * 128],
                                  cbT_bf[:, cols]))
                if e == E + 1:
                    items.append((pP2[:, :],
                                  brow_bf[:, 256 + dc * 128:256 + dc * 128 + 128],
                                  ones_bf[:, :]))
                tok_p2 = mmgroup(items, wait=[tok_h] + est["p2_rd"][dcp])
                tok_ev = DVE(lambda v, dc=dc, cols=cols, pP2=pP2:
                             v.scalar_tensor_tensor(X[:, dc, cols], pP2[:, :],
                                                    0.5, X[:, dc, cols],
                                                    ALU.mult, ALU.add),
                             wait=[tok_p2])
                est["p2_rd"][dcp] = [tok_ev]
                tok_p2l = tok_p2
            est["h_rd"][h_par] = [tok_p2l]

        for e in range(NE):
            par = e % 2
            mywtok = wtok[par]
            for tt in range(NTILE):
                cols = slice(tt * TILE, (tt + 1) * TILE)
                if e < E:
                    tok_be = PE(lambda t, e=e, cols=cols:
                                t.matmul(pC[:, :], ebc_bf[:, e * 128:(e + 1) * 128],
                                         cbT_bf[:, cols], start=True, stop=True),
                                wait=[tok_cbT[tt]] + est["be_rd"])
                h_par = (e * NTILE + tt) % 2
                tok_h = None
                for ic in range(4):
                    icp = ic % 2
                    pP1 = pA[:, icp * 512:(icp + 1) * 512]
                    pP3 = pB[:, icp * 512:(icp + 1) * 512]
                    mmgroup([(pP1, w1b[par][:, kc, ic * 128:(ic + 1) * 128],
                              xn_bf[:, kc, cols]) for kc in range(8)],
                            wait=[mywtok] + est["p13_rd"][icp])
                    tok_p13 = mmgroup([(pP3, w3b[par][:, kc, ic * 128:(ic + 1) * 128],
                                        xn_bf[:, kc, cols]) for kc in range(8)])
                    bix = (4 * e + ic) if e < E else (128 + 4 * (e - E) + ic)
                    b3x = (64 + 4 * e + ic) if e < E else (136 + 4 * (e - E) + ic)
                    tok_sil = ACT(lambda a, bix=bix, icp=icp, pP1=pP1:
                                  a.activation(sil_t[icp][:, :], pP1, AF.Silu,
                                               bias=bcol[:, bix:bix + 1]),
                                  wait=[tok_p13] + est["sil_rd"][icp])
                    if e < E:
                        DVE(lambda v, b3x=b3x, icp=icp, pP3=pP3:
                            v.scalar_tensor_tensor(t1_t[:, :], pP3,
                                                   bcol[:, b3x:b3x + 1],
                                                   sil_t[icp][:, :],
                                                   ALU.add, ALU.mult),
                            wait=[tok_sil] + est["h_rd"][h_par])
                        tok_h = DVE(lambda v, ic=ic, h_par=h_par:
                                    v.tensor_mul(h_bf[h_par][:, ic, :], t1_t[:, :],
                                                 pC[:, :]),
                                    wait=[tok_be])
                    else:
                        tok_h = DVE(lambda v, b3x=b3x, icp=icp, ic=ic, h_par=h_par,
                                    pP3=pP3:
                                    v.scalar_tensor_tensor(h_bf[h_par][:, ic, :],
                                                           pP3,
                                                           bcol[:, b3x:b3x + 1],
                                                           sil_t[icp][:, :],
                                                           ALU.add, ALU.mult),
                                    wait=[tok_sil] + est["h_rd"][h_par])
                    est["sil_rd"][icp] = [tok_h]
                    est["p13_rd"][icp] = [tok_sil, tok_h]
                if e < E:
                    est["be_rd"] = [tok_h]
                if pend is not None:
                    emit_p2(*pend)
                    if pend[4] != e and pend[4] + 2 < NE:
                        wstate["last_use"][pend[0]] = S.now("pe")
                        wtok[pend[0]] = emit_wload(pend[4] + 2)
                pend = (par, h_par, tok_h, cols, e)
        emit_p2(*pend)

    def emit_ffn_pass():
        tok_xn = emit_ln()
        tok_cbT = emit_gate(tok_xn)
        emit_experts(tok_cbT)

    def emit_attention():
        tok_xn = emit_ln()
        tok_cc1 = [None, None]
        for hi in (0, 1):
            for dc in range(8):
                SPD(lambda s, dc=dc, hi=hi: s.dma_start(
                    agx_in[hi][dc * 128:(dc + 1) * 128, :],
                    xn_bf[:, dc, hi * HNT:(hi + 1) * HNT]),
                    wait=([tok_xn[1]] if (hi, dc) == (0, 0) else
                          [tok_xn[-1]] if (hi, dc) == (1, 0) else ()))
            tok_cc1[hi] = S.on("pool",
                               lambda g, hi=hi: g.collective_compute(
                                   "AllGather", ALU.bypass,
                                   replica_groups=[list(range(NCORE))],
                                   ins=[agx_in[hi].ap().opt()],
                                   outs=[agx_out[hi].ap().opt()]),
                               wait=[S.now("xld")], inc="cc1")
        wa = [S.now("pe"), S.now("dve"), S.now("act")]
        WLD(lambda g: g.dma_start(
            wvqk_bf[:, :, :], dt["wvqk"][:, :].rearrange("(k p) v -> p k v", p=128)),
            wait=wa)
        WLD(lambda g: g.dma_start(
            wqb_bf[:, :, :], dt["wqb"][:, :].rearrange("(k p) v -> p k v", p=128)))
        WLD(lambda g: g.dma_start(
            wo_bf[:, :, :], dt["wo"][:, :].rearrange("(k p) v -> p k v", p=128)))
        tok_wat = S.now("wld")

        tok_attno_rd = [None, None]
        tok_cc2 = {}
        for sq in range(B):
            tok_vones = DVE(
                lambda v: v.memset(
                    vaug[:, :, :].rearrange("p k (h c) -> p k h c", c=65)[:, :, :, 64],
                    1.0),
                wait=wa + [S.now("pe")])
            seg_ld = [None, None]
            seg_rd = [None, None]
            seg_dve_rd = [None, None]
            pvk_rd = {0: [S.now("act"), S.now("dve")],
                      1: [S.now("act"), S.now("dve")]}
            qk_rd = [[], [], []]
            tr_rd = None
            pend_tr = None

            def flush_tr():
                nonlocal pend_tr, tr_rd
                if pend_tr is None:
                    return
                pr_, pi_, tok_chain = pend_tr
                for jj in range(8):
                    PE(lambda t, jj=jj, pr_=pr_: t.transpose(
                        pbf[0:CP, 128 * jj:128 * jj + 128],
                        lq24[pr_][:, jj, :], id_bf[:, :]),
                       wait=(tok_chain + [tok_const, tr_rd]) if jj == 0 else ())
                tok_q8 = S.now("pe")
                c1 = DVE(lambda v, pi_=pi_: v.tensor_copy(
                    lqt[:, pi_ * 1024:(pi_ + 1) * 1024], pbf[0:CP, :]),
                    wait=[tok_q8])
                for jj in range(8):
                    PE(lambda t, jj=jj, pr_=pr_: t.transpose(
                        pbf[0:CP, 128 * jj:128 * jj + 128],
                        jlk24[pr_][:, jj, :], id_bf[:, :]),
                       wait=[c1] if jj == 0 else ())
                tok_k8 = S.now("pe")
                tr_rd = DVE(lambda v, pi_=pi_: v.tensor_copy(
                    jlkt[:, pi_ * 1024:(pi_ + 1) * 1024], pbf[0:CP, :]),
                    wait=[tok_k8])
                pend_tr = None

            for seg in range(NSEG):
                par = seg % 2
                pr = (seg // 2) % 2
                ri = 4 * sq + seg // 4
                co = (seg % 4) * TILE
                hi = co // HNT
                hco = co % HNT
                seg_ld[par] = SPD(
                    lambda s, ri=ri, hco=hco, par=par, hi=hi: s.dma_start(
                        xnseg[par][:, :, :],
                        agx_out[hi][1024 * ri:1024 * (ri + 1), hco:hco + TILE]
                        .rearrange("(k p) t -> p k t", p=128)),
                    wait=[tok_cc1[hi], seg_rd[par], seg_dve_rd[par]])
                tok_xp = None
                for j in range(4):
                    kb = 4 * seg + j
                    jj = (seg % 2) * 4 + j
                    jc = slice(j * 128, (j + 1) * 128)
                    pv = pC if kb % 2 == 0 else pD
                    if j == 0:
                        stw = [seg_ld[par], seg_ld[1 - par], S.now("pe")]
                        for kc in range(8):
                            DVE(lambda v, kc=kc, par=par:
                                v.tensor_copy(xprev[:, kc, 1:128],
                                              xnseg[par][:, kc, 0:127]),
                                wait=stw if kc == 0 else ())
                            if seg == 0:
                                DVE(lambda v, kc=kc: v.memset(xprev[:, kc, 0:1],
                                                              0.0))
                            else:
                                DVE(lambda v, kc=kc, par=par:
                                    v.tensor_copy(xprev[:, kc, 0:1],
                                                  xnseg[1 - par][:, kc,
                                                                 TILE - 1:TILE]))
                        tok_xp = S.now("dve")
                        seg_dve_rd[1 - par] = tok_xp
                    items = [(pv[:, 0:WVQK], xnseg[par][:, kc, jc],
                              wvqk_bf[:, kc, :]) for kc in range(8)]
                    for kc in range(8):
                        lhs = (xprev[:, kc, :] if j == 0 else
                               xnseg[par][:, kc, j * 128 - 1:(j + 1) * 128 - 1])
                        items.append((pv[:, 128:144], lhs, wqb_bf[:, kc, :]))
                    items.append((pv[:, 0:128], ones_bf[:, 0:128],
                                  brow_bf[:, 0:128]))
                    tok_pv = mmgroup(
                        items,
                        wait=[seg_ld[par], tok_wat, tok_vones,
                              tok_xp if j == 0 else None] + pvk_rd[kb % 2])
                    ev = []
                    for hh in range(HL):
                        ev.append(ACT(lambda a, hh=hh, kb=kb, pv=pv:
                                      a.copy(vaug[:, kb, 65 * hh:65 * hh + 64],
                                             pv[:, 64 * hh:64 * hh + 64]),
                                      wait=[tok_pv]))
                    ev.append(ACT(lambda a, jj=jj, pr=pr, pv=pv:
                                  a.copy(qk12[pr][:, jj, :], pv[:, 128:160]),
                                  wait=[tok_pv] + (qk_rd[pr] if jj == 0 else [])))
                    pvk_rd[kb % 2] = ev
                seg_rd[par] = S.now("pe")
                if seg % 2 == 0:
                    continue
                flush_tr()
                tok_qk = S.now("act")
                norm_toks = []
                qk_rd[pr] = []
                for side in ("q", "k"):
                    ENG = POOL if side == "q" else DVE
                    colbase = 0 if side == "q" else 16
                    Ldst = lq_f if side == "q" else lk_f
                    sqd = sq24 if side == "q" else sq24k
                    n2d = n2q if side == "q" else n2k
                    rnd = rn_q if side == "q" else rn_k
                    pa_ = prd if side == "q" else prdk
                    pb_ = prd2 if side == "q" else prd2k
                    tk = [tok_qk]
                    for p, (i, jx) in enumerate(_PAIRS):
                        t1_ = ENG(lambda g, i=i, jx=jx, pr=pr, cb_=colbase, pa_=pa_:
                                  g.tensor_mul(pa_[:, :],
                                               fv(qk12[pr], [(32, 8), (4, 2)],
                                                  cb_ + i),
                                               fv(qk12[pr], [(32, 8), (4, 2)],
                                                  cb_ + 8 + jx)),
                                  wait=tk if isinstance(tk, list) else [tk])
                        t2_ = ENG(lambda g, i=i, jx=jx, pr=pr, cb_=colbase, pb_=pb_:
                                  g.tensor_mul(pb_[:, :],
                                               fv(qk12[pr], [(32, 8), (4, 2)],
                                                  cb_ + jx),
                                               fv(qk12[pr], [(32, 8), (4, 2)],
                                                  cb_ + 8 + i)),
                                  wait=[t1_])
                        tk = ENG(lambda g, p=p, Ldst=Ldst, pa_=pa_, pb_=pb_:
                                 g.tensor_sub(fv(Ldst, [(CP, 8), (6, 2)], p),
                                              pa_[:, :], pb_[:, :]),
                                 wait=[t2_])
                    qk_rd[pr].append(tk)
                    tk = ENG(lambda g, Ldst=Ldst, sqd=sqd:
                             g.tensor_mul(sqd[:, :, :], Ldst[:, :, :],
                                          Ldst[:, :, :]), wait=[tk])
                    tk = DVE(lambda v, n2d=n2d, sqd=sqd:
                             v.tensor_reduce(n2d[:, :],
                                             fv(sqd, [(CP, 8), (6, 2), (1, 6)]),
                                             AX.X, ALU.add), wait=[tk])
                    tok_n2 = DVE(lambda v, n2d=n2d:
                                 v.tensor_scalar_max(n2d[:, :], n2d[:, :], 1e-24),
                                 wait=[tk])
                    tok_lnr = ACT(lambda a, rnd=rnd, n2d=n2d:
                                  a.activation(rnd[:, :], n2d[:, :], AF.Ln),
                                  wait=[tok_n2])
                    tok_rn = ACT(lambda a, rnd=rnd:
                                 a.activation(rnd[:, :], rnd[:, :], AF.Exp,
                                              scale=-0.5), wait=[tok_lnr])
                    if side == "q":
                        tkn = None
                        for p in range(6):
                            tkn = DVE(lambda v, p=p, pr=pr: v.scalar_tensor_tensor(
                                fv(lq24[pr % 2], [(CP, 8), (6, 2)], p),
                                fv(lq_f, [(CP, 8), (6, 2)], p), 1.0,
                                fv(rn_q, [(2, 8), (1, 2)]), ALU.mult, ALU.mult),
                                wait=[tok_rn] if p == 0 else [tkn])
                        norm_toks.append(tkn)
                    else:
                        tkn = None
                        for c_ in range(6):
                            tkn = DVE(lambda v, c_=c_, pr=pr: v.scalar_tensor_tensor(
                                fv(jlk24[pr % 2], [(CP, 8), (6, 2)], c_),
                                fv(lk_f, [(CP, 8), (6, 2)], 5 - c_), _JSIGN[c_],
                                fv(rn_k, [(2, 8), (1, 2)]), ALU.mult, ALU.mult),
                                wait=[tok_rn] if c_ == 0 else [tkn])
                        norm_toks.append(tkn)
                tk = norm_toks
                pend_tr = (pr % 2, seg // 2, tk)
            flush_tr()
            tok_proj = [S.now("pe"), S.now("dve"), S.now("act"), S.now("pool")]

            if stage <= 3:
                return

            scst = {"exp_rd": {(h, i): [] for h in (0, 1) for i in (0, 1)},
                    "pacc_rd": {0: [], 1: []}, "pbc_rd": [],
                    "t64_rd": {0: [], 1: []}, "rc": {0: None, 1: None},
                    "lqm_tok": {}, "lqm_rd": {}}
            pend_ev = None

            def emit_lqm(qt):
                for h in (0, 1):
                    buf = lqm[h][qt % 2]
                    scst["lqm_tok"][(qt, h)] = DVE(
                        lambda v, qt=qt, h=h, buf=buf: v.tensor_scalar_mul(
                            buf[:, :], lqt[:, qt * TILE:(qt + 1) * TILE],
                            hmask[:, h:h + 1]),
                        wait=tok_proj + scst["lqm_rd"].get((qt - 2, h), []))

            def flush_ev():
                nonlocal pend_ev
                if pend_ev is None:
                    return
                qt_, at_, rcs, t64s, aw_ = pend_ev
                bc0 = PE(lambda t: t.matmul(pE_[0:64, :], onesr_f[:, 0:64],
                                            s_rec[0][:, :], start=True, stop=True),
                         wait=[rcs[0]] + scst["pbc_rd"])
                bc1 = PE(lambda t: t.matmul(pE_[64:128, :], onesr_f[:, 0:64],
                                            s_rec[1][:, :], start=True, stop=True),
                         wait=[rcs[1]])
                at0 = DVE(lambda v, at_=at_:
                          v.tensor_mul(attno[at_][0:64, :], tmp64[0][:, :],
                                       pE_[0:64, :]),
                          wait=[bc1, t64s[0]] + scst["t64_rd"][0] + aw_)
                at1 = DVE(lambda v, at_=at_:
                          v.tensor_mul(attno[at_][64:128, :], tmp64[1][:, :],
                                       pE_[64:128, :]),
                          wait=[t64s[1]] + scst["t64_rd"][1])
                SPD(lambda s, sq=sq, qt_=qt_, at_=at_: s.dma_start(
                    aga_in[128 * sq:128 * sq + 128,
                           qt_ * TILE:(qt_ + 1) * TILE],
                    attno[at_][:, :]), wait=[at1])
                tok_attno_rd[at_] = S.now("xld")
                scst["pbc_rd"] = [at1]
                scst["t64_rd"] = {0: [at0], 1: [at1]}
                pend_ev = None

            for qt in range(NSEG):
                at = qt % 2
                aw = [tok_attno_rd[at]]
                if (qt, 0) not in scst["lqm_tok"]:
                    emit_lqm(qt)
                tok_exp = {}
                last_mm1 = {}
                last_mm2 = {0: None, 1: None}

                def emit_mm2(kp):
                    for h in (0, 1):
                        pacc = pC if h == 0 else pD
                        pp = kp % 2
                        for sub in range(2):
                            kb = 2 * kp + sub
                            i = kb - 4 * qt
                            off = 128 * i if i > 0 else 0
                            last_mm2[h] = S.on(
                                "pe",
                                (lambda t, pacc=pacc, h=h, pp=pp, sub=sub,
                                 off=off, kb=kb, qt=qt:
                                 t.matmul(pacc[0:65, off:TILE],
                                          vaug[:, kb, 65 * h:65 * h + 65],
                                          exp_t[h][pp][:, sub * 512 + off:
                                                       sub * 512 + 512],
                                          start=(kb == 0),
                                          stop=(kb == 4 * qt + 3))),
                                wait=(tok_exp[(kp, h)] +
                                      (scst["pacc_rd"][h] if kb == 0 else []))
                                if sub == 0 else (),
                                inc="pe")
                        scst["exp_rd"][(h, pp)] = [last_mm2[h]]

                for kp in range(2 * qt + 2):
                    pp = kp % 2
                    for sub in range(2):
                        kb = 2 * kp + sub
                        i = kb - 4 * qt
                        off = 128 * i if i > 0 else 0
                        for h in (0, 1):
                            pS = pA if h == 0 else pB
                            psw = (tok_exp[(kp - 1, h)][0:1]
                                   if (kp - 1, h) in tok_exp else [])
                            last_mm1[h] = S.on(
                                "pe",
                                (lambda t, pS=pS, h=h, sub=sub, off=off, kb=kb,
                                 qt=qt:
                                 t.matmul(pS[:, sub * 512 + off:sub * 512 + 512],
                                          jlkt[:, kb * 128:(kb + 1) * 128],
                                          lqm[h][qt % 2][:, off:TILE],
                                          start=True, stop=True)),
                                wait=([scst["lqm_tok"][(qt, h)]] + psw +
                                      (scst["pacc_rd"][h] if kp == 0 else []))
                                if sub == 0 else (),
                                inc="pe")
                    if kp == 1:
                        flush_ev()
                        if qt + 1 < NSEG:
                            emit_lqm(qt + 1)
                    for h in (0, 1):
                        pS = pA if h == 0 else pB
                        t_exp = ACT(
                            lambda a, h=h, pp=pp, pS=pS:
                            a.activation(exp_t[h][pp][:, :], pS[:, :], AF.Exp,
                                         scale=SCALE),
                            wait=[last_mm1[h]] + scst["exp_rd"][(h, pp)])
                        trtoks = []
                        for sub in range(2):
                            kb = 2 * kp + sub
                            i = kb - 4 * qt
                            if 0 <= i < 4:
                                trtoks.append(DVE(
                                    lambda v, h=h, pp=pp, sub=sub, i=i:
                                    v.tensor_mul(
                                        exp_t[h][pp][:, sub * 512 + 128 * i:
                                                     sub * 512 + 128 * i + 128],
                                        exp_t[h][pp][:, sub * 512 + 128 * i:
                                                     sub * 512 + 128 * i + 128],
                                        tri_bf[:, :]),
                                    wait=[t_exp]))
                        tok_exp[(kp, h)] = [t_exp] + trtoks
                    if kp > 0:
                        emit_mm2(kp - 1)
                emit_mm2(2 * qt + 1)
                rcs = {}
                t64s = {}
                for h in (0, 1):
                    pacc = pC if h == 0 else pD
                    tok_sr = ACT(lambda a, pacc=pacc, h=h:
                                 a.copy(s_row[h][:, :], pacc[64:65, :]),
                                 wait=[last_mm2[h], scst["rc"][h]])
                    rcs[h] = DVE(lambda v, h=h:
                                 v.reciprocal(s_rec[h][:, :], s_row[h][:, :]),
                                 wait=[tok_sr])
                    scst["rc"][h] = rcs[h]
                    t64s[h] = ACT(lambda a, pacc=pacc, h=h:
                                  a.copy(tmp64[h][:, :], pacc[0:64, :]),
                                  wait=[last_mm2[h]] + scst["t64_rd"][h])
                    scst["pacc_rd"][h] = [t64s[h], tok_sr]
                    scst["lqm_rd"][(qt, h)] = [last_mm2[h]]
                pend_ev = (qt, at, rcs, t64s, aw)
            flush_ev()
            tok_cc2[sq] = S.on("pool",
                               lambda g, sq=sq: g.collective_compute(
                                   "AllGather", ALU.bypass,
                                   replica_groups=[list(range(NCORE))],
                                   ins=[aga_in[128 * sq:128 * sq + 128, :].opt()],
                                   outs=[aga_out[1024 * sq:1024 * (sq + 1), :].opt()]),
                               wait=[S.now("xld")], inc="cc2")

        if stage <= 4:
            return
        wpre = [S.now("pe"), S.now("dve"), S.now("act"), S.now("pool"),
                S.now("xld")]
        wstate["last_use"] = [wpre, wpre]
        wstate["pre"] = [emit_wload(0), emit_wload(1)]
        attg_w = [tok_cc2[0], tok_cc2[1], S.now("pe"), S.now("dve"), S.now("act")]
        for dc in range(8):
            def rd(s, dc=dc):
                p = get_pid(s)
                return s.dma_start(
                    attg[:, dc, :],
                    aga_out[bass.ds(p["row"] + dc * 128, 128),
                            bass.ds(p["col"], NT)])
            SPD(rd, wait=attg_w)
        tok_attg = S.now("xld")
        pw_rd = {0: [], 1: []}
        for tt in range(NTILE):
            cols = slice(tt * TILE, (tt + 1) * TILE)
            for dc in range(8):
                dcp = dc % 2
                pw = pC if dcp == 0 else pD
                items = [(pw[:, :], wo_bf[:, kc, dc * 128:(dc + 1) * 128],
                          attg[:, kc, cols]) for kc in range(8)]
                tok_pw = mmgroup(items, wait=[tok_attg] + pw_rd[dcp])
                pw_rd[dcp] = [DVE(lambda v, dc=dc, cols=cols, pw=pw:
                                  v.scalar_tensor_tensor(X[:, dc, cols], pw[:, :],
                                                         bcol[:, 144 + dc:145 + dc],
                                                         X[:, dc, cols],
                                                         ALU.add, ALU.add),
                                  wait=[tok_pw])]

    if stage == 0:
        emit_ln()
    elif stage == 1:
        emit_gate(emit_ln())
    else:
        emit_ffn_pass()
        if stage >= 3:
            S.barrier()
            emit_attention()
            if stage >= 6:
                S.barrier()
                emit_ffn_pass()

    dbg_t = {}
    if dbg:
        S.barrier()
        for name in dbg:
            th = sb[name]
            shp = list(th.shape)
            nfree = int(np.prod(shp[1:]))
            dbg_t[name] = nc.dram_tensor("dbg_" + name, [shp[0], nfree], F32,
                                         kind="ExternalOutput")
            src = bass.AP(th, 0, [list(th[:].ap[0]), [1, nfree]])
            WLD(lambda g, name=name, src=src: g.dma_start(dbg_t[name][:, :], src))
        S.on("pool", None, wait=[S.now("wld")])

    S.barrier()
    for dc in range(8):
        STD(lambda s, dc=dc: s.dma_start(xout[dc * 128:(dc + 1) * 128, :],
                                         X[:, dc, :]))
    S.on("sp", None, wait=[S.now("st")])

    engmap = {"pe": "tensor", "act": "scalar", "dve": "vector",
              "pool": "gpsimd", "sp": "sync"}
    with nc.Block() as blk:
        for engname in Sch.ENGS:
            items = S.prog[engname]

            def body(eng, items=items):
                for fn, waits, inc in items:
                    for s_, v in waits:
                        eng.wait_ge(sems[s_], v)
                    if fn is not None:
                        ins = fn(eng)
                        if inc is not None:
                            ins.then_inc(sems[inc[0]], inc[1])
            getattr(blk, engmap[engname])(body)
    return nc


def _stage_inputs(inputs):
    f32 = np.float32
    x = np.asarray(inputs["L0_attn_x"], f32)
    xf = np.ascontiguousarray(x.reshape(B * T, D))
    gwT = np.ascontiguousarray(np.asarray(inputs["L0_ffn_gate_w"], f32).T)
    we1 = np.ascontiguousarray(np.asarray(inputs["L0_ffn_We1"], f32))
    we3 = np.ascontiguousarray(np.asarray(inputs["L0_ffn_We3"], f32))
    we2 = np.ascontiguousarray(np.asarray(inputs["L0_ffn_We2"], f32))
    ws1 = np.ascontiguousarray(np.asarray(inputs["L0_ffn_ws1"], f32))
    ws3 = np.ascontiguousarray(np.asarray(inputs["L0_ffn_ws3"], f32))
    ws2 = np.ascontiguousarray(np.asarray(inputs["L0_ffn_ws2"], f32))
    be1 = np.asarray(inputs["L0_ffn_be1"], f32)
    be3 = np.asarray(inputs["L0_ffn_be3"], f32)
    be2 = np.ascontiguousarray(np.asarray(inputs["L0_ffn_be2"], f32))
    bs1 = np.asarray(inputs["L0_ffn_bs1"], f32)
    bs3 = np.asarray(inputs["L0_ffn_bs3"], f32)
    bs2 = np.asarray(inputs["L0_ffn_bs2"], f32)
    gate_b = np.asarray(inputs["L0_ffn_gate_b"], f32)
    w1q = np.asarray(inputs["L0_attn_W1_q"], f32)
    w2q = np.asarray(inputs["L0_attn_W2_q"], f32)
    w1k = np.asarray(inputs["L0_attn_W1_k"], f32)
    w2k = np.asarray(inputs["L0_attn_W2_k"], f32)
    wv = np.asarray(inputs["L0_attn_W_v"], f32)
    bv = np.asarray(inputs["L0_attn_b_v"], f32)
    wo = np.ascontiguousarray(np.asarray(inputs["L0_attn_W_o"], f32))
    bo = np.asarray(inputs["L0_attn_b_o"], f32)

    bcol_h = np.zeros((128, NBCOL), f32)
    for e in range(E):
        for ic in range(4):
            bcol_h[:, 4 * e + ic] = be1[e, 128 * ic:128 * (ic + 1)]
            bcol_h[:, 64 + 4 * e + ic] = be3[e, 128 * ic:128 * (ic + 1)]
    for s_ in range(2):
        for ic in range(4):
            bcol_h[:, 128 + 4 * s_ + ic] = \
                bs1[512 * s_ + 128 * ic:512 * s_ + 128 * (ic + 1)]
            bcol_h[:, 136 + 4 * s_ + ic] = \
                bs3[512 * s_ + 128 * ic:512 * s_ + 128 * (ic + 1)]
    for dc in range(8):
        bcol_h[:, 144 + dc] = bo[128 * dc:128 * (dc + 1)]

    ident = np.eye(128, dtype=f32)
    trih = np.triu(np.ones((128, 128), f32))
    hmask_h = np.zeros((CP, HL), f32)
    for j in range(HL):
        hmask_h[6 * j:6 * j + 6, j] = 1.0
    ebc_h = np.zeros((E, E * 128), f32)
    for e in range(E):
        ebc_h[e, 128 * e:128 * (e + 1)] = 1.0
    growf = np.ascontiguousarray(np.tile(gate_b, 4).reshape(1, 4 * E))

    in_maps = []
    for c in range(NCORE):
        qs = slice(8 * c, 8 * c + 8)
        vs = slice(128 * c, 128 * (c + 1))
        brow_h = np.zeros((1, NBROW), f32)
        brow_h[0, 0:128] = bv[vs]
        brow_h[0, 256:1280] = bs2
        brow_h[0, 1280:2304] = bo
        wvqk = np.concatenate(
            [wv[:, vs], w1q[:D, qs], w2q[:D, qs], w1k[:, qs], w2k[:, qs]], axis=1)
        wqb = np.concatenate([w1q[D:, qs], w2q[D:, qs]], axis=1)
        in_maps.append({
            "xin": np.ascontiguousarray(xf[NT * c:NT * (c + 1)].T),
            "we1": we1, "we3": we3, "we2": we2,
            "ws1": ws1, "ws3": ws3, "ws2": ws2,
            "gwT": gwT, "be2m": be2, "bcol_h": bcol_h,
            "brow_h": brow_h, "growf": growf,
            "wvqk": np.ascontiguousarray(wvqk),
            "wqb": np.ascontiguousarray(wqb),
            "wo": wo,
            "ident": ident, "trih": trih, "hmask_h": hmask_h, "ebc_h": ebc_h,
        })
    return in_maps


_NC_CACHE = {}


def kernel(**inputs):
    if "nc" not in _NC_CACHE:
        _NC_CACHE["nc"] = build_nc()
    nc = _NC_CACHE["nc"]
    in_maps = _stage_inputs(inputs)
    res = run_bass_kernel_spmd(nc, in_maps, core_ids=list(range(NCORE)))
    out = np.empty((B * T, D), np.float32)
    for c in range(NCORE):
        out[NT * c:NT * (c + 1)] = res.results[c]["xout"].T
    return out.reshape(B, T, D)
